# version 08
# baseline (speedup 1.0000x reference)
import numpy as np

import concourse.bass as bass
import concourse.mybir as mybir
from concourse.bass_utils import run_bass_kernel_spmd

F32 = mybir.dt.float32
BF16 = mybir.dt.bfloat16
AF = mybir.ActivationFunctionType
ALU = mybir.AluOpType

D = 1024
T = 4096
NCH = 8
H = 16
DK = 64
NP = 8
NL = 6
DI = 2730
NJ = 22
JL = 42
BLK = 512
NBLK = T // BLK
NTC = BLK // 128
NEG = -1e30
EPS_LN = 1e-5
EPS_L2 = 1e-19
EPS_GLA = 1e-19
RD = 1.0 / D

ENGINES = ("sync", "tensor", "vector", "scalar", "gpsimd")
EX = {"tensor": "cP", "vector": "cV", "scalar": "cA", "gpsimd": "cG"}
_bcache = {}

_PC = {}
_cidx = 0
def _pc(name, n=1):
    global _cidx
    _PC[name] = _cidx
    _cidx += n
for _p in range(NP):
    _pc(f"wg{_p}"); _pc(f"bg{_p}"); _pc(f"wl{_p}"); _pc(f"bl{_p}")
for _l in range(NL):
    _pc(f"qs{_l}"); _pc(f"qssw{_l}"); _pc(f"ks{_l}"); _pc(f"kssw{_l}")
    _pc(f"lng{_l}", NCH); _pc(f"lnb{_l}", NCH); _pc(f"b2{_l}", NCH)
    _pc(f"b1a{_l}", NJ); _pc(f"b1g{_l}", NJ)
PARW = _cidx


def host_consts():
    if "c" in _bcache:
        return _bcache["c"]
    ident = np.eye(128, dtype=np.float32)
    S = np.zeros((128, 128), np.float32)
    for b0 in (0, 64):
        for d in range(64):
            S[b0 + d, b0 + (d + 32) % 64] = 1.0
    swapT = np.ascontiguousarray(S.T)
    onesf = np.ones((128, 128), np.float32)
    onesh = np.zeros((128, 128), np.float32)
    onesh[:64, :64] = 1.0
    onesh[64:, 64:] = 1.0
    ii = np.arange(128)[:, None]
    jj = np.arange(128)[None, :]
    masklog = np.zeros((128, 384), np.float32)
    masklog[:, :128] = np.where(ii <= jj, 0.0, NEG)
    masklog[:, 256:] = np.where(ii >= jj, 0.0, NEG)
    dmask = np.where(ii <= jj, 0.0, NEG).astype(np.float32)
    half = np.arange(0, DK, 2, dtype=np.float32)
    inv_freq = 1.0 / 10000.0 ** (half / DK)
    t = np.arange(T, dtype=np.float32)
    freqs = np.concatenate([t[:, None] * inv_freq[None, :]] * 2, -1)
    base_scale = (half + 0.4 * DK) / (1.4 * DK)
    power = (t - T // 2) / 128.0
    scale = np.concatenate([base_scale[None, :] ** power[:, None]] * 2, -1)
    cos, sin = np.cos(freqs), np.sin(freqs)
    sign = np.concatenate([-np.ones(32), np.ones(32)]).astype(np.float32)
    s8 = np.sqrt(8.0)
    dup = lambda a: np.ascontiguousarray(np.concatenate([a, a], 0), dtype=np.float32)
    c = dict(
        ident=ident, swapT=swapT, onesf=onesf, onesh=onesh, masklog=masklog, dmask=dmask,
        ctq=dup((cos * scale * s8).T), stq=dup((sin * scale * sign[None, :] * s8).T),
        ctk=dup((cos / scale * s8).T), stk=dup((sin / scale * sign[None, :] * s8).T),
    )
    _bcache["c"] = c
    return c


def host_par(ins):
    par = np.zeros((128, PARW), np.float32)
    dup2 = lambda v: np.concatenate([v, v])
    for p in range(NP):
        for nm, key in [("wg", "L0_attn_W_gamma"), ("bg", "L0_attn_b_gamma"),
                        ("wl", "L0_attn_W_lambda"), ("bl", "L0_attn_b_lambda")]:
            arr = np.asarray(ins[key], np.float32)
            col = np.empty(128, np.float32)
            col[:64] = arr[2 * p]
            col[64:] = arr[2 * p + 1]
            par[:, _PC[f"{nm}{p}"]] = col
    for l in range(NL):
        qs = np.asarray(ins["L0_ffn_q_scale"][l], np.float32)
        ks = np.asarray(ins["L0_ffn_k_scale"][l], np.float32)
        sw = lambda v: np.concatenate([v[32:], v[:32]])
        par[:, _PC[f"qs{l}"]] = dup2(qs)
        par[:, _PC[f"qssw{l}"]] = dup2(sw(qs))
        par[:, _PC[f"ks{l}"]] = dup2(ks)
        par[:, _PC[f"kssw{l}"]] = dup2(sw(ks))
        par[:, _PC[f"lng{l}"]:_PC[f"lng{l}"] + NCH] = np.asarray(ins["L0_ffn_ln_g"][l], np.float32).reshape(NCH, 128).T
        par[:, _PC[f"lnb{l}"]:_PC[f"lnb{l}"] + NCH] = np.asarray(ins["L0_ffn_ln_b"][l], np.float32).reshape(NCH, 128).T
        par[:, _PC[f"b2{l}"]:_PC[f"b2{l}"] + NCH] = np.asarray(ins["L0_ffn_b2"][l], np.float32).reshape(NCH, 128).T
        b1 = np.asarray(ins["L0_ffn_b1"][l], np.float32)
        b1p = np.zeros((2, NJ * 128), np.float32)
        b1p[0, :DI] = b1[:DI]
        b1p[1, :DI] = b1[DI:]
        par[:, _PC[f"b1a{l}"]:_PC[f"b1a{l}"] + NJ] = b1p[0].reshape(NJ, 128).T
        par[:, _PC[f"b1g{l}"]:_PC[f"b1g{l}"] + NJ] = b1p[1].reshape(NJ, 128).T
    return par


class Ph:
    def __init__(self, k):
        self.k = k
        self.fns = {e: [] for e in ENGINES}

    def flush(self):
        if not any(self.fns.values()):
            return
        with self.k.nc.Block() as blk:
            for e in ENGINES:
                if self.fns[e]:
                    def body(h, fs=tuple(self.fns[e])):
                        for f in fs:
                            f(h)
                    getattr(blk, e)(body)
        self.fns = {e: [] for e in ENGINES}


class Kern:
    def __init__(self, nc):
        self.nc = nc
        self.ctr = {}
        self.sems = {}
        self.inc = {}
        self.lastw = {}
        self.ph = Ph(self)

    def mksem(self, name, inc):
        h = self.nc.semaphore(name).__enter__()
        self.sems[name] = h
        self.ctr[name] = 0
        self.inc[name] = inc

    def emit(self, eng, fn, tick=None, waits=None):
        wl = []
        if waits:
            for q, v in waits.items():
                if not v or v <= 0:
                    continue
                assert v <= self.ctr[q], f"wait on future count {q}={v}>{self.ctr[q]}"
                key = (eng, q)
                if self.lastw.get(key, -1) >= v:
                    continue
                self.lastw[key] = v
                wl.append((self.sems[q], v))
        tk = None
        if tick is not None:
            self.ctr[tick] += self.inc[tick]
            tk = (self.sems[tick], self.inc[tick])

        def closure(h, fn=fn, wl=tuple(wl), tk=tk):
            for s, v in wl:
                h.wait_ge(s, v)
            inst = fn(h)
            if tk is not None:
                inst.then_inc(tk[0], tk[1])
        self.ph.fns[eng].append(closure)
        return self.ctr[tick] if tick else None


def build(taps=()):
    nc = bass.Bass("TRN2", target_bir_lowering=False)
    k = Kern(nc)
    dsems = ["dW", "dT", "dWo0", "dWo1", "dWo2", "dW10", "dW11", "dCv0", "dCv1",
             "dCs0", "dCs1", "dXl0", "dXl1", "dXs0", "dXs1", "dOut", "dTap",
             "dXr0", "dXr1", "dXb0", "dXb1"]
    for q in dsems:
        k.mksem(q, 16)
    for q in ["cP", "cV", "cA", "cG"]:
        k.mksem(q, 1)

    def din(name, shape, dtype=F32):
        return nc.dram_tensor(name, list(shape), dtype, kind="ExternalInput").ap()

    xT = din("xT", (D, T))
    wqT = din("wqT", (D, D)); wkT = din("wkT", (D, D)); wvT = din("wvT", (D, D)); woTg = din("woTg", (D, D))
    wqkv_in = din("wqkv_in", (NL, D, 3 * D))
    wo_in = din("wo_in", (NL, D, D))
    w1_in = din("w1_in", (NL, D, 2 * DI))
    w2_in = din("w2_in", (NL, DI, D))
    par_in = din("par_in", (128, PARW))
    c_ident = din("c_ident", (128, 128)); c_swapT = din("c_swapT", (128, 128))
    c_onesf = din("c_onesf", (128, 128)); c_onesh = din("c_onesh", (128, 128))
    c_mask = din("c_mask", (128, 384)); c_dmask = din("c_dmask", (128, 128))
    ctabs = [din("c_ctq", (128, T)), din("c_stq", (128, T)), din("c_ctk", (128, T)), din("c_stk", (128, T))]

    outT = nc.dram_tensor("outT", [NCH, 128, T], F32, kind="ExternalOutput").ap()
    tap_t = {}
    for name, shape, *dt in taps:
        tdt = BF16 if (dt and dt[0] == "bf16") else F32
        tap_t[name] = nc.dram_tensor("tap_" + name, list(shape), tdt, kind="ExternalOutput").ap()

    xmas = nc.dram_tensor("xmas", [NCH, 128, T], F32, kind="Internal").ap()
    xbfh = nc.dram_tensor("xbfh", [NCH, 128, T + 1], BF16, kind="Internal").ap()
    x1m = nc.dram_tensor("x1m", [NCH, 128, T], F32, kind="Internal").ap()
    qkvc = nc.dram_tensor("qkvc", [NL, NCH, 128, 3 * D], BF16, kind="Internal").ap()
    woc = nc.dram_tensor("woc", [NL, NCH, 128, D], BF16, kind="Internal").ap()
    wotc = nc.dram_tensor("wotc", [NCH, 128, D], BF16, kind="Internal").ap()
    w1c = nc.dram_tensor("w1c", [NL, 2, NJ, NCH, 128, 128], BF16, kind="Internal").ap()
    w2c = nc.dram_tensor("w2c", [NL, NJ, 128, D], BF16, kind="Internal").ap()

    def S(name, shape, dtype):
        return nc.alloc_sbuf_tensor(name, list(shape), dtype)

    warena = S("warena", (128, 24576), BF16)
    wor = S("wor", (128, 3, NCH, 128), BF16)
    w1r = S("w1r", (128, 4, 2, NCH, 128), BF16)
    tabs = S("tabs", (128, 4, BLK), F32)
    xfio = S("xfio", (128, 4, BLK), F32)
    cvf = S("cvf", (128, 2, 512), F32)
    cvb = S("cvb", (128, 2, 512), BF16)
    qp = S("qp", (128, NP, BLK), BF16)
    ovl = S("ovl", (128, 12288), BF16)
    aout = S("aout", (128, NCH, BLK), BF16)
    chs = S("chs", (128, 6, BLK), BF16)
    rns = S("rns", (128, 2, BLK), BF16)
    fsc = S("fsc", (128, 4, BLK), F32)
    pbf = S("pbf", (128, 3, 384), BF16)
    ga = S("ga", (128, 2, BLK), F32)
    gb = S("gb", (128, 2, BLK), F32)
    gc = S("gc", (128, 2, BLK), F32)
    xrd = S("xrd", (128, 2, NCH, BLK + 1), BF16)
    sstg = S("sstg", (128, 2, BLK), BF16)
    xnb = S("xnb", (128, NCH, BLK), BF16)
    ssb = S("ssb", (128, NP, 64), F32)
    sbfA = S("sbfA", (128, NP, 64), BF16)
    sbfB = S("sbfB", (128, NP, 64), BF16)
    cst = S("cst", (128, 1056), F32)
    cbt = S("cbt", (128, 1024), BF16)
    par = S("par", (128, PARW), F32)

    rem = nc.sbuf_bytes_remaining
    assert rem >= 0, f"SBUF over budget by {-rem} bytes"

    kring = ovl.ap()[:, 0:6144].rearrange("p (a b c) -> p a b c", a=NP, b=6)
    vtok = ovl.ap()[:, 6144:12288].rearrange("p (a b) -> p a b", a=6)
    usb = ovl.ap()[:, 0:NJ * BLK].rearrange("p (a b) -> p a b", a=NJ)
    kview = ovl.ap()[:, 0:4096].rearrange("p (a b) -> p a b", a=NP)
    vview = ovl.ap()[:, 4096:8192]
    t1bv = ovl.ap()[:, 8192:12288].rearrange("p (a b) -> p a b", a=NCH)

    psA = nc.alloc_psum_tensor("psA", [128, 2048], F32)
    psB = nc.alloc_psum_tensor("psB", [128, 1024], F32)
    psC = nc.alloc_psum_tensor("psC", [128, 2048], BF16)

    CO = dict(ident=0, swapT=128, onesf=256, onesh=384, mask=512, dmask=896)
    CB = dict(identb=0, swapTb=128, onesfb=256, oneshb=384, epsrow=512)
    GW = {"wq": 0, "wk": 8192, "wv": 16384}

    def dma(queue, sem, out, in_, waits=None):
        return k.emit(queue, lambda h, o=out, i=in_: h.dma_start(out=o, in_=i), tick=sem, waits=waits)

    def op(eng, fn, waits=None):
        return k.emit(eng, fn, tick=EX[eng], waits=waits)

    def mm(out_ap, lhsT_ap, rhs_ap, start, stop, waits=None):
        return op("tensor", lambda h, o=out_ap, l=lhsT_ap, r=rhs_ap, s1=start, s2=stop:
                  h.matmul(o, l, r, start=s1, stop=s2), waits=waits)

    def tr(out_ap, in_ap, identity, waits=None):
        return op("tensor", lambda h, o=out_ap, i=in_ap, d=identity: h.transpose(o, i, d), waits=waits)

    def allnow():
        return dict(k.ctr)

    def parc(name, off=0, rows=128):
        c0 = _PC[name] + off
        return par[0:rows, c0:c0 + 1]

    def xsl(slot, c, shifted):
        off = (0 if (shifted and c >= 4) else 1)
        return xrd[:, slot, c, off:off + BLK]

    xr_gate = {0: {}, 1: {}}

    def xrd_load(b):
        slot = b % 2
        ld = dma("scalar", f"dXr{slot}", xrd[:, slot, :, :],
                 xbfh[:, :, b * BLK: b * BLK + BLK + 1].transpose([1, 0, 2]),
                 waits=xr_gate[slot])
        return slot, ld

    def xrd_mark(slot):
        xr_gate[slot] = {"cP": k.ctr["cP"], "cV": k.ctr["cV"], "cG": k.ctr["cG"]}

    def tap(name, src_ap, idx=None, waits=None):
        if name in tap_t:
            dst = tap_t[name] if idx is None else tap_t[name][idx]
            dma("scalar", "dTap", dst, src_ap, waits=waits if waits is not None else allnow())

    onesb_row = cbt[0:1, CB["onesfb"]:CB["onesfb"] + 128]
    onesf_row = cst[0:1, CO["onesf"]:CO["onesf"] + 128]
    eps_row = cbt[0:1, CB["epsrow"]:CB["epsrow"] + BLK]
    identb = cbt[:, CB["identb"]:CB["identb"] + 128]
    swapTb = cbt[:, CB["swapTb"]:CB["swapTb"] + 128]
    onesfb = cbt[:, CB["onesfb"]:CB["onesfb"] + 128]
    oneshb = cbt[:, CB["oneshb"]:CB["oneshb"] + 128]

    dma("sync", "dW", cst[:, CO["ident"]:CO["ident"] + 128], c_ident)
    dma("sync", "dW", cst[:, CO["swapT"]:CO["swapT"] + 128], c_swapT)
    dma("sync", "dW", cst[:, CO["onesf"]:CO["onesf"] + 128], c_onesf)
    dma("sync", "dW", cst[:, CO["onesh"]:CO["onesh"] + 128], c_onesh)
    dma("sync", "dW", cst[:, CO["mask"]:CO["mask"] + 384], c_mask)
    dma("sync", "dW", cst[:, CO["dmask"]:CO["dmask"] + 128], c_dmask)
    pld = dma("sync", "dW", par[:, :], par_in)
    w0 = {"dW": pld}
    for nm, dst in [("ident", "identb"), ("swapT", "swapTb"), ("onesf", "onesfb"), ("onesh", "oneshb")]:
        op("gpsimd", lambda h, s=CO[nm], d=CB[dst]: h.tensor_copy(cbt[:, d:d + 128], cst[:, s:s + 128]), waits=w0)
    op("gpsimd", lambda h: h.memset(cbt[0:1, CB["epsrow"]:CB["epsrow"] + BLK], EPS_L2))
    op("gpsimd", lambda h: h.memset(cst[:, 1024:1025], EPS_LN))
    op("gpsimd", lambda h: h.memset(ssb[:, :, :], 0.0))
    op("gpsimd", lambda h: h.memset(sbfA[:, :, :], 0.0))
    op("gpsimd", lambda h: h.memset(sbfB[:, :, :], 0.0))
    zc = op("gpsimd", lambda h: h.memset(sstg[:, 0, 0:16], 0.0))
    k.emit("scalar", lambda h: h.dma_start(out=xbfh[:, :, 0:1].transpose([1, 0, 2]),
                                           in_=sstg[:, 0:1, 0:NCH].transpose([0, 2, 1]),
                                           allow_slow_non_contiguous=True),
           tick="dXb0", waits={"cG": zc})

    cvst = {"i": 0, 0: {"rd": ("cG", 0), "st": 0}, 1: {"rd": ("cG", 0), "st": 0}}

    def convert_piece(src_ap, dst_sb=None, dst_dram=None, rows=128):
        i = cvst["i"] % 2
        cvst["i"] += 1
        g = cvst[i]
        n = src_ap.shape[-1]
        assert n <= 512
        ld = dma("sync", f"dCv{i}", cvf[0:rows, i, 0:n], src_ap,
                 waits={g["rd"][0]: g["rd"][1], f"dCs{i}": g["st"]})
        if dst_sb is not None:
            m = op("gpsimd", lambda h, r=rows, i=i, n=n, d=dst_sb: h.tensor_copy(d, cvf[0:r, i, 0:n]),
                   waits={f"dCv{i}": ld})
            g["rd"] = ("cG", m)
            g["st"] = 0
        else:
            m = op("gpsimd", lambda h, r=rows, i=i, n=n: h.tensor_copy(cvb[0:r, i, 0:n], cvf[0:r, i, 0:n]),
                   waits={f"dCv{i}": ld, f"dCs{i}": g["st"]})
            g["rd"] = ("cG", m)
            g["st"] = dma("scalar", f"dCs{i}", dst_dram, cvb[0:rows, i, 0:n], waits={"cG": m})

    for nm, src in [("wq", wqT), ("wk", wkT), ("wv", wvT)]:
        for c in range(NCH):
            for piece in range(2):
                convert_piece(src[c * 128:(c + 1) * 128, piece * 512:(piece + 1) * 512],
                              dst_sb=warena[:, GW[nm] + c * 1024 + piece * 512: GW[nm] + c * 1024 + (piece + 1) * 512])
    for c in range(NCH):
        for piece in range(2):
            convert_piece(woTg[c * 128:(c + 1) * 128, piece * 512:(piece + 1) * 512],
                          dst_dram=wotc[c, :, piece * 512:(piece + 1) * 512])
    glaw = allnow()

    def conv_items(l):
        items = []
        for c in range(NCH):
            for p in range(6):
                items.append((wqkv_in[l, c * 128:(c + 1) * 128, p * 512:(p + 1) * 512],
                              qkvc[l, c, :, p * 512:(p + 1) * 512], 128))
        for c in range(NCH):
            for p in range(2):
                items.append((wo_in[l, c * 128:(c + 1) * 128, p * 512:(p + 1) * 512],
                              woc[l, c, :, p * 512:(p + 1) * 512], 128))
        for c in range(NCH):
            for ag in range(2):
                base = ag * DI
                for j0, j1 in [(0, 4), (4, 8), (8, 12), (12, 16), (16, 20), (20, 21)]:
                    items.append((w1_in[l, c * 128:(c + 1) * 128, base + j0 * 128: base + j1 * 128],
                                  w1c[l, ag, j0:j1, c, :, :].transpose([1, 0, 2]), 128))
                items.append((w1_in[l, c * 128:(c + 1) * 128, base + 21 * 128: base + DI],
                              w1c[l, ag, 21, c, :, 0:JL], 128))
        for j in range(NJ):
            rows = 128 if j < NJ - 1 else JL
            for p in range(2):
                items.append((w2_in[l, j * 128:j * 128 + rows, p * 512:(p + 1) * 512],
                              w2c[l, j, 0:rows, p * 512:(p + 1) * 512], rows))
        return items

    convq = {"items": [], "done": {}}

    def conv_push(l):
        convq["items"].extend((l, it) for it in conv_items(l))

    def conv_step(nmax):
        for _ in range(nmax):
            if not convq["items"]:
                return
            l, (src, dst, rows) = convq["items"].pop(0)
            convert_piece(src, dst_dram=dst, rows=rows)
            if not convq["items"] or convq["items"][0][0] != l:
                convq["done"][l] = allnow()

    conv_push(0)

    def ln_stats(srcs, bank, extra_waits=None):
        pe = 0
        ws = dict(extra_waits or {})
        for c, src in enumerate(srcs):
            sq = chs[:, 4 + (c % 2), :]
            pl = op("gpsimd", lambda h, s=src, d=sq: h.tensor_tensor(d, s, s, ALU.mult),
                    waits={**ws, "cP": pe})
            mm(psA[:, bank:bank + BLK], onesfb, src, start=(c == 0), stop=(c == NCH - 1), waits=ws if c == 0 else None)
            pe = mm(psA[:, bank + 512:bank + 512 + BLK], onesfb, sq, start=(c == 0), stop=(c == NCH - 1),
                    waits={"cG": pl})
        return pe

    def ln_chain(bank, rstd_dst, waits):
        mu2 = fsc[:, 2, 0:BLK]
        a1 = op("scalar", lambda h: h.activation(mu2, psA[:, bank:bank + BLK], AF.Square, scale=RD), waits=waits)
        v1 = op("vector", lambda h: h.scalar_tensor_tensor(rstd_dst, psA[:, bank + 512:bank + 512 + BLK], RD, mu2,
                                                           ALU.mult, ALU.subtract), waits={"cA": a1})
        a2 = op("scalar", lambda h: h.activation(rstd_dst, rstd_dst, AF.Ln, bias=cst[:, 1024:1025]), waits={"cV": v1})
        return op("scalar", lambda h: h.activation(rstd_dst, rstd_dst, AF.Exp, scale=-0.5), waits={"cA": a2})

    xbst = {0: 0, 1: 0}

    def xb_store(c, t0, src_f32, buf, vmark):
        pl = op("gpsimd", lambda h, s=src_f32, buf=buf: h.tensor_copy(sstg[:, buf, :], s),
                waits={"cV": vmark, f"dXb{buf}": xbst[buf]})
        xbst[buf] = dma("scalar", f"dXb{buf}", xbfh[c, :, 1 + t0:1 + t0 + BLK], sstg[:, buf, :],
                        waits={"cG": pl})
        return pl

    prep_state = {"g1": 0, "last": 0}

    def prep(src_bf, rn_ap, dst_writes, cS, sS, swap_bank, l, isq, waits):
        waits = dict(waits)
        waits["cG"] = max(waits.get("cG", 0), prep_state["g1"])
        waits["cV"] = max(waits.get("cV", 0), prep_state["last"])
        sw = mm(psA[:, swap_bank:swap_bank + BLK], swapTb, src_bf, start=True, stop=True, waits=waits)
        t1 = chs[:, 2, :]
        t2 = chs[:, 3, :]
        sc = ("qs", "qssw") if isq else ("ks", "kssw")
        v1 = op("vector", lambda h, s=src_bf: h.scalar_tensor_tensor(t1, s, parc(f"{sc[0]}{l}"), cS, ALU.mult, ALU.mult),
                waits=waits)
        v2 = op("vector", lambda h, sb=swap_bank: h.scalar_tensor_tensor(
            t2, psA[:, sb:sb + BLK], parc(f"{sc[1]}{l}"), sS, ALU.mult, ALU.mult),
            waits={"cP": sw, "cG": waits["cG"]})
        g1 = op("gpsimd", lambda h: h.tensor_tensor(t1, t1, t2, ALU.add), waits={"cV": v2})
        last = 0
        for dst, c0, n in dst_writes:
            last = op("vector", lambda h, d=dst, c0=c0, n=n: h.tensor_tensor(
                d, t1[:, c0:c0 + n], rn_ap[:, c0:c0 + n], ALU.mult), waits={"cG": g1})
        prep_state["g1"] = g1
        prep_state["last"] = last
        return {"cV": last, "cG": g1}

    wo_ring = {"use": {}, "m": {}}

    def wo_load(cache, oc, slot, conv_w=None):
        w = {"cP": wo_ring["use"].get(slot, 0)}
        if conv_w:
            w.update(conv_w)
        m = dma("sync", f"dWo{slot}", wor[:, slot, :, :],
                cache[:, :, oc * 128:(oc + 1) * 128].transpose([1, 0, 2]), waits=w)
        wo_ring["m"][oc] = (f"dWo{slot}", m)
        return m

    def gla_block(b):
        t0 = b * BLK
        cg = 0
        for c in range(NCH):
            sl = fsc[:, c % 2, 0:BLK]
            xld = dma("scalar", f"dXl{c % 2}", sl, xT[c * 128:(c + 1) * 128, t0:t0 + BLK], waits={"cG": cg})
            cg = op("gpsimd", lambda h, c=c, sl=sl: h.tensor_copy(xnb[:, c, :], sl), waits={f"dXl{c % 2}": xld})
        pe = ln_stats([xnb[:, c, :] for c in range(NCH)], 1024, {"cG": cg})
        rstd = fsc[:, 3, 0:BLK]
        a3 = ln_chain(1024, rstd, {"cP": pe})
        t1b = t1bv
        vdone = 0
        for c in range(NCH):
            v1 = op("vector", lambda h, c=c: h.scalar_tensor_tensor(
                ga[:, c % 2, :], psA[:, 1024:1024 + BLK], -RD, xnb[:, c, :], ALU.mult, ALU.add),
                waits={"cA": a3, "cP": pe, "cV": vdone})
            vdone = op("vector", lambda h, c=c: h.tensor_tensor(t1b[:, c, :], ga[:, c % 2, :], rstd, ALU.mult),
                       waits={"cV": v1})
        tap("t1b", ovl[:, 8192:12288].rearrange("p (a b) -> p a b", a=NCH), np.s_[:, :, t0:t0 + BLK])
        evm = {}
        for oc in range(24):
            kind, p = oc // NP, oc % NP
            buf = oc % 2
            pe0 = 0
            for c in range(NCH):
                w = None
                if c == 0:
                    w = {"cG": glaw["cG"], "cA": glaw["cA"], "cV": max(vdone, evm.get(oc - 2, 0))}
                sec = ("wq", "wk", "wv")[kind]
                pe0 = mm(psB[:, buf * 512:buf * 512 + BLK],
                         warena[:, GW[sec] + c * 1024 + p * 128: GW[sec] + c * 1024 + (p + 1) * 128],
                         t1b[:, c, :], start=(c == 0), stop=(c == NCH - 1), waits=w)
            if kind == 0:
                dst = qp[:, p, :]
            elif kind == 1:
                dst = kview[:, p, 0:BLK]
            else:
                dst = vview[:, p * BLK:(p + 1) * BLK]
            evm[oc] = op("vector", lambda h, d=dst, buf=buf: h.tensor_copy(d, psB[:, buf * 512:buf * 512 + BLK]),
                         waits={"cP": pe0})
        conv_step(16)
        evq = max(evm[oc] for oc in range(0, 8))
        evk = max(evm[oc] for oc in range(8, 16))
        evv = max(evm[oc] for oc in range(16, 24))
        khm = {}
        rn_a = {}
        rn_pe = {}
        for p in range(NP):
            bank = 1024 + (p % 2) * 512
            sq = chs[:, 4 + (p % 2), :]
            pl = op("gpsimd", lambda h, s=kview[:, p, 0:BLK], d=sq: h.tensor_tensor(d, s, s, ALU.mult),
                    waits={"cV": evk, "cP": rn_pe.get(p - 2, 0), "cA": rn_a.get(p - 2, 0)})
            pe1 = mm(psA[:, bank:bank + BLK], oneshb, sq, start=True, stop=False,
                     waits={"cG": pl, "cA": rn_a.get(p - 2, 0)})
            rn_pe[p] = pe1 = mm(psA[:, bank:bank + BLK], onesb_row, eps_row, start=False, stop=True)
            a1 = op("scalar", lambda h, bank=bank: h.activation(fsc[:, 2, 0:BLK], psA[:, bank:bank + BLK], AF.Ln),
                    waits={"cP": pe1})
            a2 = op("scalar", lambda h, p=p: h.activation(rns[:, p % 2, :], fsc[:, 2, 0:BLK], AF.Exp, scale=-0.5),
                    waits={"cA": a1, "cV": khm.get(p - 2, 0)})
            rn_a[p] = a2
            khm[p] = op("vector", lambda h, p=p: h.tensor_tensor(
                kview[:, p, 0:BLK], kview[:, p, 0:BLK], rns[:, p % 2, :], ALU.mult), waits={"cA": a2})
        tap("qp", qp[:, :, :], np.s_[b])
        tap("kh", ovl[:, 0:4096], np.s_[b])
        tap("vv", ovl[:, 4096:8192], np.s_[b])
        lastr = {"chs": 0, "vt_pe": 0, "yev": 0}
        dpipe = {}

        def pipeline(p):
            s = p % 2
            gap, gbp, gcp = ga[:, s, :], gb[:, s, :], gc[:, s, :]
            pe1 = mm(psA[:, 1024:1024 + BLK], oneshb, kview[:, p, 0:BLK], start=True, stop=True,
                     waits={"cV": max(khm[p], lastr["yev"], dpipe.get(p - 2, 0)), "cA": k.ctr["cA"]})
            v1 = op("vector", lambda h, gap=gap: h.tensor_scalar(gap, psA[:, 1024:1024 + BLK], 1.0 / DK, None, ALU.mult),
                    waits={"cP": pe1})
            s1 = op("scalar", lambda h, p=p, gap=gap, gbp=gbp: h.activation(
                gbp, gap, AF.Sigmoid, bias=parc(f"bg{p}"), scale=parc(f"wg{p}")), waits={"cV": v1, **w0})
            s2 = op("scalar", lambda h, p=p, gap=gap, gcp=gcp: h.activation(
                gcp, gap, AF.Sigmoid, bias=parc(f"bl{p}"), scale=parc(f"wl{p}")), waits={"cV": v1})
            v2 = op("vector", lambda h, gap=gap, gbp=gbp, gcp=gcp: h.tensor_tensor(gap, gcp, gbp, ALU.mult),
                    waits={"cA": s2})
            v2 = op("vector", lambda h, gap=gap, gbp=gbp: h.tensor_tensor(gap, gbp, gap, ALU.subtract),
                    waits={"cV": v2, "cA": s1})
            v3 = op("vector", lambda h, gap=gap, gbp=gbp: h.tensor_scalar(gbp, gap, -1.0, 1.0, ALU.mult, ALU.add),
                    waits={"cV": v2})
            v2 = op("vector", lambda h, gap=gap: h.tensor_scalar_max(gap, gap, EPS_GLA), waits={"cV": v2})
            v3 = op("vector", lambda h, gbp=gbp: h.tensor_scalar_max(gbp, gbp, EPS_GLA), waits={"cV": v3})
            s3 = op("scalar", lambda h, gap=gap, gcp=gcp: h.activation(gcp, gap, AF.Ln), waits={"cV": v2})
            s4 = op("scalar", lambda h, gap=gap, gbp=gbp: h.activation(gap, gbp, AF.Ln), waits={"cV": v3, "cA": s3})
            sc = 0
            for tc in range(NTC):
                sc = op("vector", lambda h, gbp=gbp, gcp=gcp, tc=tc: h.tensor_tensor_scan(
                    gbp[:, tc * 128:(tc + 1) * 128], gcp[:, tc * 128:(tc + 1) * 128],
                    gcp[:, tc * 128:(tc + 1) * 128], 0.0, ALU.add, ALU.bypass),
                    waits={"cA": s4})
            v4 = op("vector", lambda h, gap=gap, gbp=gbp, gcp=gcp: h.tensor_tensor(gcp, gap, gbp, ALU.subtract),
                    waits={"cV": sc, "cA": s4})
            dpipe[p] = v4

        pipeline(0)
        for p in range(NP):
            if p + 1 < NP:
                pipeline(p + 1)
            s = p % 2
            gbp, gcp = gb[:, s, :], gc[:, s, :]
            expL = chs[:, 0, :]
            qq = chs[:, 1, :]
            e1 = op("scalar", lambda h, gbp=gbp: h.activation(expL, gbp, AF.Exp),
                    waits={"cV": max(dpipe[p], lastr["chs"])})
            q2 = op("vector", lambda h, p=p: h.tensor_tensor(qq, qp[:, p, :], expL, ALU.mult),
                    waits={"cA": e1, "cV": evq, "cP": k.ctr["cP"]})
            for tc in range(NTC):
                c0 = tc * 128
                eL = fsc[:, 1, tc:tc + 1]
                eA = op("scalar", lambda h, gbp=gbp, eL=eL, c0=c0: h.activation(
                    eL, gbp[:, c0 + 127:c0 + 128], AF.Exp), waits={"cV": max(dpipe[p], lastr["chs"])})
                fv = chs[:, 2, 0:128]
                f1 = op("scalar", lambda h, gbp=gbp, gcp=gcp, c0=c0: h.activation(
                    chs[:, 2, 0:128], gcp[:, c0:c0 + 128], AF.Exp, bias=gbp[:, c0 + 127:c0 + 128]),
                    waits={"cV": max(dpipe[p], lastr["chs"])})
                f2 = op("vector", lambda h, p=p, c0=c0: h.tensor_tensor(
                    chs[:, 3, 0:128], vview[:, p * BLK + c0: p * BLK + c0 + 128], fv, ALU.mult),
                    waits={"cA": f1, "cV": evv, "cP": lastr["vt_pe"]})
                t1m = tr(psC[:, 0:128], kview[:, p, c0:c0 + 128], identb,
                         waits={"cV": max(khm[p], lastr["chs"])})
                ke = op("vector", lambda h: h.tensor_copy(chs[:, 4, 0:128], psC[:, 0:128]),
                        waits={"cP": t1m})
                t2m = tr(psC[:, 1024:1152], chs[:, 3, 0:128], identb, waits={"cV": f2})
                ve = op("vector", lambda h: h.tensor_copy(chs[:, 5, 0:128], psC[:, 1024:1152]),
                        waits={"cP": t2m})
                amk = {}
                for hh in range(2):
                    r0 = hh * 64
                    araw_bank = 0 if hh == 0 else 512
                    d_bank = 256 if hh == 0 else 768
                    ar = mm(psB[:, araw_bank:araw_bank + 128],
                            vview[r0:r0 + 64, p * BLK + c0: p * BLK + c0 + 128],
                            qp[r0:r0 + 64, p, c0:c0 + 128], start=True, stop=True,
                            waits={"cV": max(evv, evq, dpipe[p])})
                    ones_r = cst[r0:r0 + 1, CO["onesf"]:CO["onesf"] + 128]
                    d1 = mm(psA[:, d_bank:d_bank + 128], gcp[r0:r0 + 1, c0:c0 + 128], ones_r,
                            start=True, stop=True, waits={"cV": dpipe[p]})
                    d1 = mm(psA[:, d_bank:d_bank + 128], ones_r, gbp[r0:r0 + 1, c0:c0 + 128],
                            start=False, stop=True)
                    md = op("vector", lambda h, d_bank=d_bank: h.tensor_tensor(
                        fsc[:, 2, 0:128], psA[:, d_bank:d_bank + 128], cst[:, CO["dmask"]:CO["dmask"] + 128], ALU.add),
                        waits={"cP": d1, "cA": k.ctr["cA"]})
                    de = op("scalar", lambda h: h.activation(pbf[:, 2, 0:128], fsc[:, 2, 0:128], AF.Exp),
                            waits={"cV": md})
                    amk[hh] = op("vector", lambda h, hh=hh, araw_bank=araw_bank: h.tensor_tensor(
                        pbf[:, 2, 128 * (hh + 1):128 * (hh + 2)], psB[:, araw_bank:araw_bank + 128],
                        pbf[:, 2, 0:128], ALU.mult),
                        waits={"cA": de, "cP": ar})
                if b == 0 and p == 0 and tc == 0:
                    tap("dbg_at", pbf[:, 2, :])
                    tap("dbg_L", gb[:, 0, :])
                    tap("dbg_lnad", gc[:, 0, :])
                    tap("dbg_ktok", chs[:, 4, 0:128])
                    tap("dbg_qq", chs[:, 1, :])
                ylast = mm(psA[0:64, 1024:1152], chs[:, 4, 0:64], pbf[:, 2, 128:256], start=True, stop=False,
                           waits={"cV": max(ke, amk[0], q2), "cG": k.ctr["cG"]})
                mm(psA[0:64, 1024:1152], sbfA[:, p, :], qq[:, c0:c0 + 128], start=False, stop=False)
                mm(psA[64:128, 1024:1152], chs[:, 4, 64:128], pbf[:, 2, 256:384], start=True, stop=False,
                   waits={"cV": amk[1]})
                ylast = mm(psA[64:128, 1024:1152], sbfB[:, p, :], qq[:, c0:c0 + 128], start=False, stop=True)
                yev = op("vector", lambda h, p=p, c0=c0: h.tensor_copy(aout[:, p, c0:c0 + 128], psA[:, 1024:1152]),
                         waits={"cP": ylast})
                lastr["yev"] = yev
                wA = mm(psA[0:64, 1536:1600], chs[:, 5, 0:64], chs[:, 4, 0:64], start=True, stop=True,
                        waits={"cV": max(ve, ke), "cG": k.ctr["cG"]})
                wB = mm(psA[64:128, 1536:1600], chs[:, 5, 64:128], chs[:, 4, 64:128], start=True, stop=True)
                lastr["vt_pe"] = wB
                su = op("vector", lambda h, p=p, eL=eL: h.scalar_tensor_tensor(
                    ssb[:, p, :], ssb[:, p, :], eL, psA[:, 1536:1600], ALU.mult, ALU.add),
                    waits={"cP": wB, "cA": eA, "cG": k.ctr["cG"]})
                op("gpsimd", lambda h, p=p: h.tensor_copy(sbfA[0:64, p, :], ssb[0:64, p, :]),
                   waits={"cV": su, "cP": ylast})
                op("gpsimd", lambda h, p=p: h.tensor_copy(sbfB[64:128, p, :], ssb[64:128, p, :]),
                   waits={"cV": su})
                lastr["chs"] = max(yev, su)
        tap("glay", aout[:, :, :], np.s_[:, :, t0:t0 + BLK])
        rb = xnb
        evo = 0
        cvw = {"dCs0": glaw["dCs0"], "dCs1": glaw["dCs1"]}
        wo_load(wotc, 0, 0, cvw)
        wo_load(wotc, 1, 1, cvw)
        for oc in range(NCH):
            slot = oc % 3
            buf = oc % 2
            pe0 = 0
            for c in range(NCH):
                w = None
                if c == 0:
                    w = {"cV": max(lastr["yev"], evo), "cG": k.ctr["cG"], wo_ring["m"][oc][0]: wo_ring["m"][oc][1]}
                pe0 = mm(psB[:, buf * 512:buf * 512 + BLK], wor[:, slot, c, :], aout[:, c, :],
                         start=(c == 0), stop=(c == NCH - 1), waits=w)
            wo_ring["use"][slot] = pe0
            if oc + 2 < NCH:
                wo_load(wotc, oc + 2, (oc + 2) % 3, cvw)
            evo = op("vector", lambda h, oc=oc, buf=buf: h.tensor_copy(rb[:, oc, :], psB[:, buf * 512:buf * 512 + BLK]),
                     waits={"cP": pe0})
        pe = ln_stats([rb[:, c, :] for c in range(NCH)], 1024, {"cV": evo})
        a3 = ln_chain(1024, fsc[:, 3, 0:BLK], {"cP": pe})
        x1st = {0: 0, 1: 0}
        x1cg = 0
        for c in range(NCH):
            sl = fsc[:, c % 2, 0:BLK]
            ld = dma("scalar", f"dXl{c % 2}", sl, xT[c * 128:(c + 1) * 128, t0:t0 + BLK],
                     waits={"cV": k.ctr["cV"]})
            s2_ = c % 2
            v1 = op("vector", lambda h, c=c, s2_=s2_: h.scalar_tensor_tensor(
                ga[:, s2_, :], psA[:, 1024:1024 + BLK], -RD, rb[:, c, :], ALU.mult, ALU.add),
                waits={"cA": a3, "cP": pe, "cG": x1cg, f"dXs{c % 2}": x1st[c % 2]})
            v2 = op("vector", lambda h, s2_=s2_: h.tensor_tensor(ga[:, s2_, :], ga[:, s2_, :], fsc[:, 3, 0:BLK], ALU.mult),
                    waits={"cV": v1})
            v3 = op("vector", lambda h, sl=sl, s2_=s2_: h.tensor_tensor(gb[:, s2_, :], ga[:, s2_, :], sl, ALU.add),
                    waits={"cV": v2, f"dXl{c % 2}": ld})
            x1st[c % 2] = dma("scalar", f"dXs{c % 2}", x1m[c, :, t0:t0 + BLK], gb[:, s2_, :], waits={"cV": v3})
            x1cg = op("gpsimd", lambda h, c=c, s2_=s2_: h.tensor_copy(t1bv[:, c, :], gb[:, s2_, :]), waits={"cV": v3})
        x1b = t1bv
        tap("x1", ovl[:, 8192:12288].rearrange("p (a b) -> p a b", a=NCH), np.s_[:, :, t0:t0 + BLK])
        pe = ln_stats([x1b[:, c, :] for c in range(NCH)], 1024, {"cG": x1cg})
        a3 = ln_chain(1024, fsc[:, 3, 0:BLK], {"cP": pe})
        xst = {0: 0, 1: 0}
        t2cg = 0
        for c in range(NCH):
            s2_ = c % 2
            v1 = op("vector", lambda h, c=c, s2_=s2_: h.scalar_tensor_tensor(
                ga[:, s2_, :], psA[:, 1024:1024 + BLK], -RD, x1b[:, c, :], ALU.mult, ALU.add),
                waits={"cA": a3, "cP": pe, f"dXs{c % 2}": xst[c % 2], "cG": t2cg})
            v2 = op("vector", lambda h, s2_=s2_: h.tensor_tensor(ga[:, s2_, :], ga[:, s2_, :], fsc[:, 3, 0:BLK], ALU.mult),
                    waits={"cV": v1})
            xst[c % 2] = dma("scalar", f"dXs{c % 2}", xmas[c, :, t0:t0 + BLK], ga[:, s2_, :], waits={"cV": v2})
            t2cg = xb_store(c, t0, ga[:, s2_, :], c % 2, v2)
        conv_step(12)
        k.ph.flush()

    lw = {"wqkv": None, "w2": None}

    def attn_phase(l):
        cw = convq["done"].get(l)
        assert cw is not None, f"conversions for layer {l} not done in time"
        if l > 0:
            for c in range(NCH):
                tap(f"ffn{l - 1}", xmas[c], idx=np.s_[c])
        lw["wqkv"] = dma("sync", "dW", warena.ap().rearrange("p (c n) -> p c n", c=NCH),
                         qkvc[l].transpose([1, 0, 2]),
                         waits={"dCs0": cw["dCs0"], "dCs1": cw["dCs1"], "cP": k.ctr["cP"]})
        ldm = {0: xrd_load(0)[1]}
        for b in range(NBLK):
            attn_block(l, b, ldm)

    def attn_block(l, b, ldm):
        t0 = b * BLK
        tld = 0
        for i in range(4):
            tld = dma("sync", "dT", tabs[:, i, :], ctabs[i][:, t0:t0 + BLK],
                      waits={"cV": k.ctr["cV"], "cG": k.ctr["cG"]} if i == 0 else None)
        evac = {}
        prep_marks = {}
        rn_a = {}
        rnuse = {0: 0, 1: 0}
        order = list(range(8, 16)) + list(range(16, 24)) + list(range(0, 8))
        for idx, oc in enumerate(order):
            kind, p = oc // NP, oc % NP
            buf = idx % 2
            pe0 = 0
            for c in range(NCH):
                w = None
                if c == 0:
                    pv_ = evac.get(idx - 2, ("cV", 0))
                    w = {"dW": lw["wqkv"], pv_[0]: pv_[1], f"dXr{b % 2}": ldm[b]}
                pe0 = mm(psB[:, buf * 512:buf * 512 + BLK],
                         warena[:, c * 3072 + oc * 128: c * 3072 + (oc + 1) * 128],
                         xsl(b % 2, c, True), start=(c == 0), stop=(c == NCH - 1), waits=w)
            dst = chs[:, buf, :]
            pmw = prep_marks.get(idx - 2, {})
            evw = {"cP": pe0, "cG": pmw.get("cG", 0), "cV": pmw.get("cV", 0), "cA": pmw.get("cA", 0)}
            if idx % 2 == 0:
                ev = ("cV", op("vector", lambda h, d=dst, buf=buf: h.tensor_copy(d, psB[:, buf * 512:buf * 512 + BLK]),
                               waits=evw))
            else:
                ev = ("cA", op("scalar", lambda h, d=dst, buf=buf: h.activation(d, psB[:, buf * 512:buf * 512 + BLK], AF.Copy),
                               waits=evw))
            evac[idx] = ev
            if kind == 2:
                tm = tprev = 0
                tmbank = {0: 0, 1: 0}
                for tc in range(NTC):
                    slot = (NTC * b + tc) % 6
                    _w = {"cV": tmbank[tc % 2]}
                    _w[ev[0]] = max(_w.get(ev[0], 0), ev[1])
                    t1m = tr(psC[:, (tc % 2) * 1024:(tc % 2) * 1024 + 128], dst[:, tc * 128:(tc + 1) * 128], identb,
                             waits=_w)
                    tm = op("vector", lambda h, slot=slot, p=p, tc=tc: h.tensor_copy(
                        vtok[:, slot, p * 128:(p + 1) * 128], psC[:, (tc % 2) * 1024:(tc % 2) * 1024 + 128]),
                        waits={"cP": t1m})
                    tmbank[tc % 2] = tm
                    tprev = t1m
                prep_marks[idx] = {"cV": tm, "cG": 0, "cA": 0, "cP": tprev}
            else:
                bank = 1024 + (p % 2) * 512
                sq = chs[:, 4 + (p % 2), :]
                _w = {"cA": rnuse[p % 2]}
                _w[ev[0]] = max(_w.get(ev[0], 0), ev[1])
                pl = op("gpsimd", lambda h, d=dst, sq=sq: h.tensor_tensor(sq, d, d, ALU.mult), waits=_w)
                pe1 = mm(psA[:, bank:bank + BLK], oneshb, sq, start=True, stop=False,
                         waits={"cG": pl, "cA": rnuse[p % 2]})
                pe1 = mm(psA[:, bank:bank + BLK], onesb_row, eps_row, start=False, stop=True)
                a1 = op("scalar", lambda h, bank=bank: h.activation(fsc[:, 2, 0:BLK], psA[:, bank:bank + BLK], AF.Ln),
                        waits={"cP": pe1, "cV": prep_state["last"]})
                rnuse[p % 2] = a1
                a2 = op("scalar", lambda h, p=p: h.activation(rns[:, p % 2, :], fsc[:, 2, 0:BLK], AF.Exp, scale=-0.5),
                        waits={"cA": a1, "cV": prep_state["last"]})
                rn_a[(kind, p)] = a2
                _w = {"cA": a2, "dT": tld, "dW": pld}
                _w[ev[0]] = max(_w.get(ev[0], 0), ev[1])
                if kind == 1:
                    dsts = [(kring[:, p, (NTC * b + tc) % 6, :], tc * 128, 128) for tc in range(NTC)]
                    pm = prep(dst, rns[:, p % 2, :], dsts, tabs[:, 2, :], tabs[:, 3, :],
                              swap_bank=(p % 2) * 512, l=l, isq=False, waits=_w)
                else:
                    pm = prep(dst, rns[:, p % 2, :], [(qp[:, p, :], 0, BLK)], tabs[:, 0, :], tabs[:, 1, :],
                              swap_bank=(p % 2) * 512, l=l, isq=True, waits=_w)
                pm["cA"] = a2
                prep_marks[idx] = pm
        qkv_done = k.ctr["cP"]
        xrd_mark(b % 2)
        if b + 1 < NBLK:
            ldm[b + 1] = xrd_load(b + 1)[1]
        conv_step(14)
        for _oc in range(3):
            wo_load(woc[l], _oc, _oc)
        prep_all = {"cV": k.ctr["cV"], "cG": k.ctr["cG"], "cA": k.ctr["cA"]}
        pvev = {}
        scm = {}
        pvpe = {}
        last_a1 = 0
        last_tpe = 0
        for qt in range(NTC):
            g = NTC * b + qt
            kcs = [c for c in (g - 2, g - 1, g) if c >= 0]
            nkc = len(kcs)
            nk = nkc * 128
            for p in range(NP):
                pi = qt * NP + p
                sb0 = (pi % 2) * 1024
                sb1 = (pi % 2) * 1024 + 512
                pe0 = 0
                for i, kc in enumerate(kcs):
                    slot = kc % 6
                    w = {**prep_all, "cA": max(prep_all["cA"], scm.get(pi - 2, 0))} if i == 0 else None
                    mm(psA[:, sb0 + i * 128: sb0 + (i + 1) * 128], qp[0:64, p, qt * 128:(qt + 1) * 128],
                       kring[0:64, p, slot, :], start=True, stop=True, waits=w)
                    pe0 = mm(psA[:, sb1 + i * 128: sb1 + (i + 1) * 128], qp[64:128, p, qt * 128:(qt + 1) * 128],
                             kring[64:128, p, slot, :], start=True, stop=True)
                for hh, sb in ((0, sb0), (1, sb1)):
                    mrow = cst[:, CO["mask"] + 384 - nk: CO["mask"] + 384]
                    v1 = op("vector", lambda h, sb=sb, nk=nk, mrow=mrow: h.tensor_tensor(
                        fsc[:, 0, 0:nk], psA[:, sb:sb + nk], mrow, ALU.add),
                        waits={"cP": pe0, "cA": last_a1})
                    v2 = op("vector", lambda h, nk=nk: h.reduce_max(
                        fsc[:, 1, 0:1], fsc[:, 0, 0:nk], mybir.AxisListType.X, negate=True), waits={"cV": v1})
                    a1 = op("scalar", lambda h, nk=nk: h.activation(
                        pbf[:, 0, 0:nk], fsc[:, 0, 0:nk], AF.Exp, bias=fsc[:, 1, 0:1], accum_out=fsc[:, 1, 1:2]),
                        waits={"cV": v2, "cP": last_tpe})
                    last_a1 = a1
                    v3 = op("vector", lambda h: h.reciprocal(fsc[:, 1, 2:3], fsc[:, 1, 1:2]), waits={"cA": a1})
                    v4 = op("vector", lambda h, nk=nk: h.tensor_scalar_mul(pbf[:, 0, 0:nk], pbf[:, 0, 0:nk],
                                                                           fsc[:, 1, 2:3]), waits={"cV": v3})
                    tpe = 0
                    for i in range(nkc):
                        tpe = tr(psC[:, (pi % 2) * 1024 + i * 128:(pi % 2) * 1024 + (i + 1) * 128],
                                 pbf[:, 0, i * 128:(i + 1) * 128], identb, waits={"cV": v4})
                    last_tpe = tpe
                    v5 = op("vector", lambda h, pi=pi, nk=nk: h.tensor_copy(
                        pbf[:, 1, 0:nk], psC[:, (pi % 2) * 1024:(pi % 2) * 1024 + nk]), waits={"cP": tpe})
                    scm[pi] = max(scm.get(pi, 0), a1)
                    pvbank = 512 * ((pi // 4) % 2)
                    col = (pi % 4) * 128
                    for i, kc in enumerate(kcs):
                        slot = kc % 6
                        w = {"cV": max(v5, pvev.get((pi // 4) - 2, 0))}
                        pvpe[pi] = mm(psB[hh * 64:(hh + 1) * 64, pvbank + col: pvbank + col + 128],
                                      vtok[:, slot, p * 128 + hh * 64: p * 128 + hh * 64 + 64],
                                      pbf[:, 1, i * 128:(i + 1) * 128],
                                      start=(i == 0), stop=(i == nkc - 1), waits=w)
                if pi % 4 == 3:
                    grp = pi // 4
                    pvev[grp] = op("vector", lambda h, qt=qt, p=p, pvbank=512 * (grp % 2): h.tensor_copy(
                        aout[:, p - 3:p + 1, qt * 128:(qt + 1) * 128], psB[:, pvbank:pvbank + 512]),
                        waits={"cP": pvpe[pi]})
        att_done = {"cV": k.ctr["cV"]}
        stm = {}
        stq = {0: k.ctr["dXs0"], 1: k.ctr["dXs1"]}
        for oc in range(NCH):
            slot = oc % 3
            buf = oc % 2
            ld = dma("scalar", f"dXl{buf}", xfio[:, buf, :], xmas[oc, :, t0:t0 + BLK],
                     waits={"cV": stm.get(oc - 2, 0), "dXs0": k.ctr["dXs0"], "dXs1": k.ctr["dXs1"]})
            pe0 = 0
            for c in range(NCH):
                w = None
                if c == 0:
                    w = {wo_ring["m"][oc][0]: wo_ring["m"][oc][1], "cV": max(att_done["cV"], stm.get(oc - 2, 0))}
                pe0 = mm(psB[:, buf * 512:buf * 512 + BLK], wor[:, slot, c, :], aout[:, c, :],
                         start=(c == 0), stop=(c == NCH - 1), waits=w)
            wo_ring["use"][slot] = pe0
            if oc + 3 < NCH:
                wo_load(woc[l], oc + 3, (oc + 3) % 3)
            v1 = op("vector", lambda h, buf=buf: h.tensor_tensor(
                xfio[:, 2 + buf, :], xfio[:, buf, :], psB[:, buf * 512:buf * 512 + BLK], ALU.add),
                waits={"cP": pe0, f"dXl{buf}": ld, f"dXs{buf}": stq[buf]})
            stm[oc] = v1
            stq[buf] = dma("scalar", f"dXs{buf}", xmas[oc, :, t0:t0 + BLK], xfio[:, 2 + buf, :], waits={"cV": v1})
            xb_store(oc, t0, xfio[:, 2 + buf, :], buf, v1)
        k.ph.flush()

    def ffn_phase(l):
        cw = convq["done"][l]
        for c in range(NCH):
            tap(f"attn{l}", xmas[c], idx=np.s_[c])
        lw["w2"] = dma("sync", "dW", warena[:, 0:NJ * 1024].rearrange("p (j n) -> p j n", j=NJ),
                       w2c[l].transpose([1, 0, 2]),
                       waits={"dCs0": cw["dCs0"], "dCs1": cw["dCs1"], "cP": k.ctr["cP"]})
        ldm = {0: xrd_load(0)[1]}
        for b in range(NBLK):
            ffn_block(l, b, ldm)

    def ffn_block(l, b, ldm):
        t0 = b * BLK
        w1m = {}
        ring_free = {}

        def w1_load(j):
            slot = j % 4
            m = 0
            for ag in range(2):
                m = dma("sync", f"dW1{slot % 2}", w1r[:, slot, ag, :, :], w1c[l, ag, j].transpose([1, 0, 2]),
                        waits={"cP": ring_free.get(slot, 0)})
            w1m[j] = (f"dW1{slot % 2}", m)
        for jj in range(4):
            w1_load(jj)
        srcs = [xsl(b % 2, c, True) for c in range(NCH)]
        pe = ln_stats(srcs, 1024, {"cV": k.ctr["cV"], "cG": k.ctr["cG"], "cP": k.ctr["cP"],
                                   f"dXr{b % 2}": ldm[b]})
        rstd = fsc[:, 3, 0:BLK]
        a3 = ln_chain(1024, rstd, {"cP": pe})
        xn_done = 0
        for c in range(NCH):
            v1 = op("vector", lambda h, c=c, s=srcs[c]: h.scalar_tensor_tensor(
                ga[:, c % 2, :], psA[:, 1024:1024 + BLK], -RD, s, ALU.mult, ALU.add),
                waits={"cA": a3, "cP": pe, "cV": xn_done})
            v2 = op("vector", lambda h, c=c: h.tensor_tensor(ga[:, c % 2, :], ga[:, c % 2, :], rstd, ALU.mult),
                    waits={"cV": v1})
            xn_done = op("vector", lambda h, c=c: h.tensor_scalar(
                xnb[:, c, :], ga[:, c % 2, :], parc(f"lng{l}", c), parc(f"lnb{l}", c), ALU.mult, ALU.add),
                waits={"cV": v2, **w0})
        xrd_mark(b % 2)
        if b + 1 < NBLK:
            ldm[b + 1] = xrd_load(b + 1)[1]
        conv_step(7)
        um = {}
        for j in range(NJ):
            jn = 128 if j < NJ - 1 else JL
            slot = j % 4
            ab = (j % 2) * 1024
            gbk = (j % 2) * 1024 + 512
            pe0 = 0
            for c in range(NCH):
                w = None
                if c == 0:
                    w = {w1m[j][0]: w1m[j][1], "cV": xn_done, "cG": um.get(j - 2, 0)}
                mm(psA[0:jn, ab:ab + BLK], w1r[:, slot, 0, c, 0:jn], xnb[:, c, :],
                   start=(c == 0), stop=(c == NCH - 1), waits=w)
                pe0 = mm(psA[0:jn, gbk:gbk + BLK], w1r[:, slot, 1, c, 0:jn], xnb[:, c, :],
                         start=(c == 0), stop=(c == NCH - 1))
            ring_free[slot] = pe0
            if j + 4 < NJ:
                w1_load(j + 4)
            ga1 = op("scalar", lambda h, j=j, jn=jn, gbk=gbk: h.activation(
                chs[0:jn, 0, :], psA[0:jn, gbk:gbk + BLK], AF.Gelu, bias=parc(f"b1g{l}", j, jn)),
                waits={"cP": pe0, "cG": um.get(j - 1, 0)})
            v1 = op("vector", lambda h, j=j, jn=jn, ab=ab: h.tensor_scalar(
                chs[0:jn, 1, :], psA[0:jn, ab:ab + BLK], parc(f"b1a{l}", j, jn), None, ALU.add),
                waits={"cP": pe0, "cG": um.get(j - 1, 0)})
            um[j] = op("gpsimd", lambda h, j=j, jn=jn: h.tensor_tensor(
                usb[0:jn, j, :], chs[0:jn, 1, :], chs[0:jn, 0, :], ALU.mult),
                waits={"cA": ga1, "cV": v1})
        conv_step(7)
        u_all = {"cG": k.ctr["cG"]}
        stm = {}
        stq = {0: k.ctr["dXs0"], 1: k.ctr["dXs1"]}
        for oc in range(NCH):
            buf = oc % 2
            ld = dma("scalar", f"dXl{buf}", xfio[:, buf, :], xmas[oc, :, t0:t0 + BLK],
                     waits={"cV": stm.get(oc - 2, 0), "dXs0": k.ctr["dXs0"], "dXs1": k.ctr["dXs1"]})
            pe0 = 0
            for j in range(NJ):
                jn = 128 if j < NJ - 1 else JL
                w = None
                if j == 0:
                    w = {**u_all, "dW": lw["w2"], "cV": stm.get(oc - 2, 0)}
                pe0 = mm(psB[:, buf * 512:buf * 512 + BLK],
                         warena[0:jn, j * 1024 + oc * 128: j * 1024 + (oc + 1) * 128],
                         usb[0:jn, j, :], start=(j == 0), stop=(j == NJ - 1), waits=w)
            v1 = op("vector", lambda h, oc=oc, buf=buf: h.scalar_tensor_tensor(
                xfio[:, 2 + buf, :], psB[:, buf * 512:buf * 512 + BLK], parc(f"b2{l}", oc), xfio[:, buf, :],
                ALU.add, ALU.add), waits={"cP": pe0, f"dXl{buf}": ld, f"dXs{buf}": stq[buf]})
            stm[oc] = v1
            stq[buf] = dma("scalar", f"dXs{buf}", xmas[oc, :, t0:t0 + BLK], xfio[:, 2 + buf, :], waits={"cV": v1})
            xb_store(oc, t0, xfio[:, 2 + buf, :], buf, v1)
        k.ph.flush()

    def epi_block(b, ldm):
        t0 = b * BLK
        srcs = [xsl(b % 2, c, False) for c in range(NCH)]
        pe = ln_stats(srcs, 1024, {"cV": k.ctr["cV"], "cG": k.ctr["cG"], "cP": k.ctr["cP"],
                                   f"dXr{b % 2}": ldm[b]})
        rstd = fsc[:, 3, 0:BLK]
        a3 = ln_chain(1024, rstd, {"cP": pe})
        stm = {}
        stq = {0: k.ctr["dOut"], 1: k.ctr["dOut"]}
        for c in range(NCH):
            buf = c % 2
            ld = dma("scalar", f"dXl{buf}", xfio[:, buf, :], x1m[c, :, t0:t0 + BLK],
                     waits={"cV": stm.get(c - 2, 0), "dXs0": k.ctr["dXs0"], "dXs1": k.ctr["dXs1"]})
            v1 = op("vector", lambda h, c=c, s=srcs[c]: h.scalar_tensor_tensor(
                ga[:, c % 2, :], psA[:, 1024:1024 + BLK], -RD, s, ALU.mult, ALU.add),
                waits={"cA": a3, "cP": pe, "cV": stm.get(c - 2, 0)})
            v2 = op("vector", lambda h, c=c: h.tensor_tensor(ga[:, c % 2, :], ga[:, c % 2, :], rstd, ALU.mult),
                    waits={"cV": v1})
            v3 = op("vector", lambda h, c=c, buf=buf: h.tensor_tensor(xfio[:, 2 + buf, :], ga[:, c % 2, :],
                                                                      xfio[:, buf, :], ALU.add),
                    waits={"cV": v2, f"dXl{buf}": ld, "dOut": stq[buf]})
            stm[c] = v3
            stq[buf] = dma("scalar", "dOut", outT[c, :, t0:t0 + BLK], xfio[:, 2 + buf, :], waits={"cV": v3})
        xrd_mark(b % 2)
        if b + 1 < NBLK:
            ldm[b + 1] = xrd_load(b + 1)[1]
        k.ph.flush()

    k.ph.flush()
    for b in range(NBLK):
        gla_block(b)
    for l in range(NL):
        if l + 1 < NL:
            conv_push(l + 1)
        attn_phase(l)
        ffn_phase(l)
    ldm_e = {0: xrd_load(0)[1]}
    for b in range(NBLK):
        epi_block(b, ldm_e)
    for q in ["dOut", "dTap", "dXs0", "dXs1", "dXb0", "dXb1"]:
        k.emit("sync", lambda h, q=q: h.wait_ge(k.sems[q], k.ctr[q]))
    k.ph.flush()
    return nc


def _in_maps(inputs):
    c = host_consts()
    par = host_par(inputs)
    f32c = lambda a: np.ascontiguousarray(np.asarray(a, np.float32))
    shared = dict(
        wqT=f32c(np.asarray(inputs["L0_attn_Wq"]).T), wkT=f32c(np.asarray(inputs["L0_attn_Wk"]).T),
        wvT=f32c(np.asarray(inputs["L0_attn_Wv"]).T), woTg=f32c(np.asarray(inputs["L0_attn_Wo"]).T),
        wqkv_in=f32c(inputs["L0_ffn_Wqkv"]), wo_in=f32c(inputs["L0_ffn_Wo"]),
        w1_in=f32c(inputs["L0_ffn_W1"]), w2_in=f32c(inputs["L0_ffn_W2"]),
        par_in=par,
        c_ident=c["ident"], c_swapT=c["swapT"], c_onesf=c["onesf"], c_onesh=c["onesh"],
        c_mask=c["masklog"], c_dmask=c["dmask"],
        c_ctq=c["ctq"], c_stq=c["stq"], c_ctk=c["ctk"], c_stk=c["stk"],
    )
    x = np.asarray(inputs["L0_attn_x"], np.float32)
    maps = []
    for core in range(8):
        m = dict(shared)
        m["xT"] = np.ascontiguousarray(x[core].T)
        maps.append(m)
    return maps


def kernel(**inputs):
    if "nc" not in _bcache:
        _bcache["nc"] = build()
    nc = _bcache["nc"]
    maps = _in_maps(inputs)
    res = run_bass_kernel_spmd(nc, maps, core_ids=list(range(8)))
    out = np.empty((8, T, D), np.float32)
    for core in range(8):
        out[core] = res.results[core]["outT"].reshape(D, T).T
    return out


# version 09
# speedup vs baseline: 1.0127x; 1.0127x over previous
import numpy as np

import concourse.bass as bass
import concourse.mybir as mybir
from concourse.bass_utils import run_bass_kernel_spmd

F32 = mybir.dt.float32
BF16 = mybir.dt.bfloat16
AF = mybir.ActivationFunctionType
ALU = mybir.AluOpType

D = 1024
T = 4096
NCH = 8
H = 16
DK = 64
NP = 8
NL = 6
DI = 2730
NJ = 22
JL = 42
BLK = 512
NBLK = T // BLK
NTC = BLK // 128
NEG = -1e30
EPS_LN = 1e-5
EPS_L2 = 1e-19
EPS_GLA = 1e-19
RD = 1.0 / D

ENGINES = ("sync", "tensor", "vector", "scalar", "gpsimd")
EX = {"tensor": "cP", "vector": "cV", "scalar": "cA", "gpsimd": "cG"}
_bcache = {}

_PC = {}
_cidx = 0
def _pc(name, n=1):
    global _cidx
    _PC[name] = _cidx
    _cidx += n
for _p in range(NP):
    _pc(f"wg{_p}"); _pc(f"bg{_p}"); _pc(f"wl{_p}"); _pc(f"bl{_p}")
for _l in range(NL):
    _pc(f"qs{_l}"); _pc(f"qssw{_l}"); _pc(f"ks{_l}"); _pc(f"kssw{_l}")
    _pc(f"lng{_l}", NCH); _pc(f"lnb{_l}", NCH); _pc(f"b2{_l}", NCH)
    _pc(f"b1a{_l}", NJ); _pc(f"b1g{_l}", NJ)
PARW = _cidx


def host_consts():
    if "c" in _bcache:
        return _bcache["c"]
    ident = np.eye(128, dtype=np.float32)
    S = np.zeros((128, 128), np.float32)
    for b0 in (0, 64):
        for d in range(64):
            S[b0 + d, b0 + (d + 32) % 64] = 1.0
    swapT = np.ascontiguousarray(S.T)
    onesf = np.ones((128, 128), np.float32)
    onesh = np.zeros((128, 128), np.float32)
    onesh[:64, :64] = 1.0
    onesh[64:, 64:] = 1.0
    ii = np.arange(128)[:, None]
    jj = np.arange(128)[None, :]
    masklog = np.zeros((128, 384), np.float32)
    masklog[:, :128] = np.where(ii <= jj, 0.0, NEG)
    masklog[:, 256:] = np.where(ii >= jj, 0.0, NEG)
    dmask = np.where(ii <= jj, 0.0, NEG).astype(np.float32)
    half = np.arange(0, DK, 2, dtype=np.float32)
    inv_freq = 1.0 / 10000.0 ** (half / DK)
    t = np.arange(T, dtype=np.float32)
    freqs = np.concatenate([t[:, None] * inv_freq[None, :]] * 2, -1)
    base_scale = (half + 0.4 * DK) / (1.4 * DK)
    power = (t - T // 2) / 128.0
    scale = np.concatenate([base_scale[None, :] ** power[:, None]] * 2, -1)
    cos, sin = np.cos(freqs), np.sin(freqs)
    sign = np.concatenate([-np.ones(32), np.ones(32)]).astype(np.float32)
    s8 = np.sqrt(8.0)
    dup = lambda a: np.ascontiguousarray(np.concatenate([a, a], 0), dtype=np.float32)
    c = dict(
        ident=ident, swapT=swapT, onesf=onesf, onesh=onesh, masklog=masklog, dmask=dmask,
        ctq=dup((cos * scale * s8).T), stq=dup((sin * scale * sign[None, :] * s8).T),
        ctk=dup((cos / scale * s8).T), stk=dup((sin / scale * sign[None, :] * s8).T),
    )
    _bcache["c"] = c
    return c


def host_par(ins):
    par = np.zeros((128, PARW), np.float32)
    dup2 = lambda v: np.concatenate([v, v])
    for p in range(NP):
        for nm, key in [("wg", "L0_attn_W_gamma"), ("bg", "L0_attn_b_gamma"),
                        ("wl", "L0_attn_W_lambda"), ("bl", "L0_attn_b_lambda")]:
            arr = np.asarray(ins[key], np.float32)
            col = np.empty(128, np.float32)
            col[:64] = arr[2 * p]
            col[64:] = arr[2 * p + 1]
            par[:, _PC[f"{nm}{p}"]] = col
    for l in range(NL):
        qs = np.asarray(ins["L0_ffn_q_scale"][l], np.float32)
        ks = np.asarray(ins["L0_ffn_k_scale"][l], np.float32)
        sw = lambda v: np.concatenate([v[32:], v[:32]])
        par[:, _PC[f"qs{l}"]] = dup2(qs)
        par[:, _PC[f"qssw{l}"]] = dup2(sw(qs))
        par[:, _PC[f"ks{l}"]] = dup2(ks)
        par[:, _PC[f"kssw{l}"]] = dup2(sw(ks))
        par[:, _PC[f"lng{l}"]:_PC[f"lng{l}"] + NCH] = np.asarray(ins["L0_ffn_ln_g"][l], np.float32).reshape(NCH, 128).T
        par[:, _PC[f"lnb{l}"]:_PC[f"lnb{l}"] + NCH] = np.asarray(ins["L0_ffn_ln_b"][l], np.float32).reshape(NCH, 128).T
        par[:, _PC[f"b2{l}"]:_PC[f"b2{l}"] + NCH] = np.asarray(ins["L0_ffn_b2"][l], np.float32).reshape(NCH, 128).T
        b1 = np.asarray(ins["L0_ffn_b1"][l], np.float32)
        b1p = np.zeros((2, NJ * 128), np.float32)
        b1p[0, :DI] = b1[:DI]
        b1p[1, :DI] = b1[DI:]
        par[:, _PC[f"b1a{l}"]:_PC[f"b1a{l}"] + NJ] = b1p[0].reshape(NJ, 128).T
        par[:, _PC[f"b1g{l}"]:_PC[f"b1g{l}"] + NJ] = b1p[1].reshape(NJ, 128).T
    return par


class Ph:
    def __init__(self, k):
        self.k = k
        self.fns = {e: [] for e in ENGINES}

    def flush(self):
        if not any(self.fns.values()):
            return
        with self.k.nc.Block() as blk:
            for e in ENGINES:
                if self.fns[e]:
                    def body(h, fs=tuple(self.fns[e])):
                        for f in fs:
                            f(h)
                    getattr(blk, e)(body)
        self.fns = {e: [] for e in ENGINES}


class Kern:
    def __init__(self, nc):
        self.nc = nc
        self.ctr = {}
        self.sems = {}
        self.inc = {}
        self.lastw = {}
        self.ph = Ph(self)

    def mksem(self, name, inc):
        h = self.nc.semaphore(name).__enter__()
        self.sems[name] = h
        self.ctr[name] = 0
        self.inc[name] = inc

    def emit(self, eng, fn, tick=None, waits=None):
        wl = []
        if waits:
            for q, v in waits.items():
                if not v or v <= 0:
                    continue
                assert v <= self.ctr[q], f"wait on future count {q}={v}>{self.ctr[q]}"
                key = (eng, q)
                if self.lastw.get(key, -1) >= v:
                    continue
                self.lastw[key] = v
                wl.append((self.sems[q], v))
        tk = None
        if tick is not None:
            self.ctr[tick] += self.inc[tick]
            tk = (self.sems[tick], self.inc[tick])

        def closure(h, fn=fn, wl=tuple(wl), tk=tk):
            for s, v in wl:
                h.wait_ge(s, v)
            inst = fn(h)
            if tk is not None:
                inst.then_inc(tk[0], tk[1])
        self.ph.fns[eng].append(closure)
        return self.ctr[tick] if tick else None


def build(taps=()):
    nc = bass.Bass("TRN2", target_bir_lowering=False)
    k = Kern(nc)
    dsems = ["dW", "dT", "dWo0", "dWo1", "dWo2", "dW10", "dW11", "dCv0", "dCv1",
             "dCs0", "dCs1", "dXl0", "dXl1", "dXs0", "dXs1", "dOut", "dTap",
             "dXr0", "dXr1", "dXb0", "dXb1"]
    for q in dsems:
        k.mksem(q, 16)
    for q in ["cP", "cV", "cA", "cG"]:
        k.mksem(q, 1)

    def din(name, shape, dtype=F32):
        return nc.dram_tensor(name, list(shape), dtype, kind="ExternalInput").ap()

    xT = din("xT", (D, T))
    wqT = din("wqT", (D, D)); wkT = din("wkT", (D, D)); wvT = din("wvT", (D, D)); woTg = din("woTg", (D, D))
    wqkv_in = din("wqkv_in", (NL, D, 3 * D))
    wo_in = din("wo_in", (NL, D, D))
    w1_in = din("w1_in", (NL, D, 2 * DI))
    w2_in = din("w2_in", (NL, DI, D))
    par_in = din("par_in", (128, PARW))
    c_ident = din("c_ident", (128, 128)); c_swapT = din("c_swapT", (128, 128))
    c_onesf = din("c_onesf", (128, 128)); c_onesh = din("c_onesh", (128, 128))
    c_mask = din("c_mask", (128, 384)); c_dmask = din("c_dmask", (128, 128))
    ctabs = [din("c_ctq", (128, T)), din("c_stq", (128, T)), din("c_ctk", (128, T)), din("c_stk", (128, T))]

    outT = nc.dram_tensor("outT", [NCH, 128, T], F32, kind="ExternalOutput").ap()
    tap_t = {}
    for name, shape, *dt in taps:
        tdt = BF16 if (dt and dt[0] == "bf16") else F32
        tap_t[name] = nc.dram_tensor("tap_" + name, list(shape), tdt, kind="ExternalOutput").ap()

    xmas = nc.dram_tensor("xmas", [NCH, 128, T], F32, kind="Internal").ap()
    xbfh = nc.dram_tensor("xbfh", [NCH, 128, T + 1], BF16, kind="Internal").ap()
    x1m = nc.dram_tensor("x1m", [NCH, 128, T], F32, kind="Internal").ap()
    qkvc = nc.dram_tensor("qkvc", [NL, NCH, 128, 3 * D], BF16, kind="Internal").ap()
    woc = nc.dram_tensor("woc", [NL, NCH, 128, D], BF16, kind="Internal").ap()
    wotc = nc.dram_tensor("wotc", [NCH, 128, D], BF16, kind="Internal").ap()
    w1c = nc.dram_tensor("w1c", [NL, 2, NJ, NCH, 128, 128], BF16, kind="Internal").ap()
    w2c = nc.dram_tensor("w2c", [NL, NJ, 128, D], BF16, kind="Internal").ap()

    def S(name, shape, dtype):
        return nc.alloc_sbuf_tensor(name, list(shape), dtype)

    warena = S("warena", (128, 24576), BF16)
    wor = S("wor", (128, 3, NCH, 128), BF16)
    w1r = S("w1r", (128, 4, 2, NCH, 128), BF16)
    tabs = S("tabs", (128, 4, BLK), F32)
    xfio = S("xfio", (128, 4, BLK), F32)
    cvf = S("cvf", (128, 2, 512), F32)
    cvb = S("cvb", (128, 2, 512), BF16)
    qp = S("qp", (128, NP, BLK), BF16)
    ovl = S("ovl", (128, 12288), BF16)
    aout = S("aout", (128, NCH, BLK), BF16)
    chs = S("chs", (128, 6, BLK), BF16)
    rns = S("rns", (128, 2, BLK), BF16)
    fsc = S("fsc", (128, 4, BLK), F32)
    pbf = S("pbf", (128, 5, 384), BF16)
    ga = S("ga", (128, 2, BLK), F32)
    gb = S("gb", (128, 2, BLK), F32)
    gc = S("gc", (128, 2, BLK), F32)
    xrd = S("xrd", (128, 2, NCH, BLK + 1), BF16)
    sstg = S("sstg", (128, 2, BLK), BF16)
    xnb = S("xnb", (128, NCH, BLK), BF16)
    ssb = S("ssb", (128, NP, 64), F32)
    sbfA = S("sbfA", (128, NP, 64), BF16)
    sbfB = S("sbfB", (128, NP, 64), BF16)
    cst = S("cst", (128, 1056), F32)
    cbt = S("cbt", (128, 1024), BF16)
    par = S("par", (128, PARW), F32)

    rem = nc.sbuf_bytes_remaining
    assert rem >= 0, f"SBUF over budget by {-rem} bytes"

    kring = ovl.ap()[:, 0:6144].rearrange("p (a b c) -> p a b c", a=NP, b=6)
    vtok = ovl.ap()[:, 6144:12288].rearrange("p (a b) -> p a b", a=6)
    usb = ovl.ap()[:, 0:NJ * BLK].rearrange("p (a b) -> p a b", a=NJ)
    kview = ovl.ap()[:, 0:4096].rearrange("p (a b) -> p a b", a=NP)
    vview = ovl.ap()[:, 4096:8192]
    t1bv = ovl.ap()[:, 8192:12288].rearrange("p (a b) -> p a b", a=NCH)

    psA = nc.alloc_psum_tensor("psA", [128, 2048], F32)
    psB = nc.alloc_psum_tensor("psB", [128, 1024], F32)
    psC = nc.alloc_psum_tensor("psC", [128, 2048], BF16)

    CO = dict(ident=0, swapT=128, onesf=256, onesh=384, mask=512, dmask=896)
    CB = dict(identb=0, swapTb=128, onesfb=256, oneshb=384, epsrow=512)
    GW = {"wq": 0, "wk": 8192, "wv": 16384}

    def dma(queue, sem, out, in_, waits=None):
        return k.emit(queue, lambda h, o=out, i=in_: h.dma_start(out=o, in_=i), tick=sem, waits=waits)

    def op(eng, fn, waits=None):
        return k.emit(eng, fn, tick=EX[eng], waits=waits)

    def mm(out_ap, lhsT_ap, rhs_ap, start, stop, waits=None):
        return op("tensor", lambda h, o=out_ap, l=lhsT_ap, r=rhs_ap, s1=start, s2=stop:
                  h.matmul(o, l, r, start=s1, stop=s2), waits=waits)

    def tr(out_ap, in_ap, identity, waits=None):
        return op("tensor", lambda h, o=out_ap, i=in_ap, d=identity: h.transpose(o, i, d), waits=waits)

    def allnow():
        return dict(k.ctr)

    def parc(name, off=0, rows=128):
        c0 = _PC[name] + off
        return par[0:rows, c0:c0 + 1]

    def xsl(slot, c, shifted):
        off = (0 if (shifted and c >= 4) else 1)
        return xrd[:, slot, c, off:off + BLK]

    xr_gate = {0: {}, 1: {}}

    def xrd_load(b):
        slot = b % 2
        ld = dma("scalar", f"dXr{slot}", xrd[:, slot, :, :],
                 xbfh[:, :, b * BLK: b * BLK + BLK + 1].transpose([1, 0, 2]),
                 waits=xr_gate[slot])
        return slot, ld

    def xrd_mark(slot):
        xr_gate[slot] = {"cP": k.ctr["cP"], "cV": k.ctr["cV"], "cG": k.ctr["cG"]}

    def tap(name, src_ap, idx=None, waits=None):
        if name in tap_t:
            dst = tap_t[name] if idx is None else tap_t[name][idx]
            dma("scalar", "dTap", dst, src_ap, waits=waits if waits is not None else allnow())

    onesb_row = cbt[0:1, CB["onesfb"]:CB["onesfb"] + 128]
    onesf_row = cst[0:1, CO["onesf"]:CO["onesf"] + 128]
    eps_row = cbt[0:1, CB["epsrow"]:CB["epsrow"] + BLK]
    identb = cbt[:, CB["identb"]:CB["identb"] + 128]
    swapTb = cbt[:, CB["swapTb"]:CB["swapTb"] + 128]
    onesfb = cbt[:, CB["onesfb"]:CB["onesfb"] + 128]
    oneshb = cbt[:, CB["oneshb"]:CB["oneshb"] + 128]

    dma("sync", "dW", cst[:, CO["ident"]:CO["ident"] + 128], c_ident)
    dma("sync", "dW", cst[:, CO["swapT"]:CO["swapT"] + 128], c_swapT)
    dma("sync", "dW", cst[:, CO["onesf"]:CO["onesf"] + 128], c_onesf)
    dma("sync", "dW", cst[:, CO["onesh"]:CO["onesh"] + 128], c_onesh)
    dma("sync", "dW", cst[:, CO["mask"]:CO["mask"] + 384], c_mask)
    dma("sync", "dW", cst[:, CO["dmask"]:CO["dmask"] + 128], c_dmask)
    pld = dma("sync", "dW", par[:, :], par_in)
    w0 = {"dW": pld}
    for nm, dst in [("ident", "identb"), ("swapT", "swapTb"), ("onesf", "onesfb"), ("onesh", "oneshb")]:
        op("gpsimd", lambda h, s=CO[nm], d=CB[dst]: h.tensor_copy(cbt[:, d:d + 128], cst[:, s:s + 128]), waits=w0)
    op("gpsimd", lambda h: h.memset(cbt[0:1, CB["epsrow"]:CB["epsrow"] + BLK], EPS_L2))
    op("gpsimd", lambda h: h.memset(cst[:, 1024:1025], EPS_LN))
    op("gpsimd", lambda h: h.memset(ssb[:, :, :], 0.0))
    op("gpsimd", lambda h: h.memset(sbfA[:, :, :], 0.0))
    op("gpsimd", lambda h: h.memset(sbfB[:, :, :], 0.0))
    zc = op("gpsimd", lambda h: h.memset(sstg[:, 0, 0:16], 0.0))
    k.emit("scalar", lambda h: h.dma_start(out=xbfh[:, :, 0:1].transpose([1, 0, 2]),
                                           in_=sstg[:, 0:1, 0:NCH].transpose([0, 2, 1]),
                                           allow_slow_non_contiguous=True),
           tick="dXb0", waits={"cG": zc})

    cvst = {"i": 0, 0: {"rd": ("cG", 0), "st": 0}, 1: {"rd": ("cG", 0), "st": 0}}

    def convert_piece(src_ap, dst_sb=None, dst_dram=None, rows=128):
        i = cvst["i"] % 2
        cvst["i"] += 1
        g = cvst[i]
        n = src_ap.shape[-1]
        assert n <= 512
        ld = dma("sync", f"dCv{i}", cvf[0:rows, i, 0:n], src_ap,
                 waits={g["rd"][0]: g["rd"][1], f"dCs{i}": g["st"]})
        if dst_sb is not None:
            m = op("gpsimd", lambda h, r=rows, i=i, n=n, d=dst_sb: h.tensor_copy(d, cvf[0:r, i, 0:n]),
                   waits={f"dCv{i}": ld})
            g["rd"] = ("cG", m)
            g["st"] = 0
        else:
            m = op("gpsimd", lambda h, r=rows, i=i, n=n: h.tensor_copy(cvb[0:r, i, 0:n], cvf[0:r, i, 0:n]),
                   waits={f"dCv{i}": ld, f"dCs{i}": g["st"]})
            g["rd"] = ("cG", m)
            g["st"] = dma("scalar", f"dCs{i}", dst_dram, cvb[0:rows, i, 0:n], waits={"cG": m})

    for nm, src in [("wq", wqT), ("wk", wkT), ("wv", wvT)]:
        for c in range(NCH):
            for piece in range(2):
                convert_piece(src[c * 128:(c + 1) * 128, piece * 512:(piece + 1) * 512],
                              dst_sb=warena[:, GW[nm] + c * 1024 + piece * 512: GW[nm] + c * 1024 + (piece + 1) * 512])
    for c in range(NCH):
        for piece in range(2):
            convert_piece(woTg[c * 128:(c + 1) * 128, piece * 512:(piece + 1) * 512],
                          dst_dram=wotc[c, :, piece * 512:(piece + 1) * 512])
    glaw = allnow()

    def conv_items(l):
        items = []
        for c in range(NCH):
            for p in range(6):
                items.append((wqkv_in[l, c * 128:(c + 1) * 128, p * 512:(p + 1) * 512],
                              qkvc[l, c, :, p * 512:(p + 1) * 512], 128))
        for c in range(NCH):
            for p in range(2):
                items.append((wo_in[l, c * 128:(c + 1) * 128, p * 512:(p + 1) * 512],
                              woc[l, c, :, p * 512:(p + 1) * 512], 128))
        for c in range(NCH):
            for ag in range(2):
                base = ag * DI
                for j0, j1 in [(0, 4), (4, 8), (8, 12), (12, 16), (16, 20), (20, 21)]:
                    items.append((w1_in[l, c * 128:(c + 1) * 128, base + j0 * 128: base + j1 * 128],
                                  w1c[l, ag, j0:j1, c, :, :].transpose([1, 0, 2]), 128))
                items.append((w1_in[l, c * 128:(c + 1) * 128, base + 21 * 128: base + DI],
                              w1c[l, ag, 21, c, :, 0:JL], 128))
        for j in range(NJ):
            rows = 128 if j < NJ - 1 else JL
            for p in range(2):
                items.append((w2_in[l, j * 128:j * 128 + rows, p * 512:(p + 1) * 512],
                              w2c[l, j, 0:rows, p * 512:(p + 1) * 512], rows))
        return items

    convq = {"items": [], "done": {}}

    def conv_push(l):
        convq["items"].extend((l, it) for it in conv_items(l))

    def conv_step(nmax):
        for _ in range(nmax):
            if not convq["items"]:
                return
            l, (src, dst, rows) = convq["items"].pop(0)
            convert_piece(src, dst_dram=dst, rows=rows)
            if not convq["items"] or convq["items"][0][0] != l:
                convq["done"][l] = allnow()

    conv_push(0)

    def ln_stats(srcs, bank, extra_waits=None):
        pe = 0
        ws = dict(extra_waits or {})
        for c, src in enumerate(srcs):
            sq = chs[:, 4 + (c % 2), :]
            pl = op("gpsimd", lambda h, s=src, d=sq: h.tensor_tensor(d, s, s, ALU.mult),
                    waits={**ws, "cP": pe})
            mm(psA[:, bank:bank + BLK], onesfb, src, start=(c == 0), stop=(c == NCH - 1), waits=ws if c == 0 else None)
            pe = mm(psA[:, bank + 512:bank + 512 + BLK], onesfb, sq, start=(c == 0), stop=(c == NCH - 1),
                    waits={"cG": pl})
        return pe

    def ln_chain(bank, rstd_dst, waits):
        mu2 = fsc[:, 2, 0:BLK]
        a1 = op("scalar", lambda h: h.activation(mu2, psA[:, bank:bank + BLK], AF.Square, scale=RD), waits=waits)
        v1 = op("vector", lambda h: h.scalar_tensor_tensor(rstd_dst, psA[:, bank + 512:bank + 512 + BLK], RD, mu2,
                                                           ALU.mult, ALU.subtract), waits={"cA": a1})
        a2 = op("scalar", lambda h: h.activation(rstd_dst, rstd_dst, AF.Ln, bias=cst[:, 1024:1025]), waits={"cV": v1})
        return op("scalar", lambda h: h.activation(rstd_dst, rstd_dst, AF.Exp, scale=-0.5), waits={"cA": a2})

    xbst = {0: 0, 1: 0}

    def xb_store(c, t0, src_f32, buf, vmark):
        pl = op("gpsimd", lambda h, s=src_f32, buf=buf: h.tensor_copy(sstg[:, buf, :], s),
                waits={"cV": vmark, f"dXb{buf}": xbst[buf]})
        xbst[buf] = dma("scalar", f"dXb{buf}", xbfh[c, :, 1 + t0:1 + t0 + BLK], sstg[:, buf, :],
                        waits={"cG": pl})
        return pl

    prep_state = {"g1": 0, "last": 0}

    def prep(src_bf, rn_ap, dst_writes, cS, sS, swap_bank, l, isq, waits):
        waits = dict(waits)
        waits["cG"] = max(waits.get("cG", 0), prep_state["g1"])
        waits["cV"] = max(waits.get("cV", 0), prep_state["last"])
        sw = mm(psA[:, swap_bank:swap_bank + BLK], swapTb, src_bf, start=True, stop=True, waits=waits)
        t1 = chs[:, 2, :]
        t2 = chs[:, 3, :]
        sc = ("qs", "qssw") if isq else ("ks", "kssw")
        v1 = op("vector", lambda h, s=src_bf: h.scalar_tensor_tensor(t1, s, parc(f"{sc[0]}{l}"), cS, ALU.mult, ALU.mult),
                waits=waits)
        v2 = op("vector", lambda h, sb=swap_bank: h.scalar_tensor_tensor(
            t2, psA[:, sb:sb + BLK], parc(f"{sc[1]}{l}"), sS, ALU.mult, ALU.mult),
            waits={"cP": sw, "cG": waits["cG"]})
        g1 = op("gpsimd", lambda h: h.tensor_tensor(t1, t1, t2, ALU.add), waits={"cV": v2})
        last = 0
        for dst, c0, n in dst_writes:
            last = op("vector", lambda h, d=dst, c0=c0, n=n: h.tensor_tensor(
                d, t1[:, c0:c0 + n], rn_ap[:, c0:c0 + n], ALU.mult), waits={"cG": g1})
        prep_state["g1"] = g1
        prep_state["last"] = last
        return {"cV": last, "cG": g1}

    wo_ring = {"use": {}, "m": {}}

    def wo_load(cache, oc, slot, conv_w=None):
        w = {"cP": wo_ring["use"].get(slot, 0)}
        if conv_w:
            w.update(conv_w)
        m = dma("sync", f"dWo{slot}", wor[:, slot, :, :],
                cache[:, :, oc * 128:(oc + 1) * 128].transpose([1, 0, 2]), waits=w)
        wo_ring["m"][oc] = (f"dWo{slot}", m)
        return m

    def gla_block(b):
        t0 = b * BLK
        cg = 0
        for c in range(NCH):
            sl = fsc[:, c % 2, 0:BLK]
            xld = dma("scalar", f"dXl{c % 2}", sl, xT[c * 128:(c + 1) * 128, t0:t0 + BLK], waits={"cG": cg})
            cg = op("gpsimd", lambda h, c=c, sl=sl: h.tensor_copy(xnb[:, c, :], sl), waits={f"dXl{c % 2}": xld})
        pe = ln_stats([xnb[:, c, :] for c in range(NCH)], 1024, {"cG": cg})
        rstd = fsc[:, 3, 0:BLK]
        a3 = ln_chain(1024, rstd, {"cP": pe})
        t1b = t1bv
        vdone = 0
        for c in range(NCH):
            v1 = op("vector", lambda h, c=c: h.scalar_tensor_tensor(
                ga[:, c % 2, :], psA[:, 1024:1024 + BLK], -RD, xnb[:, c, :], ALU.mult, ALU.add),
                waits={"cA": a3, "cP": pe, "cV": vdone})
            vdone = op("vector", lambda h, c=c: h.tensor_tensor(t1b[:, c, :], ga[:, c % 2, :], rstd, ALU.mult),
                       waits={"cV": v1})
        tap("t1b", ovl[:, 8192:12288].rearrange("p (a b) -> p a b", a=NCH), np.s_[:, :, t0:t0 + BLK])
        evm = {}
        for oc in range(24):
            kind, p = oc // NP, oc % NP
            buf = oc % 2
            pe0 = 0
            for c in range(NCH):
                w = None
                if c == 0:
                    w = {"cG": glaw["cG"], "cA": glaw["cA"], "cV": max(vdone, evm.get(oc - 2, 0))}
                sec = ("wq", "wk", "wv")[kind]
                pe0 = mm(psB[:, buf * 512:buf * 512 + BLK],
                         warena[:, GW[sec] + c * 1024 + p * 128: GW[sec] + c * 1024 + (p + 1) * 128],
                         t1b[:, c, :], start=(c == 0), stop=(c == NCH - 1), waits=w)
            if kind == 0:
                dst = qp[:, p, :]
            elif kind == 1:
                dst = kview[:, p, 0:BLK]
            else:
                dst = vview[:, p * BLK:(p + 1) * BLK]
            evm[oc] = op("vector", lambda h, d=dst, buf=buf: h.tensor_copy(d, psB[:, buf * 512:buf * 512 + BLK]),
                         waits={"cP": pe0})
        conv_step(16)
        evq = max(evm[oc] for oc in range(0, 8))
        evk = max(evm[oc] for oc in range(8, 16))
        evv = max(evm[oc] for oc in range(16, 24))
        khm = {}
        rn_a = {}
        rn_pe = {}
        for p in range(NP):
            bank = 1024 + (p % 2) * 512
            sq = chs[:, 4 + (p % 2), :]
            pl = op("gpsimd", lambda h, s=kview[:, p, 0:BLK], d=sq: h.tensor_tensor(d, s, s, ALU.mult),
                    waits={"cV": evk, "cP": rn_pe.get(p - 2, 0), "cA": rn_a.get(p - 2, 0)})
            pe1 = mm(psA[:, bank:bank + BLK], oneshb, sq, start=True, stop=False,
                     waits={"cG": pl, "cA": rn_a.get(p - 2, 0)})
            rn_pe[p] = pe1 = mm(psA[:, bank:bank + BLK], onesb_row, eps_row, start=False, stop=True)
            a1 = op("scalar", lambda h, bank=bank: h.activation(fsc[:, 2, 0:BLK], psA[:, bank:bank + BLK], AF.Ln),
                    waits={"cP": pe1})
            a2 = op("scalar", lambda h, p=p: h.activation(rns[:, p % 2, :], fsc[:, 2, 0:BLK], AF.Exp, scale=-0.5),
                    waits={"cA": a1, "cV": khm.get(p - 2, 0)})
            rn_a[p] = a2
            khm[p] = op("vector", lambda h, p=p: h.tensor_tensor(
                kview[:, p, 0:BLK], kview[:, p, 0:BLK], rns[:, p % 2, :], ALU.mult), waits={"cA": a2})
        tap("qp", qp[:, :, :], np.s_[b])
        tap("kh", ovl[:, 0:4096], np.s_[b])
        tap("vv", ovl[:, 4096:8192], np.s_[b])
        lastr = {"chs": 0, "vt_pe": 0, "yev": 0}
        dpipe = {}

        def pipeline(p):
            s = p % 2
            gap, gbp, gcp = ga[:, s, :], gb[:, s, :], gc[:, s, :]
            pe1 = mm(psA[:, 1024:1024 + BLK], oneshb, kview[:, p, 0:BLK], start=True, stop=True,
                     waits={"cV": max(khm[p], lastr["yev"], dpipe.get(p - 2, 0)), "cA": k.ctr["cA"]})
            v1 = op("vector", lambda h, gap=gap: h.tensor_scalar(gap, psA[:, 1024:1024 + BLK], 1.0 / DK, None, ALU.mult),
                    waits={"cP": pe1})
            s1 = op("scalar", lambda h, p=p, gap=gap, gbp=gbp: h.activation(
                gbp, gap, AF.Sigmoid, bias=parc(f"bg{p}"), scale=parc(f"wg{p}")), waits={"cV": v1, **w0})
            s2 = op("scalar", lambda h, p=p, gap=gap, gcp=gcp: h.activation(
                gcp, gap, AF.Sigmoid, bias=parc(f"bl{p}"), scale=parc(f"wl{p}")), waits={"cV": v1})
            v2 = op("vector", lambda h, gap=gap, gbp=gbp, gcp=gcp: h.tensor_tensor(gap, gcp, gbp, ALU.mult),
                    waits={"cA": s2})
            v2 = op("vector", lambda h, gap=gap, gbp=gbp: h.tensor_tensor(gap, gbp, gap, ALU.subtract),
                    waits={"cV": v2, "cA": s1})
            v3 = op("vector", lambda h, gap=gap, gbp=gbp: h.tensor_scalar(gbp, gap, -1.0, 1.0, ALU.mult, ALU.add),
                    waits={"cV": v2})
            v2 = op("vector", lambda h, gap=gap: h.tensor_scalar_max(gap, gap, EPS_GLA), waits={"cV": v2})
            v3 = op("vector", lambda h, gbp=gbp: h.tensor_scalar_max(gbp, gbp, EPS_GLA), waits={"cV": v3})
            s3 = op("scalar", lambda h, gap=gap, gcp=gcp: h.activation(gcp, gap, AF.Ln), waits={"cV": v2})
            s4 = op("scalar", lambda h, gap=gap, gbp=gbp: h.activation(gap, gbp, AF.Ln), waits={"cV": v3, "cA": s3})
            sc = 0
            for tc in range(NTC):
                sc = op("vector", lambda h, gbp=gbp, gcp=gcp, tc=tc: h.tensor_tensor_scan(
                    gbp[:, tc * 128:(tc + 1) * 128], gcp[:, tc * 128:(tc + 1) * 128],
                    gcp[:, tc * 128:(tc + 1) * 128], 0.0, ALU.add, ALU.bypass),
                    waits={"cA": s4})
            v4 = op("vector", lambda h, gap=gap, gbp=gbp, gcp=gcp: h.tensor_tensor(gcp, gap, gbp, ALU.subtract),
                    waits={"cV": sc, "cA": s4})
            dpipe[p] = v4

        pipeline(0)
        for p in range(NP):
            if p + 1 < NP:
                pipeline(p + 1)
            s = p % 2
            gbp, gcp = gb[:, s, :], gc[:, s, :]
            expL = chs[:, 0, :]
            qq = chs[:, 1, :]
            e1 = op("scalar", lambda h, gbp=gbp: h.activation(expL, gbp, AF.Exp),
                    waits={"cV": max(dpipe[p], lastr["chs"])})
            q2 = op("vector", lambda h, p=p: h.tensor_tensor(qq, qp[:, p, :], expL, ALU.mult),
                    waits={"cA": e1, "cV": evq, "cP": k.ctr["cP"]})
            for tc in range(NTC):
                c0 = tc * 128
                eL = fsc[:, 1, tc:tc + 1]
                eA = op("scalar", lambda h, gbp=gbp, eL=eL, c0=c0: h.activation(
                    eL, gbp[:, c0 + 127:c0 + 128], AF.Exp), waits={"cV": max(dpipe[p], lastr["chs"])})
                fv = chs[:, 2, 0:128]
                f1 = op("scalar", lambda h, gbp=gbp, gcp=gcp, c0=c0: h.activation(
                    chs[:, 2, 0:128], gcp[:, c0:c0 + 128], AF.Exp, bias=gbp[:, c0 + 127:c0 + 128]),
                    waits={"cV": max(dpipe[p], lastr["chs"])})
                f2 = op("vector", lambda h, p=p, c0=c0: h.tensor_tensor(
                    chs[:, 3, 0:128], vview[:, p * BLK + c0: p * BLK + c0 + 128], fv, ALU.mult),
                    waits={"cA": f1, "cV": evv, "cP": lastr["vt_pe"]})
                t1m = tr(psC[:, 0:128], kview[:, p, c0:c0 + 128], identb,
                         waits={"cV": max(khm[p], lastr["chs"])})
                ke = op("vector", lambda h: h.tensor_copy(chs[:, 4, 0:128], psC[:, 0:128]),
                        waits={"cP": t1m})
                t2m = tr(psC[:, 1024:1152], chs[:, 3, 0:128], identb, waits={"cV": f2})
                ve = op("vector", lambda h: h.tensor_copy(chs[:, 5, 0:128], psC[:, 1024:1152]),
                        waits={"cP": t2m})
                amk = {}
                for hh in range(2):
                    r0 = hh * 64
                    araw_bank = 0 if hh == 0 else 512
                    d_bank = 256 if hh == 0 else 768
                    ar = mm(psB[:, araw_bank:araw_bank + 128],
                            vview[r0:r0 + 64, p * BLK + c0: p * BLK + c0 + 128],
                            qp[r0:r0 + 64, p, c0:c0 + 128], start=True, stop=True,
                            waits={"cV": max(evv, evq, dpipe[p])})
                    ones_r = cst[r0:r0 + 1, CO["onesf"]:CO["onesf"] + 128]
                    d1 = mm(psA[:, d_bank:d_bank + 128], gcp[r0:r0 + 1, c0:c0 + 128], ones_r,
                            start=True, stop=True, waits={"cV": dpipe[p]})
                    d1 = mm(psA[:, d_bank:d_bank + 128], ones_r, gbp[r0:r0 + 1, c0:c0 + 128],
                            start=False, stop=True)
                    md = op("vector", lambda h, d_bank=d_bank: h.tensor_tensor(
                        fsc[:, 2, 0:128], psA[:, d_bank:d_bank + 128], cst[:, CO["dmask"]:CO["dmask"] + 128], ALU.add),
                        waits={"cP": d1, "cA": k.ctr["cA"]})
                    de = op("scalar", lambda h: h.activation(pbf[:, 2, 0:128], fsc[:, 2, 0:128], AF.Exp),
                            waits={"cV": md})
                    amk[hh] = op("vector", lambda h, hh=hh, araw_bank=araw_bank: h.tensor_tensor(
                        pbf[:, 2, 128 * (hh + 1):128 * (hh + 2)], psB[:, araw_bank:araw_bank + 128],
                        pbf[:, 2, 0:128], ALU.mult),
                        waits={"cA": de, "cP": ar})
                if b == 0 and p == 0 and tc == 0:
                    tap("dbg_at", pbf[:, 2, :])
                    tap("dbg_L", gb[:, 0, :])
                    tap("dbg_lnad", gc[:, 0, :])
                    tap("dbg_ktok", chs[:, 4, 0:128])
                    tap("dbg_qq", chs[:, 1, :])
                ylast = mm(psA[0:64, 1024:1152], chs[:, 4, 0:64], pbf[:, 2, 128:256], start=True, stop=False,
                           waits={"cV": max(ke, amk[0], q2), "cG": k.ctr["cG"]})
                mm(psA[0:64, 1024:1152], sbfA[:, p, :], qq[:, c0:c0 + 128], start=False, stop=False)
                mm(psA[64:128, 1024:1152], chs[:, 4, 64:128], pbf[:, 2, 256:384], start=True, stop=False,
                   waits={"cV": amk[1]})
                ylast = mm(psA[64:128, 1024:1152], sbfB[:, p, :], qq[:, c0:c0 + 128], start=False, stop=True)
                yev = op("vector", lambda h, p=p, c0=c0: h.tensor_copy(aout[:, p, c0:c0 + 128], psA[:, 1024:1152]),
                         waits={"cP": ylast})
                lastr["yev"] = yev
                wA = mm(psA[0:64, 1536:1600], chs[:, 5, 0:64], chs[:, 4, 0:64], start=True, stop=True,
                        waits={"cV": max(ve, ke), "cG": k.ctr["cG"]})
                wB = mm(psA[64:128, 1536:1600], chs[:, 5, 64:128], chs[:, 4, 64:128], start=True, stop=True)
                lastr["vt_pe"] = wB
                su = op("vector", lambda h, p=p, eL=eL: h.scalar_tensor_tensor(
                    ssb[:, p, :], ssb[:, p, :], eL, psA[:, 1536:1600], ALU.mult, ALU.add),
                    waits={"cP": wB, "cA": eA, "cG": k.ctr["cG"]})
                op("gpsimd", lambda h, p=p: h.tensor_copy(sbfA[0:64, p, :], ssb[0:64, p, :]),
                   waits={"cV": su, "cP": ylast})
                op("gpsimd", lambda h, p=p: h.tensor_copy(sbfB[64:128, p, :], ssb[64:128, p, :]),
                   waits={"cV": su})
                lastr["chs"] = max(yev, su)
        tap("glay", aout[:, :, :], np.s_[:, :, t0:t0 + BLK])
        rb = xnb
        evo = 0
        cvw = {"dCs0": glaw["dCs0"], "dCs1": glaw["dCs1"]}
        wo_load(wotc, 0, 0, cvw)
        wo_load(wotc, 1, 1, cvw)
        for oc in range(NCH):
            slot = oc % 3
            buf = oc % 2
            pe0 = 0
            for c in range(NCH):
                w = None
                if c == 0:
                    w = {"cV": max(lastr["yev"], evo), "cG": k.ctr["cG"], wo_ring["m"][oc][0]: wo_ring["m"][oc][1]}
                pe0 = mm(psB[:, buf * 512:buf * 512 + BLK], wor[:, slot, c, :], aout[:, c, :],
                         start=(c == 0), stop=(c == NCH - 1), waits=w)
            wo_ring["use"][slot] = pe0
            if oc + 2 < NCH:
                wo_load(wotc, oc + 2, (oc + 2) % 3, cvw)
            evo = op("vector", lambda h, oc=oc, buf=buf: h.tensor_copy(rb[:, oc, :], psB[:, buf * 512:buf * 512 + BLK]),
                     waits={"cP": pe0})
        pe = ln_stats([rb[:, c, :] for c in range(NCH)], 1024, {"cV": evo})
        a3 = ln_chain(1024, fsc[:, 3, 0:BLK], {"cP": pe})
        x1st = {0: 0, 1: 0}
        x1cg = 0
        for c in range(NCH):
            sl = fsc[:, c % 2, 0:BLK]
            ld = dma("scalar", f"dXl{c % 2}", sl, xT[c * 128:(c + 1) * 128, t0:t0 + BLK],
                     waits={"cV": k.ctr["cV"]})
            s2_ = c % 2
            v1 = op("vector", lambda h, c=c, s2_=s2_: h.scalar_tensor_tensor(
                ga[:, s2_, :], psA[:, 1024:1024 + BLK], -RD, rb[:, c, :], ALU.mult, ALU.add),
                waits={"cA": a3, "cP": pe, "cG": x1cg, f"dXs{c % 2}": x1st[c % 2]})
            v2 = op("vector", lambda h, s2_=s2_: h.tensor_tensor(ga[:, s2_, :], ga[:, s2_, :], fsc[:, 3, 0:BLK], ALU.mult),
                    waits={"cV": v1})
            v3 = op("vector", lambda h, sl=sl, s2_=s2_: h.tensor_tensor(gb[:, s2_, :], ga[:, s2_, :], sl, ALU.add),
                    waits={"cV": v2, f"dXl{c % 2}": ld})
            x1st[c % 2] = dma("scalar", f"dXs{c % 2}", x1m[c, :, t0:t0 + BLK], gb[:, s2_, :], waits={"cV": v3})
            x1cg = op("gpsimd", lambda h, c=c, s2_=s2_: h.tensor_copy(t1bv[:, c, :], gb[:, s2_, :]), waits={"cV": v3})
        x1b = t1bv
        tap("x1", ovl[:, 8192:12288].rearrange("p (a b) -> p a b", a=NCH), np.s_[:, :, t0:t0 + BLK])
        pe = ln_stats([x1b[:, c, :] for c in range(NCH)], 1024, {"cG": x1cg})
        a3 = ln_chain(1024, fsc[:, 3, 0:BLK], {"cP": pe})
        xst = {0: 0, 1: 0}
        t2cg = 0
        for c in range(NCH):
            s2_ = c % 2
            v1 = op("vector", lambda h, c=c, s2_=s2_: h.scalar_tensor_tensor(
                ga[:, s2_, :], psA[:, 1024:1024 + BLK], -RD, x1b[:, c, :], ALU.mult, ALU.add),
                waits={"cA": a3, "cP": pe, f"dXs{c % 2}": xst[c % 2], "cG": t2cg})
            v2 = op("vector", lambda h, s2_=s2_: h.tensor_tensor(ga[:, s2_, :], ga[:, s2_, :], fsc[:, 3, 0:BLK], ALU.mult),
                    waits={"cV": v1})
            xst[c % 2] = dma("scalar", f"dXs{c % 2}", xmas[c, :, t0:t0 + BLK], ga[:, s2_, :], waits={"cV": v2})
            t2cg = xb_store(c, t0, ga[:, s2_, :], c % 2, v2)
        conv_step(12)
        k.ph.flush()

    lw = {"wqkv": None, "w2": None}

    def attn_phase(l):
        cw = convq["done"].get(l)
        assert cw is not None, f"conversions for layer {l} not done in time"
        if l > 0:
            for c in range(NCH):
                tap(f"ffn{l - 1}", xmas[c], idx=np.s_[c])
        lw["wqkv"] = dma("sync", "dW", warena.ap().rearrange("p (c n) -> p c n", c=NCH),
                         qkvc[l].transpose([1, 0, 2]),
                         waits={"dCs0": cw["dCs0"], "dCs1": cw["dCs1"], "cP": k.ctr["cP"]})
        ldm = {0: xrd_load(0)[1]}
        for b in range(NBLK):
            attn_block(l, b, ldm)

    def attn_block(l, b, ldm):
        t0 = b * BLK
        tld = 0
        for i in range(4):
            tld = dma("sync", "dT", tabs[:, i, :], ctabs[i][:, t0:t0 + BLK],
                      waits={"cV": k.ctr["cV"], "cG": k.ctr["cG"]} if i == 0 else None)
        evac = {}
        prep_marks = {}
        rn_a = {}
        rnuse = {0: 0, 1: 0}
        order = list(range(8, 16)) + list(range(16, 24)) + list(range(0, 8))
        for idx, oc in enumerate(order):
            kind, p = oc // NP, oc % NP
            buf = idx % 2
            pe0 = 0
            for c in range(NCH):
                w = None
                if c == 0:
                    pv_ = evac.get(idx - 2, ("cV", 0))
                    w = {"dW": lw["wqkv"], pv_[0]: pv_[1], f"dXr{b % 2}": ldm[b]}
                pe0 = mm(psB[:, buf * 512:buf * 512 + BLK],
                         warena[:, c * 3072 + oc * 128: c * 3072 + (oc + 1) * 128],
                         xsl(b % 2, c, True), start=(c == 0), stop=(c == NCH - 1), waits=w)
            dst = chs[:, buf, :]
            pmw = prep_marks.get(idx - 2, {})
            evw = {"cP": pe0, "cG": pmw.get("cG", 0), "cV": pmw.get("cV", 0), "cA": pmw.get("cA", 0)}
            if idx % 2 == 0:
                ev = ("cV", op("vector", lambda h, d=dst, buf=buf: h.tensor_copy(d, psB[:, buf * 512:buf * 512 + BLK]),
                               waits=evw))
            else:
                ev = ("cA", op("scalar", lambda h, d=dst, buf=buf: h.activation(d, psB[:, buf * 512:buf * 512 + BLK], AF.Copy),
                               waits=evw))
            evac[idx] = ev
            if kind == 2:
                tm = tprev = 0
                tmbank = {0: 0, 1: 0}
                for tc in range(NTC):
                    slot = (NTC * b + tc) % 6
                    _w = {"cV": tmbank[tc % 2]}
                    _w[ev[0]] = max(_w.get(ev[0], 0), ev[1])
                    t1m = tr(psC[:, (tc % 2) * 1024:(tc % 2) * 1024 + 128], dst[:, tc * 128:(tc + 1) * 128], identb,
                             waits=_w)
                    tm = op("vector", lambda h, slot=slot, p=p, tc=tc: h.tensor_copy(
                        vtok[:, slot, p * 128:(p + 1) * 128], psC[:, (tc % 2) * 1024:(tc % 2) * 1024 + 128]),
                        waits={"cP": t1m})
                    tmbank[tc % 2] = tm
                    tprev = t1m
                prep_marks[idx] = {"cV": tm, "cG": 0, "cA": 0, "cP": tprev}
            else:
                bank = 1024 + (p % 2) * 512
                sq = chs[:, 4 + (p % 2), :]
                _w = {"cA": rnuse[p % 2]}
                _w[ev[0]] = max(_w.get(ev[0], 0), ev[1])
                pl = op("gpsimd", lambda h, d=dst, sq=sq: h.tensor_tensor(sq, d, d, ALU.mult), waits=_w)
                pe1 = mm(psA[:, bank:bank + BLK], oneshb, sq, start=True, stop=False,
                         waits={"cG": pl, "cA": rnuse[p % 2]})
                pe1 = mm(psA[:, bank:bank + BLK], onesb_row, eps_row, start=False, stop=True)
                a1 = op("scalar", lambda h, bank=bank: h.activation(fsc[:, 2, 0:BLK], psA[:, bank:bank + BLK], AF.Ln),
                        waits={"cP": pe1, "cV": prep_state["last"]})
                rnuse[p % 2] = a1
                a2 = op("scalar", lambda h, p=p: h.activation(rns[:, p % 2, :], fsc[:, 2, 0:BLK], AF.Exp, scale=-0.5),
                        waits={"cA": a1, "cV": prep_state["last"]})
                rn_a[(kind, p)] = a2
                _w = {"cA": a2, "dT": tld, "dW": pld}
                _w[ev[0]] = max(_w.get(ev[0], 0), ev[1])
                if kind == 1:
                    dsts = [(kring[:, p, (NTC * b + tc) % 6, :], tc * 128, 128) for tc in range(NTC)]
                    pm = prep(dst, rns[:, p % 2, :], dsts, tabs[:, 2, :], tabs[:, 3, :],
                              swap_bank=(p % 2) * 512, l=l, isq=False, waits=_w)
                else:
                    pm = prep(dst, rns[:, p % 2, :], [(qp[:, p, :], 0, BLK)], tabs[:, 0, :], tabs[:, 1, :],
                              swap_bank=(p % 2) * 512, l=l, isq=True, waits=_w)
                pm["cA"] = a2
                prep_marks[idx] = pm
        qkv_done = k.ctr["cP"]
        xrd_mark(b % 2)
        if b + 1 < NBLK:
            ldm[b + 1] = xrd_load(b + 1)[1]
        conv_step(14)
        for _oc in range(3):
            wo_load(woc[l], _oc, _oc)
        prep_all = {"cV": k.ctr["cV"], "cG": k.ctr["cG"], "cA": k.ctr["cA"]}
        pvev = {}
        scm = {}
        pvpe = {}
        last_a1 = 0
        last_tpe = 0
        stq_state = {}

        def iters():
            for qt in range(NTC):
                g = NTC * b + qt
                kcs = [c for c in (g - 2, g - 1, g) if c >= 0]
                for p in range(NP):
                    yield qt, p, kcs

        def stageA(qt, p, kcs, pi):
            nonlocal last_a1
            nkc = len(kcs)
            nk = nkc * 128
            sb0 = (pi % 2) * 1024
            sb1 = (pi % 2) * 1024 + 512
            pslot = (pi % 2) * 2
            pe0 = 0
            for i, kc in enumerate(kcs):
                slot = kc % 6
                w = {**prep_all, "cA": max(prep_all["cA"], scm.get(pi - 1, 0))} if i == 0 else None
                mm(psA[:, sb0 + i * 128: sb0 + (i + 1) * 128], qp[0:64, p, qt * 128:(qt + 1) * 128],
                   kring[0:64, p, slot, :], start=True, stop=True, waits=w)
                pe0 = mm(psA[:, sb1 + i * 128: sb1 + (i + 1) * 128], qp[64:128, p, qt * 128:(qt + 1) * 128],
                         kring[64:128, p, slot, :], start=True, stop=True)
            res = []
            for hh, sb in ((0, sb0), (1, sb1)):
                mrow = cst[:, CO["mask"] + 384 - nk: CO["mask"] + 384]
                v1 = op("vector", lambda h, sb=sb, nk=nk, mrow=mrow, hh=hh: h.tensor_tensor(
                    fsc[:, hh, 0:nk], psA[:, sb:sb + nk], mrow, ALU.add),
                    waits={"cP": pe0, "cA": last_a1})
                v2 = op("vector", lambda h, nk=nk, hh=hh: h.reduce_max(
                    fsc[:, 2, 4 * hh:4 * hh + 1], fsc[:, hh, 0:nk], mybir.AxisListType.X, negate=True),
                    waits={"cV": v1})
                a1 = op("scalar", lambda h, nk=nk, hh=hh, pslot=pslot: h.activation(
                    pbf[:, pslot + hh, 0:nk], fsc[:, hh, 0:nk], AF.Exp, bias=fsc[:, 2, 4 * hh:4 * hh + 1],
                    accum_out=fsc[:, 2, 4 * hh + 1:4 * hh + 2]),
                    waits={"cV": v2, "cP": last_tpe})
                last_a1 = a1
                v3 = op("vector", lambda h, hh=hh: h.reciprocal(fsc[:, 2, 4 * hh + 2:4 * hh + 3],
                                                                fsc[:, 2, 4 * hh + 1:4 * hh + 2]), waits={"cA": a1})
                v4 = op("vector", lambda h, nk=nk, hh=hh, pslot=pslot: h.tensor_scalar_mul(
                    pbf[:, pslot + hh, 0:nk], pbf[:, pslot + hh, 0:nk], fsc[:, 2, 4 * hh + 2:4 * hh + 3]),
                    waits={"cV": v3})
                scm[pi] = max(scm.get(pi, 0), a1)
                res.append(v4)
            return res

        def stageB(qt, p, kcs, pi, v4s):
            nonlocal last_tpe
            nkc = len(kcs)
            nk = nkc * 128
            pslot = (pi % 2) * 2
            for hh in range(2):
                tpe = 0
                for i in range(nkc):
                    tpe = tr(psC[:, (pi % 2) * 1024 + i * 128:(pi % 2) * 1024 + (i + 1) * 128],
                             pbf[:, pslot + hh, i * 128:(i + 1) * 128], identb, waits={"cV": v4s[hh]})
                last_tpe = tpe
                v5 = op("vector", lambda h, pi=pi, nk=nk: h.tensor_copy(
                    pbf[:, 4, 0:nk], psC[:, (pi % 2) * 1024:(pi % 2) * 1024 + nk]), waits={"cP": tpe})
                pvbank = 512 * ((pi // 4) % 2)
                col = (pi % 4) * 128
                for i, kc in enumerate(kcs):
                    slot = kc % 6
                    w = {"cV": max(v5, pvev.get((pi // 4) - 2, 0))}
                    pvpe[pi] = mm(psB[hh * 64:(hh + 1) * 64, pvbank + col: pvbank + col + 128],
                                  vtok[:, slot, p * 128 + hh * 64: p * 128 + hh * 64 + 64],
                                  pbf[:, 4, i * 128:(i + 1) * 128],
                                  start=(i == 0), stop=(i == nkc - 1), waits=w)
            if pi % 4 == 3:
                grp = pi // 4
                pvev[grp] = op("vector", lambda h, qt=qt, p=p, pvbank=512 * (grp % 2): h.tensor_copy(
                    aout[:, p - 3:p + 1, qt * 128:(qt + 1) * 128], psB[:, pvbank:pvbank + 512]),
                    waits={"cP": pvpe[pi]})

        prev = None
        for pi, (qt, p, kcs) in enumerate(iters()):
            v4s = stageA(qt, p, kcs, pi)
            if prev is not None:
                stageB(*prev)
            prev = (qt, p, kcs, pi, v4s)
        stageB(*prev)
        att_done = {"cV": k.ctr["cV"]}
        stm = {}
        stq = {0: k.ctr["dXs0"], 1: k.ctr["dXs1"]}
        for oc in range(NCH):
            slot = oc % 3
            buf = oc % 2
            ld = dma("scalar", f"dXl{buf}", xfio[:, buf, :], xmas[oc, :, t0:t0 + BLK],
                     waits={"cV": stm.get(oc - 2, 0), "dXs0": k.ctr["dXs0"], "dXs1": k.ctr["dXs1"]})
            pe0 = 0
            for c in range(NCH):
                w = None
                if c == 0:
                    w = {wo_ring["m"][oc][0]: wo_ring["m"][oc][1], "cV": max(att_done["cV"], stm.get(oc - 2, 0))}
                pe0 = mm(psB[:, buf * 512:buf * 512 + BLK], wor[:, slot, c, :], aout[:, c, :],
                         start=(c == 0), stop=(c == NCH - 1), waits=w)
            wo_ring["use"][slot] = pe0
            if oc + 3 < NCH:
                wo_load(woc[l], oc + 3, (oc + 3) % 3)
            v1 = op("vector", lambda h, buf=buf: h.tensor_tensor(
                xfio[:, 2 + buf, :], xfio[:, buf, :], psB[:, buf * 512:buf * 512 + BLK], ALU.add),
                waits={"cP": pe0, f"dXl{buf}": ld, f"dXs{buf}": stq[buf]})
            stm[oc] = v1
            stq[buf] = dma("scalar", f"dXs{buf}", xmas[oc, :, t0:t0 + BLK], xfio[:, 2 + buf, :], waits={"cV": v1})
            xb_store(oc, t0, xfio[:, 2 + buf, :], buf, v1)
        k.ph.flush()

    def ffn_phase(l):
        cw = convq["done"][l]
        for c in range(NCH):
            tap(f"attn{l}", xmas[c], idx=np.s_[c])
        lw["w2"] = dma("sync", "dW", warena[:, 0:NJ * 1024].rearrange("p (j n) -> p j n", j=NJ),
                       w2c[l].transpose([1, 0, 2]),
                       waits={"dCs0": cw["dCs0"], "dCs1": cw["dCs1"], "cP": k.ctr["cP"]})
        ldm = {0: xrd_load(0)[1]}
        for b in range(NBLK):
            ffn_block(l, b, ldm)

    def ffn_block(l, b, ldm):
        t0 = b * BLK
        w1m = {}
        ring_free = {}

        def w1_load(j):
            slot = j % 4
            m = 0
            for ag in range(2):
                m = dma("sync", f"dW1{slot % 2}", w1r[:, slot, ag, :, :], w1c[l, ag, j].transpose([1, 0, 2]),
                        waits={"cP": ring_free.get(slot, 0)})
            w1m[j] = (f"dW1{slot % 2}", m)
        for jj in range(4):
            w1_load(jj)
        srcs = [xsl(b % 2, c, True) for c in range(NCH)]
        pe = ln_stats(srcs, 1024, {"cV": k.ctr["cV"], "cG": k.ctr["cG"], "cP": k.ctr["cP"],
                                   f"dXr{b % 2}": ldm[b]})
        rstd = fsc[:, 3, 0:BLK]
        a3 = ln_chain(1024, rstd, {"cP": pe})
        xn_done = 0
        for c in range(NCH):
            v1 = op("vector", lambda h, c=c, s=srcs[c]: h.scalar_tensor_tensor(
                ga[:, c % 2, :], psA[:, 1024:1024 + BLK], -RD, s, ALU.mult, ALU.add),
                waits={"cA": a3, "cP": pe, "cV": xn_done})
            v2 = op("vector", lambda h, c=c: h.tensor_tensor(ga[:, c % 2, :], ga[:, c % 2, :], rstd, ALU.mult),
                    waits={"cV": v1})
            xn_done = op("vector", lambda h, c=c: h.tensor_scalar(
                xnb[:, c, :], ga[:, c % 2, :], parc(f"lng{l}", c), parc(f"lnb{l}", c), ALU.mult, ALU.add),
                waits={"cV": v2, **w0})
        xrd_mark(b % 2)
        if b + 1 < NBLK:
            ldm[b + 1] = xrd_load(b + 1)[1]
        conv_step(7)
        um = {}
        for j in range(NJ):
            jn = 128 if j < NJ - 1 else JL
            slot = j % 4
            ab = (j % 2) * 1024
            gbk = (j % 2) * 1024 + 512
            pe0 = 0
            for c in range(NCH):
                w = None
                if c == 0:
                    w = {w1m[j][0]: w1m[j][1], "cV": xn_done, "cG": um.get(j - 2, 0)}
                mm(psA[0:jn, ab:ab + BLK], w1r[:, slot, 0, c, 0:jn], xnb[:, c, :],
                   start=(c == 0), stop=(c == NCH - 1), waits=w)
                pe0 = mm(psA[0:jn, gbk:gbk + BLK], w1r[:, slot, 1, c, 0:jn], xnb[:, c, :],
                         start=(c == 0), stop=(c == NCH - 1))
            ring_free[slot] = pe0
            if j + 4 < NJ:
                w1_load(j + 4)
            ga1 = op("scalar", lambda h, j=j, jn=jn, gbk=gbk: h.activation(
                chs[0:jn, 0, :], psA[0:jn, gbk:gbk + BLK], AF.Gelu, bias=parc(f"b1g{l}", j, jn)),
                waits={"cP": pe0, "cG": um.get(j - 1, 0)})
            v1 = op("vector", lambda h, j=j, jn=jn, ab=ab: h.tensor_scalar(
                chs[0:jn, 1, :], psA[0:jn, ab:ab + BLK], parc(f"b1a{l}", j, jn), None, ALU.add),
                waits={"cP": pe0, "cG": um.get(j - 1, 0)})
            um[j] = op("gpsimd", lambda h, j=j, jn=jn: h.tensor_tensor(
                usb[0:jn, j, :], chs[0:jn, 1, :], chs[0:jn, 0, :], ALU.mult),
                waits={"cA": ga1, "cV": v1})
        conv_step(7)
        u_all = {"cG": k.ctr["cG"]}
        stm = {}
        stq = {0: k.ctr["dXs0"], 1: k.ctr["dXs1"]}
        for oc in range(NCH):
            buf = oc % 2
            ld = dma("scalar", f"dXl{buf}", xfio[:, buf, :], xmas[oc, :, t0:t0 + BLK],
                     waits={"cV": stm.get(oc - 2, 0), "dXs0": k.ctr["dXs0"], "dXs1": k.ctr["dXs1"]})
            pe0 = 0
            for j in range(NJ):
                jn = 128 if j < NJ - 1 else JL
                w = None
                if j == 0:
                    w = {**u_all, "dW": lw["w2"], "cV": stm.get(oc - 2, 0)}
                pe0 = mm(psB[:, buf * 512:buf * 512 + BLK],
                         warena[0:jn, j * 1024 + oc * 128: j * 1024 + (oc + 1) * 128],
                         usb[0:jn, j, :], start=(j == 0), stop=(j == NJ - 1), waits=w)
            v1 = op("vector", lambda h, oc=oc, buf=buf: h.scalar_tensor_tensor(
                xfio[:, 2 + buf, :], psB[:, buf * 512:buf * 512 + BLK], parc(f"b2{l}", oc), xfio[:, buf, :],
                ALU.add, ALU.add), waits={"cP": pe0, f"dXl{buf}": ld, f"dXs{buf}": stq[buf]})
            stm[oc] = v1
            stq[buf] = dma("scalar", f"dXs{buf}", xmas[oc, :, t0:t0 + BLK], xfio[:, 2 + buf, :], waits={"cV": v1})
            xb_store(oc, t0, xfio[:, 2 + buf, :], buf, v1)
        k.ph.flush()

    def epi_block(b, ldm):
        t0 = b * BLK
        srcs = [xsl(b % 2, c, False) for c in range(NCH)]
        pe = ln_stats(srcs, 1024, {"cV": k.ctr["cV"], "cG": k.ctr["cG"], "cP": k.ctr["cP"],
                                   f"dXr{b % 2}": ldm[b]})
        rstd = fsc[:, 3, 0:BLK]
        a3 = ln_chain(1024, rstd, {"cP": pe})
        stm = {}
        stq = {0: k.ctr["dOut"], 1: k.ctr["dOut"]}
        for c in range(NCH):
            buf = c % 2
            ld = dma("scalar", f"dXl{buf}", xfio[:, buf, :], x1m[c, :, t0:t0 + BLK],
                     waits={"cV": stm.get(c - 2, 0), "dXs0": k.ctr["dXs0"], "dXs1": k.ctr["dXs1"]})
            v1 = op("vector", lambda h, c=c, s=srcs[c]: h.scalar_tensor_tensor(
                ga[:, c % 2, :], psA[:, 1024:1024 + BLK], -RD, s, ALU.mult, ALU.add),
                waits={"cA": a3, "cP": pe, "cV": stm.get(c - 2, 0)})
            v2 = op("vector", lambda h, c=c: h.tensor_tensor(ga[:, c % 2, :], ga[:, c % 2, :], rstd, ALU.mult),
                    waits={"cV": v1})
            v3 = op("vector", lambda h, c=c, buf=buf: h.tensor_tensor(xfio[:, 2 + buf, :], ga[:, c % 2, :],
                                                                      xfio[:, buf, :], ALU.add),
                    waits={"cV": v2, f"dXl{buf}": ld, "dOut": stq[buf]})
            stm[c] = v3
            stq[buf] = dma("scalar", "dOut", outT[c, :, t0:t0 + BLK], xfio[:, 2 + buf, :], waits={"cV": v3})
        xrd_mark(b % 2)
        if b + 1 < NBLK:
            ldm[b + 1] = xrd_load(b + 1)[1]
        k.ph.flush()

    k.ph.flush()
    for b in range(NBLK):
        gla_block(b)
    for l in range(NL):
        if l + 1 < NL:
            conv_push(l + 1)
        attn_phase(l)
        ffn_phase(l)
    ldm_e = {0: xrd_load(0)[1]}
    for b in range(NBLK):
        epi_block(b, ldm_e)
    for q in ["dOut", "dTap", "dXs0", "dXs1", "dXb0", "dXb1"]:
        k.emit("sync", lambda h, q=q: h.wait_ge(k.sems[q], k.ctr[q]))
    k.ph.flush()
    return nc


def _in_maps(inputs):
    c = host_consts()
    par = host_par(inputs)
    f32c = lambda a: np.ascontiguousarray(np.asarray(a, np.float32))
    shared = dict(
        wqT=f32c(np.asarray(inputs["L0_attn_Wq"]).T), wkT=f32c(np.asarray(inputs["L0_attn_Wk"]).T),
        wvT=f32c(np.asarray(inputs["L0_attn_Wv"]).T), woTg=f32c(np.asarray(inputs["L0_attn_Wo"]).T),
        wqkv_in=f32c(inputs["L0_ffn_Wqkv"]), wo_in=f32c(inputs["L0_ffn_Wo"]),
        w1_in=f32c(inputs["L0_ffn_W1"]), w2_in=f32c(inputs["L0_ffn_W2"]),
        par_in=par,
        c_ident=c["ident"], c_swapT=c["swapT"], c_onesf=c["onesf"], c_onesh=c["onesh"],
        c_mask=c["masklog"], c_dmask=c["dmask"],
        c_ctq=c["ctq"], c_stq=c["stq"], c_ctk=c["ctk"], c_stk=c["stk"],
    )
    x = np.asarray(inputs["L0_attn_x"], np.float32)
    maps = []
    for core in range(8):
        m = dict(shared)
        m["xT"] = np.ascontiguousarray(x[core].T)
        maps.append(m)
    return maps


def kernel(**inputs):
    if "nc" not in _bcache:
        _bcache["nc"] = build()
    nc = _bcache["nc"]
    maps = _in_maps(inputs)
    res = run_bass_kernel_spmd(nc, maps, core_ids=list(range(8)))
    out = np.empty((8, T, D), np.float32)
    for core in range(8):
        out[core] = res.results[core]["outT"].reshape(D, T).T
    return out


# version 10
# speedup vs baseline: 1.0228x; 1.0100x over previous
import numpy as np

import concourse.bass as bass
import concourse.mybir as mybir
from concourse.bass_utils import run_bass_kernel_spmd

F32 = mybir.dt.float32
BF16 = mybir.dt.bfloat16
AF = mybir.ActivationFunctionType
ALU = mybir.AluOpType

D = 1024
T = 4096
NCH = 8
H = 16
DK = 64
NP = 8
NL = 6
DI = 2730
NJ = 22
JL = 42
BLK = 512
NBLK = T // BLK
NTC = BLK // 128
NEG = -1e30
EPS_LN = 1e-5
EPS_L2 = 1e-19
EPS_GLA = 1e-19
RD = 1.0 / D

ENGINES = ("sync", "tensor", "vector", "scalar", "gpsimd")
EX = {"tensor": "cP", "vector": "cV", "scalar": "cA", "gpsimd": "cG"}
_bcache = {}

_PC = {}
_cidx = 0
def _pc(name, n=1):
    global _cidx
    _PC[name] = _cidx
    _cidx += n
for _p in range(NP):
    _pc(f"wg{_p}"); _pc(f"bg{_p}"); _pc(f"wl{_p}"); _pc(f"bl{_p}")
for _l in range(NL):
    _pc(f"qs{_l}"); _pc(f"qssw{_l}"); _pc(f"ks{_l}"); _pc(f"kssw{_l}")
    _pc(f"lng{_l}", NCH); _pc(f"lnb{_l}", NCH); _pc(f"b2{_l}", NCH)
    _pc(f"b1a{_l}", NJ); _pc(f"b1g{_l}", NJ)
PARW = _cidx


def host_consts():
    if "c" in _bcache:
        return _bcache["c"]
    ident = np.eye(128, dtype=np.float32)
    S = np.zeros((128, 128), np.float32)
    for b0 in (0, 64):
        for d in range(64):
            S[b0 + d, b0 + (d + 32) % 64] = 1.0
    swapT = np.ascontiguousarray(S.T)
    onesf = np.ones((128, 128), np.float32)
    onesh = np.zeros((128, 128), np.float32)
    onesh[:64, :64] = 1.0
    onesh[64:, 64:] = 1.0
    ii = np.arange(128)[:, None]
    jj = np.arange(128)[None, :]
    masklog = np.zeros((128, 384), np.float32)
    masklog[:, :128] = np.where(ii <= jj, 0.0, NEG)
    masklog[:, 256:] = np.where(ii >= jj, 0.0, NEG)
    dmask = np.where(ii <= jj, 0.0, NEG).astype(np.float32)
    half = np.arange(0, DK, 2, dtype=np.float32)
    inv_freq = 1.0 / 10000.0 ** (half / DK)
    t = np.arange(T, dtype=np.float32)
    freqs = np.concatenate([t[:, None] * inv_freq[None, :]] * 2, -1)
    base_scale = (half + 0.4 * DK) / (1.4 * DK)
    power = (t - T // 2) / 128.0
    scale = np.concatenate([base_scale[None, :] ** power[:, None]] * 2, -1)
    cos, sin = np.cos(freqs), np.sin(freqs)
    sign = np.concatenate([-np.ones(32), np.ones(32)]).astype(np.float32)
    s8 = np.sqrt(8.0)
    dup = lambda a: np.ascontiguousarray(np.concatenate([a, a], 0), dtype=np.float32)
    c = dict(
        ident=ident, swapT=swapT, onesf=onesf, onesh=onesh, masklog=masklog, dmask=dmask,
        ctq=dup((cos * scale * s8).T), stq=dup((sin * scale * sign[None, :] * s8).T),
        ctk=dup((cos / scale * s8).T), stk=dup((sin / scale * sign[None, :] * s8).T),
    )
    _bcache["c"] = c
    return c


def host_par(ins):
    par = np.zeros((128, PARW), np.float32)
    dup2 = lambda v: np.concatenate([v, v])
    for p in range(NP):
        for nm, key in [("wg", "L0_attn_W_gamma"), ("bg", "L0_attn_b_gamma"),
                        ("wl", "L0_attn_W_lambda"), ("bl", "L0_attn_b_lambda")]:
            arr = np.asarray(ins[key], np.float32)
            col = np.empty(128, np.float32)
            col[:64] = arr[2 * p]
            col[64:] = arr[2 * p + 1]
            par[:, _PC[f"{nm}{p}"]] = col
    for l in range(NL):
        qs = np.asarray(ins["L0_ffn_q_scale"][l], np.float32)
        ks = np.asarray(ins["L0_ffn_k_scale"][l], np.float32)
        sw = lambda v: np.concatenate([v[32:], v[:32]])
        par[:, _PC[f"qs{l}"]] = dup2(qs)
        par[:, _PC[f"qssw{l}"]] = dup2(sw(qs))
        par[:, _PC[f"ks{l}"]] = dup2(ks)
        par[:, _PC[f"kssw{l}"]] = dup2(sw(ks))
        par[:, _PC[f"lng{l}"]:_PC[f"lng{l}"] + NCH] = np.asarray(ins["L0_ffn_ln_g"][l], np.float32).reshape(NCH, 128).T
        par[:, _PC[f"lnb{l}"]:_PC[f"lnb{l}"] + NCH] = np.asarray(ins["L0_ffn_ln_b"][l], np.float32).reshape(NCH, 128).T
        par[:, _PC[f"b2{l}"]:_PC[f"b2{l}"] + NCH] = np.asarray(ins["L0_ffn_b2"][l], np.float32).reshape(NCH, 128).T
        b1 = np.asarray(ins["L0_ffn_b1"][l], np.float32)
        b1p = np.zeros((2, NJ * 128), np.float32)
        b1p[0, :DI] = b1[:DI]
        b1p[1, :DI] = b1[DI:]
        par[:, _PC[f"b1a{l}"]:_PC[f"b1a{l}"] + NJ] = b1p[0].reshape(NJ, 128).T
        par[:, _PC[f"b1g{l}"]:_PC[f"b1g{l}"] + NJ] = b1p[1].reshape(NJ, 128).T
    return par


class Ph:
    def __init__(self, k):
        self.k = k
        self.fns = {e: [] for e in ENGINES}

    def flush(self):
        if not any(self.fns.values()):
            return
        with self.k.nc.Block() as blk:
            for e in ENGINES:
                if self.fns[e]:
                    def body(h, fs=tuple(self.fns[e])):
                        for f in fs:
                            f(h)
                    getattr(blk, e)(body)
        self.fns = {e: [] for e in ENGINES}


class Kern:
    def __init__(self, nc):
        self.nc = nc
        self.ctr = {}
        self.sems = {}
        self.inc = {}
        self.lastw = {}
        self.ph = Ph(self)

    def mksem(self, name, inc):
        h = self.nc.semaphore(name).__enter__()
        self.sems[name] = h
        self.ctr[name] = 0
        self.inc[name] = inc

    def emit(self, eng, fn, tick=None, waits=None):
        wl = []
        if waits:
            for q, v in waits.items():
                if not v or v <= 0:
                    continue
                assert v <= self.ctr[q], f"wait on future count {q}={v}>{self.ctr[q]}"
                key = (eng, q)
                if self.lastw.get(key, -1) >= v:
                    continue
                self.lastw[key] = v
                wl.append((self.sems[q], v))
        tk = None
        if tick is not None:
            self.ctr[tick] += self.inc[tick]
            tk = (self.sems[tick], self.inc[tick])

        def closure(h, fn=fn, wl=tuple(wl), tk=tk):
            for s, v in wl:
                h.wait_ge(s, v)
            inst = fn(h)
            if tk is not None:
                inst.then_inc(tk[0], tk[1])
        self.ph.fns[eng].append(closure)
        return self.ctr[tick] if tick else None


def build(taps=()):
    nc = bass.Bass("TRN2", target_bir_lowering=False)
    k = Kern(nc)
    dsems = ["dW", "dT", "dWo0", "dWo1", "dWo2", "dW10", "dW11", "dCv0", "dCv1",
             "dCs0", "dCs1", "dXl0", "dXl1", "dXs0", "dXs1", "dOut", "dTap",
             "dXr0", "dXr1", "dXb0", "dXb1"]
    for q in dsems:
        k.mksem(q, 16)
    for q in ["cP", "cV", "cA", "cG"]:
        k.mksem(q, 1)

    def din(name, shape, dtype=F32):
        return nc.dram_tensor(name, list(shape), dtype, kind="ExternalInput").ap()

    xT = din("xT", (D, T))
    wqT = din("wqT", (D, D)); wkT = din("wkT", (D, D)); wvT = din("wvT", (D, D)); woTg = din("woTg", (D, D))
    wqkv_in = din("wqkv_in", (NL, D, 3 * D))
    wo_in = din("wo_in", (NL, D, D))
    w1_in = din("w1_in", (NL, D, 2 * DI))
    w2_in = din("w2_in", (NL, DI, D))
    par_in = din("par_in", (128, PARW))
    c_ident = din("c_ident", (128, 128)); c_swapT = din("c_swapT", (128, 128))
    c_onesf = din("c_onesf", (128, 128)); c_onesh = din("c_onesh", (128, 128))
    c_mask = din("c_mask", (128, 384)); c_dmask = din("c_dmask", (128, 128))
    ctabs = [din("c_ctq", (128, T)), din("c_stq", (128, T)), din("c_ctk", (128, T)), din("c_stk", (128, T))]

    outT = nc.dram_tensor("outT", [NCH, 128, T], F32, kind="ExternalOutput").ap()
    tap_t = {}
    for name, shape, *dt in taps:
        tdt = BF16 if (dt and dt[0] == "bf16") else F32
        tap_t[name] = nc.dram_tensor("tap_" + name, list(shape), tdt, kind="ExternalOutput").ap()

    xmas = nc.dram_tensor("xmas", [NCH, 128, T], F32, kind="Internal").ap()
    xbfh = nc.dram_tensor("xbfh", [NCH, 128, T + 1], BF16, kind="Internal").ap()
    x1m = nc.dram_tensor("x1m", [NCH, 128, T], F32, kind="Internal").ap()
    qkvc = nc.dram_tensor("qkvc", [NL, NCH, 128, 3 * D], BF16, kind="Internal").ap()
    woc = nc.dram_tensor("woc", [NL, NCH, 128, D], BF16, kind="Internal").ap()
    wotc = nc.dram_tensor("wotc", [NCH, 128, D], BF16, kind="Internal").ap()
    w1c = nc.dram_tensor("w1c", [NL, 2, NJ, NCH, 128, 128], BF16, kind="Internal").ap()
    w2c = nc.dram_tensor("w2c", [NL, NJ, 128, D], BF16, kind="Internal").ap()

    def S(name, shape, dtype):
        return nc.alloc_sbuf_tensor(name, list(shape), dtype)

    warena = S("warena", (128, 24576), BF16)
    wor = S("wor", (128, 3, NCH, 128), BF16)
    w1r = S("w1r", (128, 4, 2, NCH, 128), BF16)
    tabs = S("tabs", (128, 4, BLK), F32)
    xfio = S("xfio", (128, 4, BLK), F32)
    cvf = S("cvf", (128, 2, 512), F32)
    cvb = S("cvb", (128, 2, 512), BF16)
    qp = S("qp", (128, NP, BLK), BF16)
    ovl = S("ovl", (128, 12288), BF16)
    aout = S("aout", (128, NCH, BLK), BF16)
    chs = S("chs", (128, 6, BLK), BF16)
    rns = S("rns", (128, 2, BLK), BF16)
    fsc = S("fsc", (128, 4, BLK), F32)
    pbf = S("pbf", (128, 5, 384), BF16)
    ga = S("ga", (128, 2, BLK), F32)
    gb = S("gb", (128, 2, BLK), F32)
    gc = S("gc", (128, 2, BLK), F32)
    xrd = S("xrd", (128, 2, NCH, BLK + 1), BF16)
    sstg = S("sstg", (128, 2, BLK), BF16)
    xnb = S("xnb", (128, NCH, BLK), BF16)
    ssb = S("ssb", (128, NP, 64), F32)
    sbfA = S("sbfA", (128, NP, 64), BF16)
    sbfB = S("sbfB", (128, NP, 64), BF16)
    cst = S("cst", (128, 1056), F32)
    cbt = S("cbt", (128, 1024), BF16)
    par = S("par", (128, PARW), F32)

    rem = nc.sbuf_bytes_remaining
    assert rem >= 0, f"SBUF over budget by {-rem} bytes"

    kring = ovl.ap()[:, 0:6144].rearrange("p (a b c) -> p a b c", a=NP, b=6)
    vtok = ovl.ap()[:, 6144:12288].rearrange("p (a b) -> p a b", a=6)
    usb = ovl.ap()[:, 0:NJ * BLK].rearrange("p (a b) -> p a b", a=NJ)
    kview = ovl.ap()[:, 0:4096].rearrange("p (a b) -> p a b", a=NP)
    vview = ovl.ap()[:, 4096:8192]
    t1bv = ovl.ap()[:, 8192:12288].rearrange("p (a b) -> p a b", a=NCH)

    psA = nc.alloc_psum_tensor("psA", [128, 2048], F32)
    psB = nc.alloc_psum_tensor("psB", [128, 1024], F32)
    psC = nc.alloc_psum_tensor("psC", [128, 2048], BF16)

    CO = dict(ident=0, swapT=128, onesf=256, onesh=384, mask=512, dmask=896)
    CB = dict(identb=0, swapTb=128, onesfb=256, oneshb=384, epsrow=512)
    GW = {"wq": 0, "wk": 8192, "wv": 16384}

    def dma(queue, sem, out, in_, waits=None):
        return k.emit(queue, lambda h, o=out, i=in_: h.dma_start(out=o, in_=i), tick=sem, waits=waits)

    def op(eng, fn, waits=None):
        return k.emit(eng, fn, tick=EX[eng], waits=waits)

    def mm(out_ap, lhsT_ap, rhs_ap, start, stop, waits=None):
        return op("tensor", lambda h, o=out_ap, l=lhsT_ap, r=rhs_ap, s1=start, s2=stop:
                  h.matmul(o, l, r, start=s1, stop=s2), waits=waits)

    def tr(out_ap, in_ap, identity, waits=None):
        return op("tensor", lambda h, o=out_ap, i=in_ap, d=identity: h.transpose(o, i, d), waits=waits)

    def allnow():
        return dict(k.ctr)

    def parc(name, off=0, rows=128):
        c0 = _PC[name] + off
        return par[0:rows, c0:c0 + 1]

    def xsl(slot, c, shifted):
        off = (0 if (shifted and c >= 4) else 1)
        return xrd[:, slot, c, off:off + BLK]

    xr_gate = {0: {}, 1: {}}

    def xrd_load(b):
        slot = b % 2
        ld = dma("scalar", f"dXr{slot}", xrd[:, slot, :, :],
                 xbfh[:, :, b * BLK: b * BLK + BLK + 1].transpose([1, 0, 2]),
                 waits=xr_gate[slot])
        return slot, ld

    def xrd_mark(slot):
        xr_gate[slot] = {"cP": k.ctr["cP"], "cV": k.ctr["cV"], "cG": k.ctr["cG"]}

    def tap(name, src_ap, idx=None, waits=None):
        if name in tap_t:
            dst = tap_t[name] if idx is None else tap_t[name][idx]
            dma("scalar", "dTap", dst, src_ap, waits=waits if waits is not None else allnow())

    onesb_row = cbt[0:1, CB["onesfb"]:CB["onesfb"] + 128]
    onesf_row = cst[0:1, CO["onesf"]:CO["onesf"] + 128]
    eps_row = cbt[0:1, CB["epsrow"]:CB["epsrow"] + BLK]
    identb = cbt[:, CB["identb"]:CB["identb"] + 128]
    swapTb = cbt[:, CB["swapTb"]:CB["swapTb"] + 128]
    onesfb = cbt[:, CB["onesfb"]:CB["onesfb"] + 128]
    oneshb = cbt[:, CB["oneshb"]:CB["oneshb"] + 128]

    dma("sync", "dW", cst[:, CO["ident"]:CO["ident"] + 128], c_ident)
    dma("sync", "dW", cst[:, CO["swapT"]:CO["swapT"] + 128], c_swapT)
    dma("sync", "dW", cst[:, CO["onesf"]:CO["onesf"] + 128], c_onesf)
    dma("sync", "dW", cst[:, CO["onesh"]:CO["onesh"] + 128], c_onesh)
    dma("sync", "dW", cst[:, CO["mask"]:CO["mask"] + 384], c_mask)
    dma("sync", "dW", cst[:, CO["dmask"]:CO["dmask"] + 128], c_dmask)
    pld = dma("sync", "dW", par[:, :], par_in)
    w0 = {"dW": pld}
    for nm, dst in [("ident", "identb"), ("swapT", "swapTb"), ("onesf", "onesfb"), ("onesh", "oneshb")]:
        op("gpsimd", lambda h, s=CO[nm], d=CB[dst]: h.tensor_copy(cbt[:, d:d + 128], cst[:, s:s + 128]), waits=w0)
    op("gpsimd", lambda h: h.memset(cbt[0:1, CB["epsrow"]:CB["epsrow"] + BLK], EPS_L2))
    op("gpsimd", lambda h: h.memset(cst[:, 1024:1025], EPS_LN))
    op("gpsimd", lambda h: h.memset(ssb[:, :, :], 0.0))
    op("gpsimd", lambda h: h.memset(sbfA[:, :, :], 0.0))
    op("gpsimd", lambda h: h.memset(sbfB[:, :, :], 0.0))
    zc = op("gpsimd", lambda h: h.memset(sstg[:, 0, 0:16], 0.0))
    k.emit("scalar", lambda h: h.dma_start(out=xbfh[:, :, 0:1].transpose([1, 0, 2]),
                                           in_=sstg[:, 0:1, 0:NCH].transpose([0, 2, 1]),
                                           allow_slow_non_contiguous=True),
           tick="dXb0", waits={"cG": zc})

    cvst = {"i": 0, 0: {"rd": ("cG", 0), "st": 0}, 1: {"rd": ("cG", 0), "st": 0}}

    def convert_piece(src_ap, dst_sb=None, dst_dram=None, rows=128):
        i = cvst["i"] % 2
        cvst["i"] += 1
        g = cvst[i]
        n = src_ap.shape[-1]
        assert n <= 512
        ld = dma("sync", f"dCv{i}", cvf[0:rows, i, 0:n], src_ap,
                 waits={g["rd"][0]: g["rd"][1], f"dCs{i}": g["st"]})
        if dst_sb is not None:
            m = op("gpsimd", lambda h, r=rows, i=i, n=n, d=dst_sb: h.tensor_copy(d, cvf[0:r, i, 0:n]),
                   waits={f"dCv{i}": ld})
            g["rd"] = ("cG", m)
            g["st"] = 0
        else:
            m = op("gpsimd", lambda h, r=rows, i=i, n=n: h.tensor_copy(cvb[0:r, i, 0:n], cvf[0:r, i, 0:n]),
                   waits={f"dCv{i}": ld, f"dCs{i}": g["st"]})
            g["rd"] = ("cG", m)
            g["st"] = dma("scalar", f"dCs{i}", dst_dram, cvb[0:rows, i, 0:n], waits={"cG": m})

    for nm, src in [("wq", wqT), ("wk", wkT), ("wv", wvT)]:
        for c in range(NCH):
            for piece in range(2):
                convert_piece(src[c * 128:(c + 1) * 128, piece * 512:(piece + 1) * 512],
                              dst_sb=warena[:, GW[nm] + c * 1024 + piece * 512: GW[nm] + c * 1024 + (piece + 1) * 512])
    for c in range(NCH):
        for piece in range(2):
            convert_piece(woTg[c * 128:(c + 1) * 128, piece * 512:(piece + 1) * 512],
                          dst_dram=wotc[c, :, piece * 512:(piece + 1) * 512])
    glaw = allnow()

    def conv_items(l):
        items = []
        for c in range(NCH):
            for p in range(6):
                items.append((wqkv_in[l, c * 128:(c + 1) * 128, p * 512:(p + 1) * 512],
                              qkvc[l, c, :, p * 512:(p + 1) * 512], 128))
        for c in range(NCH):
            for p in range(2):
                items.append((wo_in[l, c * 128:(c + 1) * 128, p * 512:(p + 1) * 512],
                              woc[l, c, :, p * 512:(p + 1) * 512], 128))
        for c in range(NCH):
            for ag in range(2):
                base = ag * DI
                for j0, j1 in [(0, 4), (4, 8), (8, 12), (12, 16), (16, 20), (20, 21)]:
                    items.append((w1_in[l, c * 128:(c + 1) * 128, base + j0 * 128: base + j1 * 128],
                                  w1c[l, ag, j0:j1, c, :, :].transpose([1, 0, 2]), 128))
                items.append((w1_in[l, c * 128:(c + 1) * 128, base + 21 * 128: base + DI],
                              w1c[l, ag, 21, c, :, 0:JL], 128))
        for j in range(NJ):
            rows = 128 if j < NJ - 1 else JL
            for p in range(2):
                items.append((w2_in[l, j * 128:j * 128 + rows, p * 512:(p + 1) * 512],
                              w2c[l, j, 0:rows, p * 512:(p + 1) * 512], rows))
        return items

    convq = {"items": [], "done": {}}

    def conv_push(l):
        convq["items"].extend((l, it) for it in conv_items(l))

    def conv_step(nmax):
        for _ in range(nmax):
            if not convq["items"]:
                return
            l, (src, dst, rows) = convq["items"].pop(0)
            convert_piece(src, dst_dram=dst, rows=rows)
            if not convq["items"] or convq["items"][0][0] != l:
                convq["done"][l] = allnow()

    conv_push(0)

    def ln_stats(srcs, bank, extra_waits=None):
        pe = 0
        ws = dict(extra_waits or {})
        for c, src in enumerate(srcs):
            sq = chs[:, 4 + (c % 2), :]
            pl = op("gpsimd", lambda h, s=src, d=sq: h.tensor_tensor(d, s, s, ALU.mult),
                    waits={**ws, "cP": pe})
            mm(psA[:, bank:bank + BLK], onesfb, src, start=(c == 0), stop=(c == NCH - 1), waits=ws if c == 0 else None)
            pe = mm(psA[:, bank + 512:bank + 512 + BLK], onesfb, sq, start=(c == 0), stop=(c == NCH - 1),
                    waits={"cG": pl})
        return pe

    fsc2_rd = [0]

    def ln_chain(bank, rstd_dst, waits):
        mu2 = fsc[:, 2, 0:BLK]
        a1 = op("scalar", lambda h: h.activation(mu2, psA[:, bank:bank + BLK], AF.Square, scale=RD),
                waits={**waits, "cV": max(waits.get("cV", 0), fsc2_rd[0])})
        v1 = op("vector", lambda h: h.scalar_tensor_tensor(rstd_dst, psA[:, bank + 512:bank + 512 + BLK], RD, mu2,
                                                           ALU.mult, ALU.subtract), waits={"cA": a1})
        fsc2_rd[0] = v1
        a2 = op("scalar", lambda h: h.activation(rstd_dst, rstd_dst, AF.Ln, bias=cst[:, 1024:1025]), waits={"cV": v1})
        return op("scalar", lambda h: h.activation(rstd_dst, rstd_dst, AF.Exp, scale=-0.5), waits={"cA": a2})

    xbst = {0: 0, 1: 0}

    def xb_store(c, t0, src_f32, buf, vmark):
        pl = op("gpsimd", lambda h, s=src_f32, buf=buf: h.tensor_copy(sstg[:, buf, :], s),
                waits={"cV": vmark, f"dXb{buf}": xbst[buf]})
        xbst[buf] = dma("scalar", f"dXb{buf}", xbfh[c, :, 1 + t0:1 + t0 + BLK], sstg[:, buf, :],
                        waits={"cG": pl})
        return pl

    prep_state = {"g1": 0, "last": 0}

    def prep(src_bf, rn_ap, dst_writes, cS, sS, swap_bank, l, isq, waits):
        waits = dict(waits)
        waits["cG"] = max(waits.get("cG", 0), prep_state["g1"])
        waits["cV"] = max(waits.get("cV", 0), prep_state["last"])
        sw = mm(psA[:, swap_bank:swap_bank + BLK], swapTb, src_bf, start=True, stop=True, waits=waits)
        t1 = chs[:, 2, :]
        t2 = chs[:, 3, :]
        sc = ("qs", "qssw") if isq else ("ks", "kssw")
        v1 = op("vector", lambda h, s=src_bf: h.scalar_tensor_tensor(t1, s, parc(f"{sc[0]}{l}"), cS, ALU.mult, ALU.mult),
                waits=waits)
        v2 = op("vector", lambda h, sb=swap_bank: h.scalar_tensor_tensor(
            t2, psA[:, sb:sb + BLK], parc(f"{sc[1]}{l}"), sS, ALU.mult, ALU.mult),
            waits={"cP": sw, "cG": waits["cG"]})
        g1 = op("gpsimd", lambda h: h.tensor_tensor(t1, t1, t2, ALU.add), waits={"cV": v2})
        last = 0
        for dst, c0, n in dst_writes:
            last = op("vector", lambda h, d=dst, c0=c0, n=n: h.tensor_tensor(
                d, t1[:, c0:c0 + n], rn_ap[:, c0:c0 + n], ALU.mult), waits={"cG": g1})
        prep_state["g1"] = g1
        prep_state["last"] = last
        return {"cV": last, "cG": g1}

    wo_ring = {"use": {}, "m": {}}

    def wo_load(cache, oc, slot, conv_w=None):
        w = {"cP": wo_ring["use"].get(slot, 0)}
        if conv_w:
            w.update(conv_w)
        m = dma("sync", f"dWo{slot}", wor[:, slot, :, :],
                cache[:, :, oc * 128:(oc + 1) * 128].transpose([1, 0, 2]), waits=w)
        wo_ring["m"][oc] = (f"dWo{slot}", m)
        return m

    def gla_block(b):
        t0 = b * BLK
        cg = 0
        for c in range(NCH):
            sl = fsc[:, c % 2, 0:BLK]
            xld = dma("scalar", f"dXl{c % 2}", sl, xT[c * 128:(c + 1) * 128, t0:t0 + BLK], waits={"cG": cg})
            cg = op("gpsimd", lambda h, c=c, sl=sl: h.tensor_copy(xnb[:, c, :], sl), waits={f"dXl{c % 2}": xld})
        pe = ln_stats([xnb[:, c, :] for c in range(NCH)], 1024, {"cG": cg})
        rstd = fsc[:, 3, 0:BLK]
        a3 = ln_chain(1024, rstd, {"cP": pe})
        t1b = t1bv
        vdone = 0
        for c in range(NCH):
            v1 = op("vector", lambda h, c=c: h.scalar_tensor_tensor(
                ga[:, c % 2, :], psA[:, 1024:1024 + BLK], -RD, xnb[:, c, :], ALU.mult, ALU.add),
                waits={"cA": a3, "cP": pe, "cV": vdone})
            vdone = op("vector", lambda h, c=c: h.tensor_tensor(t1b[:, c, :], ga[:, c % 2, :], rstd, ALU.mult),
                       waits={"cV": v1})
        tap("t1b", ovl[:, 8192:12288].rearrange("p (a b) -> p a b", a=NCH), np.s_[:, :, t0:t0 + BLK])
        evm = {}
        for oc in range(24):
            kind, p = oc // NP, oc % NP
            buf = oc % 2
            pe0 = 0
            for c in range(NCH):
                w = None
                if c == 0:
                    w = {"cG": glaw["cG"], "cA": glaw["cA"], "cV": max(vdone, evm.get(oc - 2, 0))}
                sec = ("wq", "wk", "wv")[kind]
                pe0 = mm(psB[:, buf * 512:buf * 512 + BLK],
                         warena[:, GW[sec] + c * 1024 + p * 128: GW[sec] + c * 1024 + (p + 1) * 128],
                         t1b[:, c, :], start=(c == 0), stop=(c == NCH - 1), waits=w)
            if kind == 0:
                dst = qp[:, p, :]
            elif kind == 1:
                dst = kview[:, p, 0:BLK]
            else:
                dst = vview[:, p * BLK:(p + 1) * BLK]
            evm[oc] = op("vector", lambda h, d=dst, buf=buf: h.tensor_copy(d, psB[:, buf * 512:buf * 512 + BLK]),
                         waits={"cP": pe0})
        conv_step(16)
        evq = max(evm[oc] for oc in range(0, 8))
        evk = max(evm[oc] for oc in range(8, 16))
        evv = max(evm[oc] for oc in range(16, 24))
        khm = {}
        rn_a = {}
        rn_pe = {}
        for p in range(NP):
            bank = 1024 + (p % 2) * 512
            sq = chs[:, 4 + (p % 2), :]
            pl = op("gpsimd", lambda h, s=kview[:, p, 0:BLK], d=sq: h.tensor_tensor(d, s, s, ALU.mult),
                    waits={"cV": evk, "cP": rn_pe.get(p - 2, 0), "cA": rn_a.get(p - 2, 0)})
            pe1 = mm(psA[:, bank:bank + BLK], oneshb, sq, start=True, stop=False,
                     waits={"cG": pl, "cA": rn_a.get(p - 2, 0)})
            rn_pe[p] = pe1 = mm(psA[:, bank:bank + BLK], onesb_row, eps_row, start=False, stop=True)
            a1 = op("scalar", lambda h, bank=bank: h.activation(fsc[:, 2, 0:BLK], psA[:, bank:bank + BLK], AF.Ln),
                    waits={"cP": pe1, "cV": fsc2_rd[0]})
            a2 = op("scalar", lambda h, p=p: h.activation(rns[:, p % 2, :], fsc[:, 2, 0:BLK], AF.Exp, scale=-0.5),
                    waits={"cA": a1, "cV": khm.get(p - 2, 0)})
            rn_a[p] = a2
            khm[p] = op("vector", lambda h, p=p: h.tensor_tensor(
                kview[:, p, 0:BLK], kview[:, p, 0:BLK], rns[:, p % 2, :], ALU.mult), waits={"cA": a2})
        tap("qp", qp[:, :, :], np.s_[b])
        tap("kh", ovl[:, 0:4096], np.s_[b])
        tap("vv", ovl[:, 4096:8192], np.s_[b])
        lastr = {"chs": 0, "vt_pe": 0, "yev": 0}
        dpipe = {}

        def pipeline(p):
            s = p % 2
            gap, gbp, gcp = ga[:, s, :], gb[:, s, :], gc[:, s, :]
            pe1 = mm(psA[:, 1024:1024 + BLK], oneshb, kview[:, p, 0:BLK], start=True, stop=True,
                     waits={"cV": max(khm[p], lastr["yev"], dpipe.get(p - 2, 0)), "cA": k.ctr["cA"]})
            v1 = op("vector", lambda h, gap=gap: h.tensor_scalar(gap, psA[:, 1024:1024 + BLK], 1.0 / DK, None, ALU.mult),
                    waits={"cP": pe1})
            s1 = op("scalar", lambda h, p=p, gap=gap, gbp=gbp: h.activation(
                gbp, gap, AF.Sigmoid, bias=parc(f"bg{p}"), scale=parc(f"wg{p}")), waits={"cV": v1, **w0})
            s2 = op("scalar", lambda h, p=p, gap=gap, gcp=gcp: h.activation(
                gcp, gap, AF.Sigmoid, bias=parc(f"bl{p}"), scale=parc(f"wl{p}")), waits={"cV": v1})
            v2 = op("vector", lambda h, gap=gap, gbp=gbp, gcp=gcp: h.tensor_tensor(gap, gcp, gbp, ALU.mult),
                    waits={"cA": s2})
            v2 = op("vector", lambda h, gap=gap, gbp=gbp: h.tensor_tensor(gap, gbp, gap, ALU.subtract),
                    waits={"cV": v2, "cA": s1})
            v3 = op("vector", lambda h, gap=gap, gbp=gbp: h.tensor_scalar(gbp, gap, -1.0, 1.0, ALU.mult, ALU.add),
                    waits={"cV": v2})
            v2 = op("vector", lambda h, gap=gap: h.tensor_scalar_max(gap, gap, EPS_GLA), waits={"cV": v2})
            v3 = op("vector", lambda h, gbp=gbp: h.tensor_scalar_max(gbp, gbp, EPS_GLA), waits={"cV": v3})
            s3 = op("scalar", lambda h, gap=gap, gcp=gcp: h.activation(gcp, gap, AF.Ln), waits={"cV": v2})
            s4 = op("scalar", lambda h, gap=gap, gbp=gbp: h.activation(gap, gbp, AF.Ln), waits={"cV": v3, "cA": s3})
            sc = 0
            for tc in range(NTC):
                sc = op("vector", lambda h, gbp=gbp, gcp=gcp, tc=tc: h.tensor_tensor_scan(
                    gbp[:, tc * 128:(tc + 1) * 128], gcp[:, tc * 128:(tc + 1) * 128],
                    gcp[:, tc * 128:(tc + 1) * 128], 0.0, ALU.add, ALU.bypass),
                    waits={"cA": s4})
            v4 = op("vector", lambda h, gap=gap, gbp=gbp, gcp=gcp: h.tensor_tensor(gcp, gap, gbp, ALU.subtract),
                    waits={"cV": sc, "cA": s4})
            dpipe[p] = v4

        pipeline(0)
        for p in range(NP):
            if p + 1 < NP:
                pipeline(p + 1)
            s = p % 2
            gbp, gcp = gb[:, s, :], gc[:, s, :]
            expL = chs[:, 0, :]
            qq = chs[:, 1, :]
            e1 = op("scalar", lambda h, gbp=gbp: h.activation(expL, gbp, AF.Exp),
                    waits={"cV": max(dpipe[p], lastr["chs"])})
            q2 = op("vector", lambda h, p=p: h.tensor_tensor(qq, qp[:, p, :], expL, ALU.mult),
                    waits={"cA": e1, "cV": evq, "cP": k.ctr["cP"]})
            for tc in range(NTC):
                c0 = tc * 128
                eL = fsc[:, 1, tc:tc + 1]
                eA = op("scalar", lambda h, gbp=gbp, eL=eL, c0=c0: h.activation(
                    eL, gbp[:, c0 + 127:c0 + 128], AF.Exp), waits={"cV": max(dpipe[p], lastr["chs"])})
                fv = chs[:, 2, 0:128]
                f1 = op("scalar", lambda h, gbp=gbp, gcp=gcp, c0=c0: h.activation(
                    chs[:, 2, 0:128], gcp[:, c0:c0 + 128], AF.Exp, bias=gbp[:, c0 + 127:c0 + 128]),
                    waits={"cV": max(dpipe[p], lastr["chs"])})
                f2 = op("vector", lambda h, p=p, c0=c0: h.tensor_tensor(
                    chs[:, 3, 0:128], vview[:, p * BLK + c0: p * BLK + c0 + 128], fv, ALU.mult),
                    waits={"cA": f1, "cV": evv, "cP": lastr["vt_pe"]})
                t1m = tr(psC[:, 0:128], kview[:, p, c0:c0 + 128], identb,
                         waits={"cV": max(khm[p], lastr["chs"])})
                ke = op("vector", lambda h: h.tensor_copy(chs[:, 4, 0:128], psC[:, 0:128]),
                        waits={"cP": t1m})
                t2m = tr(psC[:, 1024:1152], chs[:, 3, 0:128], identb, waits={"cV": f2})
                ve = op("vector", lambda h: h.tensor_copy(chs[:, 5, 0:128], psC[:, 1024:1152]),
                        waits={"cP": t2m})
                amk = {}
                for hh in range(2):
                    r0 = hh * 64
                    araw_bank = 0 if hh == 0 else 512
                    d_bank = 256 if hh == 0 else 768
                    ar = mm(psB[:, araw_bank:araw_bank + 128],
                            vview[r0:r0 + 64, p * BLK + c0: p * BLK + c0 + 128],
                            qp[r0:r0 + 64, p, c0:c0 + 128], start=True, stop=True,
                            waits={"cV": max(evv, evq, dpipe[p])})
                    ones_r = cst[r0:r0 + 1, CO["onesf"]:CO["onesf"] + 128]
                    d1 = mm(psA[:, d_bank:d_bank + 128], gcp[r0:r0 + 1, c0:c0 + 128], ones_r,
                            start=True, stop=True, waits={"cV": dpipe[p]})
                    d1 = mm(psA[:, d_bank:d_bank + 128], ones_r, gbp[r0:r0 + 1, c0:c0 + 128],
                            start=False, stop=True)
                    md = op("vector", lambda h, d_bank=d_bank: h.tensor_tensor(
                        fsc[:, 2, 0:128], psA[:, d_bank:d_bank + 128], cst[:, CO["dmask"]:CO["dmask"] + 128], ALU.add),
                        waits={"cP": d1, "cA": k.ctr["cA"]})
                    de = op("scalar", lambda h: h.activation(pbf[:, 2, 0:128], fsc[:, 2, 0:128], AF.Exp),
                            waits={"cV": md})
                    amk[hh] = op("vector", lambda h, hh=hh, araw_bank=araw_bank: h.tensor_tensor(
                        pbf[:, 2, 128 * (hh + 1):128 * (hh + 2)], psB[:, araw_bank:araw_bank + 128],
                        pbf[:, 2, 0:128], ALU.mult),
                        waits={"cA": de, "cP": ar})
                if b == 0 and p == 0 and tc == 0:
                    tap("dbg_at", pbf[:, 2, :])
                    tap("dbg_L", gb[:, 0, :])
                    tap("dbg_lnad", gc[:, 0, :])
                    tap("dbg_ktok", chs[:, 4, 0:128])
                    tap("dbg_qq", chs[:, 1, :])
                ylast = mm(psA[0:64, 1024:1152], chs[:, 4, 0:64], pbf[:, 2, 128:256], start=True, stop=False,
                           waits={"cV": max(ke, amk[0], q2), "cG": k.ctr["cG"]})
                mm(psA[0:64, 1024:1152], sbfA[:, p, :], qq[:, c0:c0 + 128], start=False, stop=False)
                mm(psA[64:128, 1024:1152], chs[:, 4, 64:128], pbf[:, 2, 256:384], start=True, stop=False,
                   waits={"cV": amk[1]})
                ylast = mm(psA[64:128, 1024:1152], sbfB[:, p, :], qq[:, c0:c0 + 128], start=False, stop=True)
                yev = op("vector", lambda h, p=p, c0=c0: h.tensor_copy(aout[:, p, c0:c0 + 128], psA[:, 1024:1152]),
                         waits={"cP": ylast})
                lastr["yev"] = yev
                wA = mm(psA[0:64, 1536:1600], chs[:, 5, 0:64], chs[:, 4, 0:64], start=True, stop=True,
                        waits={"cV": max(ve, ke), "cG": k.ctr["cG"]})
                wB = mm(psA[64:128, 1536:1600], chs[:, 5, 64:128], chs[:, 4, 64:128], start=True, stop=True)
                lastr["vt_pe"] = wB
                su = op("vector", lambda h, p=p, eL=eL: h.scalar_tensor_tensor(
                    ssb[:, p, :], ssb[:, p, :], eL, psA[:, 1536:1600], ALU.mult, ALU.add),
                    waits={"cP": wB, "cA": eA, "cG": k.ctr["cG"]})
                op("gpsimd", lambda h, p=p: h.tensor_copy(sbfA[0:64, p, :], ssb[0:64, p, :]),
                   waits={"cV": su, "cP": ylast})
                op("gpsimd", lambda h, p=p: h.tensor_copy(sbfB[64:128, p, :], ssb[64:128, p, :]),
                   waits={"cV": su})
                lastr["chs"] = max(yev, su)
        tap("glay", aout[:, :, :], np.s_[:, :, t0:t0 + BLK])
        rb = xnb
        evo = 0
        cvw = {"dCs0": glaw["dCs0"], "dCs1": glaw["dCs1"]}
        wo_load(wotc, 0, 0, cvw)
        wo_load(wotc, 1, 1, cvw)
        for oc in range(NCH):
            slot = oc % 3
            buf = oc % 2
            pe0 = 0
            for c in range(NCH):
                w = None
                if c == 0:
                    w = {"cV": max(lastr["yev"], evo), "cG": k.ctr["cG"], wo_ring["m"][oc][0]: wo_ring["m"][oc][1]}
                pe0 = mm(psB[:, buf * 512:buf * 512 + BLK], wor[:, slot, c, :], aout[:, c, :],
                         start=(c == 0), stop=(c == NCH - 1), waits=w)
            wo_ring["use"][slot] = pe0
            if oc + 2 < NCH:
                wo_load(wotc, oc + 2, (oc + 2) % 3, cvw)
            evo = op("vector", lambda h, oc=oc, buf=buf: h.tensor_copy(rb[:, oc, :], psB[:, buf * 512:buf * 512 + BLK]),
                     waits={"cP": pe0})
        pe = ln_stats([rb[:, c, :] for c in range(NCH)], 1024, {"cV": evo})
        a3 = ln_chain(1024, fsc[:, 3, 0:BLK], {"cP": pe})
        x1st = {0: 0, 1: 0}
        x1cg = 0
        for c in range(NCH):
            sl = fsc[:, c % 2, 0:BLK]
            ld = dma("scalar", f"dXl{c % 2}", sl, xT[c * 128:(c + 1) * 128, t0:t0 + BLK],
                     waits={"cV": k.ctr["cV"]})
            s2_ = c % 2
            v1 = op("vector", lambda h, c=c, s2_=s2_: h.scalar_tensor_tensor(
                ga[:, s2_, :], psA[:, 1024:1024 + BLK], -RD, rb[:, c, :], ALU.mult, ALU.add),
                waits={"cA": a3, "cP": pe, "cG": x1cg, f"dXs{c % 2}": x1st[c % 2]})
            v2 = op("vector", lambda h, s2_=s2_: h.tensor_tensor(ga[:, s2_, :], ga[:, s2_, :], fsc[:, 3, 0:BLK], ALU.mult),
                    waits={"cV": v1})
            v3 = op("vector", lambda h, sl=sl, s2_=s2_: h.tensor_tensor(gb[:, s2_, :], ga[:, s2_, :], sl, ALU.add),
                    waits={"cV": v2, f"dXl{c % 2}": ld})
            x1st[c % 2] = dma("scalar", f"dXs{c % 2}", x1m[c, :, t0:t0 + BLK], gb[:, s2_, :], waits={"cV": v3})
            x1cg = op("gpsimd", lambda h, c=c, s2_=s2_: h.tensor_copy(t1bv[:, c, :], gb[:, s2_, :]), waits={"cV": v3})
        x1b = t1bv
        tap("x1", ovl[:, 8192:12288].rearrange("p (a b) -> p a b", a=NCH), np.s_[:, :, t0:t0 + BLK])
        pe = ln_stats([x1b[:, c, :] for c in range(NCH)], 1024, {"cG": x1cg})
        a3 = ln_chain(1024, fsc[:, 3, 0:BLK], {"cP": pe})
        xst = {0: 0, 1: 0}
        t2cg = 0
        for c in range(NCH):
            s2_ = c % 2
            v1 = op("vector", lambda h, c=c, s2_=s2_: h.scalar_tensor_tensor(
                ga[:, s2_, :], psA[:, 1024:1024 + BLK], -RD, x1b[:, c, :], ALU.mult, ALU.add),
                waits={"cA": a3, "cP": pe, f"dXs{c % 2}": xst[c % 2], "cG": t2cg})
            v2 = op("vector", lambda h, s2_=s2_: h.tensor_tensor(ga[:, s2_, :], ga[:, s2_, :], fsc[:, 3, 0:BLK], ALU.mult),
                    waits={"cV": v1})
            xst[c % 2] = dma("scalar", f"dXs{c % 2}", xmas[c, :, t0:t0 + BLK], ga[:, s2_, :], waits={"cV": v2})
            t2cg = xb_store(c, t0, ga[:, s2_, :], c % 2, v2)
        conv_step(12)
        k.ph.flush()

    lw = {"wqkv": None, "w2": None}

    def attn_phase(l):
        cw = convq["done"].get(l)
        assert cw is not None, f"conversions for layer {l} not done in time"
        if l > 0:
            for c in range(NCH):
                tap(f"ffn{l - 1}", xmas[c], idx=np.s_[c])
        lw["wqkv"] = dma("sync", "dW", warena.ap().rearrange("p (c n) -> p c n", c=NCH),
                         qkvc[l].transpose([1, 0, 2]),
                         waits={"dCs0": cw["dCs0"], "dCs1": cw["dCs1"], "cP": k.ctr["cP"]})
        ldm = {0: xrd_load(0)[1]}
        for b in range(NBLK):
            attn_block(l, b, ldm)

    def attn_block(l, b, ldm):
        t0 = b * BLK
        tld = 0
        for i in range(4):
            tld = dma("sync", "dT", tabs[:, i, :], ctabs[i][:, t0:t0 + BLK],
                      waits={"cV": k.ctr["cV"], "cG": k.ctr["cG"]} if i == 0 else None)
        evac = {}
        prep_marks = {}
        rn_a = {}
        rnuse = {0: 0, 1: 0}
        order = list(range(8, 16)) + list(range(16, 24)) + list(range(0, 8))
        for idx, oc in enumerate(order):
            kind, p = oc // NP, oc % NP
            buf = idx % 2
            pe0 = 0
            for c in range(NCH):
                w = None
                if c == 0:
                    pv_ = evac.get(idx - 2, ("cV", 0))
                    w = {"dW": lw["wqkv"], pv_[0]: pv_[1], f"dXr{b % 2}": ldm[b]}
                pe0 = mm(psB[:, buf * 512:buf * 512 + BLK],
                         warena[:, c * 3072 + oc * 128: c * 3072 + (oc + 1) * 128],
                         xsl(b % 2, c, True), start=(c == 0), stop=(c == NCH - 1), waits=w)
            dst = chs[:, buf, :]
            pmw = prep_marks.get(idx - 2, {})
            evw = {"cP": pe0, "cG": pmw.get("cG", 0), "cV": pmw.get("cV", 0), "cA": pmw.get("cA", 0)}
            if idx % 2 == 0:
                ev = ("cV", op("vector", lambda h, d=dst, buf=buf: h.tensor_copy(d, psB[:, buf * 512:buf * 512 + BLK]),
                               waits=evw))
            else:
                ev = ("cA", op("scalar", lambda h, d=dst, buf=buf: h.activation(d, psB[:, buf * 512:buf * 512 + BLK], AF.Copy),
                               waits=evw))
            evac[idx] = ev
            if kind == 2:
                tm = tprev = 0
                tmbank = {0: 0, 1: 0}
                for tc in range(NTC):
                    slot = (NTC * b + tc) % 6
                    _w = {"cV": tmbank[tc % 2]}
                    _w[ev[0]] = max(_w.get(ev[0], 0), ev[1])
                    t1m = tr(psC[:, (tc % 2) * 1024:(tc % 2) * 1024 + 128], dst[:, tc * 128:(tc + 1) * 128], identb,
                             waits=_w)
                    tm = op("vector", lambda h, slot=slot, p=p, tc=tc: h.tensor_copy(
                        vtok[:, slot, p * 128:(p + 1) * 128], psC[:, (tc % 2) * 1024:(tc % 2) * 1024 + 128]),
                        waits={"cP": t1m})
                    tmbank[tc % 2] = tm
                    tprev = t1m
                prep_marks[idx] = {"cV": tm, "cG": 0, "cA": 0, "cP": tprev}
            else:
                bank = 1024 + (p % 2) * 512
                sq = chs[:, 4 + (p % 2), :]
                _w = {"cA": rnuse[p % 2]}
                _w[ev[0]] = max(_w.get(ev[0], 0), ev[1])
                pl = op("gpsimd", lambda h, d=dst, sq=sq: h.tensor_tensor(sq, d, d, ALU.mult), waits=_w)
                pe1 = mm(psA[:, bank:bank + BLK], oneshb, sq, start=True, stop=False,
                         waits={"cG": pl, "cA": rnuse[p % 2]})
                pe1 = mm(psA[:, bank:bank + BLK], onesb_row, eps_row, start=False, stop=True)
                a1 = op("scalar", lambda h, bank=bank: h.activation(fsc[:, 2, 0:BLK], psA[:, bank:bank + BLK], AF.Ln),
                        waits={"cP": pe1, "cV": max(prep_state["last"], fsc2_rd[0])})
                rnuse[p % 2] = a1
                a2 = op("scalar", lambda h, p=p: h.activation(rns[:, p % 2, :], fsc[:, 2, 0:BLK], AF.Exp, scale=-0.5),
                        waits={"cA": a1, "cV": prep_state["last"]})
                rn_a[(kind, p)] = a2
                _w = {"cA": a2, "dT": tld, "dW": pld}
                _w[ev[0]] = max(_w.get(ev[0], 0), ev[1])
                if kind == 1:
                    dsts = [(kring[:, p, (NTC * b + tc) % 6, :], tc * 128, 128) for tc in range(NTC)]
                    pm = prep(dst, rns[:, p % 2, :], dsts, tabs[:, 2, :], tabs[:, 3, :],
                              swap_bank=(p % 2) * 512, l=l, isq=False, waits=_w)
                else:
                    pm = prep(dst, rns[:, p % 2, :], [(qp[:, p, :], 0, BLK)], tabs[:, 0, :], tabs[:, 1, :],
                              swap_bank=(p % 2) * 512, l=l, isq=True, waits=_w)
                pm["cA"] = a2
                prep_marks[idx] = pm
        qkv_done = k.ctr["cP"]
        xrd_mark(b % 2)
        if b + 1 < NBLK:
            ldm[b + 1] = xrd_load(b + 1)[1]
        conv_step(14)
        for _oc in range(3):
            wo_load(woc[l], _oc, _oc)
        prep_all = {"cV": k.ctr["cV"], "cG": k.ctr["cG"], "cA": k.ctr["cA"]}
        pvev = {}
        scm = {}
        pvpe = {}
        last_a1 = 0
        last_tpe = 0
        stq_state = {}

        def iters():
            for qt in range(NTC):
                g = NTC * b + qt
                kcs = [c for c in (g - 2, g - 1, g) if c >= 0]
                for p in range(NP):
                    yield qt, p, kcs

        def stageA(qt, p, kcs, pi):
            nonlocal last_a1
            nkc = len(kcs)
            nk = nkc * 128
            sb0 = (pi % 2) * 1024
            sb1 = (pi % 2) * 1024 + 512
            pslot = (pi % 2) * 2
            pe0 = 0
            for i, kc in enumerate(kcs):
                slot = kc % 6
                w = {**prep_all, "cA": max(prep_all["cA"], scm.get(pi - 1, 0))} if i == 0 else None
                mm(psA[:, sb0 + i * 128: sb0 + (i + 1) * 128], qp[0:64, p, qt * 128:(qt + 1) * 128],
                   kring[0:64, p, slot, :], start=True, stop=True, waits=w)
                pe0 = mm(psA[:, sb1 + i * 128: sb1 + (i + 1) * 128], qp[64:128, p, qt * 128:(qt + 1) * 128],
                         kring[64:128, p, slot, :], start=True, stop=True)
            res = []
            for hh, sb in ((0, sb0), (1, sb1)):
                mrow = cst[:, CO["mask"] + 384 - nk: CO["mask"] + 384]
                v1 = op("vector", lambda h, sb=sb, nk=nk, mrow=mrow, hh=hh: h.tensor_tensor(
                    fsc[:, hh, 0:nk], psA[:, sb:sb + nk], mrow, ALU.add),
                    waits={"cP": pe0, "cA": last_a1})
                v2 = op("vector", lambda h, nk=nk, hh=hh: h.reduce_max(
                    fsc[:, 2, 4 * hh:4 * hh + 1], fsc[:, hh, 0:nk], mybir.AxisListType.X, negate=True),
                    waits={"cV": v1})
                a1 = op("scalar", lambda h, nk=nk, hh=hh, pslot=pslot: h.activation(
                    pbf[:, pslot + hh, 0:nk], fsc[:, hh, 0:nk], AF.Exp, bias=fsc[:, 2, 4 * hh:4 * hh + 1],
                    accum_out=fsc[:, 2, 4 * hh + 1:4 * hh + 2]),
                    waits={"cV": v2, "cP": last_tpe})
                last_a1 = a1
                v3 = op("vector", lambda h, hh=hh: h.reciprocal(fsc[:, 2, 4 * hh + 2:4 * hh + 3],
                                                                fsc[:, 2, 4 * hh + 1:4 * hh + 2]), waits={"cA": a1})
                v4 = op("vector", lambda h, nk=nk, hh=hh, pslot=pslot: h.tensor_scalar_mul(
                    pbf[:, pslot + hh, 0:nk], pbf[:, pslot + hh, 0:nk], fsc[:, 2, 4 * hh + 2:4 * hh + 3]),
                    waits={"cV": v3})
                scm[pi] = max(scm.get(pi, 0), a1)
                res.append(v4)
            return res

        def stageB(qt, p, kcs, pi, v4s):
            nonlocal last_tpe
            nkc = len(kcs)
            nk = nkc * 128
            pslot = (pi % 2) * 2
            for hh in range(2):
                tpe = 0
                for i in range(nkc):
                    tpe = tr(psC[:, (pi % 2) * 1024 + i * 128:(pi % 2) * 1024 + (i + 1) * 128],
                             pbf[:, pslot + hh, i * 128:(i + 1) * 128], identb, waits={"cV": v4s[hh]})
                last_tpe = tpe
                v5 = op("vector", lambda h, pi=pi, nk=nk: h.tensor_copy(
                    pbf[:, 4, 0:nk], psC[:, (pi % 2) * 1024:(pi % 2) * 1024 + nk]), waits={"cP": tpe})
                pvbank = 512 * ((pi // 4) % 2)
                col = (pi % 4) * 128
                for i, kc in enumerate(kcs):
                    slot = kc % 6
                    w = {"cV": max(v5, pvev.get((pi // 4) - 2, 0))}
                    pvpe[pi] = mm(psB[hh * 64:(hh + 1) * 64, pvbank + col: pvbank + col + 128],
                                  vtok[:, slot, p * 128 + hh * 64: p * 128 + hh * 64 + 64],
                                  pbf[:, 4, i * 128:(i + 1) * 128],
                                  start=(i == 0), stop=(i == nkc - 1), waits=w)
            if pi % 4 == 3:
                grp = pi // 4
                pvev[grp] = op("vector", lambda h, qt=qt, p=p, pvbank=512 * (grp % 2): h.tensor_copy(
                    aout[:, p - 3:p + 1, qt * 128:(qt + 1) * 128], psB[:, pvbank:pvbank + 512]),
                    waits={"cP": pvpe[pi]})

        prev = None
        for pi, (qt, p, kcs) in enumerate(iters()):
            v4s = stageA(qt, p, kcs, pi)
            if prev is not None:
                stageB(*prev)
            prev = (qt, p, kcs, pi, v4s)
        stageB(*prev)
        att_done = {"cV": k.ctr["cV"]}
        stm = {}
        stq = {0: k.ctr["dXs0"], 1: k.ctr["dXs1"]}
        for oc in range(NCH):
            slot = oc % 3
            buf = oc % 2
            ld = dma("scalar", f"dXl{buf}", xfio[:, buf, :], xmas[oc, :, t0:t0 + BLK],
                     waits={"cV": stm.get(oc - 2, 0), "dXs0": k.ctr["dXs0"], "dXs1": k.ctr["dXs1"]})
            pe0 = 0
            for c in range(NCH):
                w = None
                if c == 0:
                    w = {wo_ring["m"][oc][0]: wo_ring["m"][oc][1], "cV": max(att_done["cV"], stm.get(oc - 2, 0))}
                pe0 = mm(psB[:, buf * 512:buf * 512 + BLK], wor[:, slot, c, :], aout[:, c, :],
                         start=(c == 0), stop=(c == NCH - 1), waits=w)
            wo_ring["use"][slot] = pe0
            if oc + 3 < NCH:
                wo_load(woc[l], oc + 3, (oc + 3) % 3)
            v1 = op("vector", lambda h, buf=buf: h.tensor_tensor(
                xfio[:, 2 + buf, :], xfio[:, buf, :], psB[:, buf * 512:buf * 512 + BLK], ALU.add),
                waits={"cP": pe0, f"dXl{buf}": ld, f"dXs{buf}": stq[buf]})
            stm[oc] = v1
            stq[buf] = dma("scalar", f"dXs{buf}", xmas[oc, :, t0:t0 + BLK], xfio[:, 2 + buf, :], waits={"cV": v1})
            xb_store(oc, t0, xfio[:, 2 + buf, :], buf, v1)
        k.ph.flush()

    def ffn_phase(l):
        cw = convq["done"][l]
        for c in range(NCH):
            tap(f"attn{l}", xmas[c], idx=np.s_[c])
        lw["w2"] = dma("sync", "dW", warena[:, 0:NJ * 1024].rearrange("p (j n) -> p j n", j=NJ),
                       w2c[l].transpose([1, 0, 2]),
                       waits={"dCs0": cw["dCs0"], "dCs1": cw["dCs1"], "cP": k.ctr["cP"]})
        ldm = {0: xrd_load(0)[1]}
        for b in range(NBLK):
            ffn_block(l, b, ldm)

    def ffn_block(l, b, ldm):
        t0 = b * BLK
        w1m = {}
        ring_free = {}

        def w1_load(j):
            slot = j % 4
            m = 0
            for ag in range(2):
                m = dma("sync", f"dW1{slot % 2}", w1r[:, slot, ag, :, :], w1c[l, ag, j].transpose([1, 0, 2]),
                        waits={"cP": ring_free.get(slot, 0)})
            w1m[j] = (f"dW1{slot % 2}", m)
        for jj in range(4):
            w1_load(jj)
        srcs = [xsl(b % 2, c, True) for c in range(NCH)]
        pe = ln_stats(srcs, 1024, {"cV": k.ctr["cV"], "cG": k.ctr["cG"], "cP": k.ctr["cP"],
                                   f"dXr{b % 2}": ldm[b]})
        rstd = fsc[:, 3, 0:BLK]
        a3 = ln_chain(1024, rstd, {"cP": pe})
        xn_done = 0
        for c in range(NCH):
            v1 = op("vector", lambda h, c=c, s=srcs[c]: h.scalar_tensor_tensor(
                ga[:, c % 2, :], psA[:, 1024:1024 + BLK], -RD, s, ALU.mult, ALU.add),
                waits={"cA": a3, "cP": pe, "cV": xn_done})
            v2 = op("vector", lambda h, c=c: h.tensor_tensor(ga[:, c % 2, :], ga[:, c % 2, :], rstd, ALU.mult),
                    waits={"cV": v1})
            xn_done = op("vector", lambda h, c=c: h.tensor_scalar(
                xnb[:, c, :], ga[:, c % 2, :], parc(f"lng{l}", c), parc(f"lnb{l}", c), ALU.mult, ALU.add),
                waits={"cV": v2, **w0})
        xrd_mark(b % 2)
        if b + 1 < NBLK:
            ldm[b + 1] = xrd_load(b + 1)[1]
        conv_step(7)
        um = {}
        for j in range(NJ):
            jn = 128 if j < NJ - 1 else JL
            slot = j % 4
            ab = (j % 2) * 1024
            gbk = (j % 2) * 1024 + 512
            pe0 = 0
            for c in range(NCH):
                w = None
                if c == 0:
                    w = {w1m[j][0]: w1m[j][1], "cV": xn_done, "cG": um.get(j - 2, 0)}
                mm(psA[0:jn, ab:ab + BLK], w1r[:, slot, 0, c, 0:jn], xnb[:, c, :],
                   start=(c == 0), stop=(c == NCH - 1), waits=w)
                pe0 = mm(psA[0:jn, gbk:gbk + BLK], w1r[:, slot, 1, c, 0:jn], xnb[:, c, :],
                         start=(c == 0), stop=(c == NCH - 1))
            ring_free[slot] = pe0
            if j + 4 < NJ:
                w1_load(j + 4)
            ga1 = op("scalar", lambda h, j=j, jn=jn, gbk=gbk: h.activation(
                chs[0:jn, 0, :], psA[0:jn, gbk:gbk + BLK], AF.Gelu, bias=parc(f"b1g{l}", j, jn)),
                waits={"cP": pe0, "cG": um.get(j - 1, 0)})
            v1 = op("vector", lambda h, j=j, jn=jn, ab=ab: h.tensor_scalar(
                chs[0:jn, 1, :], psA[0:jn, ab:ab + BLK], parc(f"b1a{l}", j, jn), None, ALU.add),
                waits={"cP": pe0, "cG": um.get(j - 1, 0)})
            um[j] = op("gpsimd", lambda h, j=j, jn=jn: h.tensor_tensor(
                usb[0:jn, j, :], chs[0:jn, 1, :], chs[0:jn, 0, :], ALU.mult),
                waits={"cA": ga1, "cV": v1})
        conv_step(7)
        u_all = {"cG": k.ctr["cG"]}
        stm = {}
        stq = {0: k.ctr["dXs0"], 1: k.ctr["dXs1"]}
        for oc in range(NCH):
            buf = oc % 2
            ld = dma("scalar", f"dXl{buf}", xfio[:, buf, :], xmas[oc, :, t0:t0 + BLK],
                     waits={"cV": stm.get(oc - 2, 0), "dXs0": k.ctr["dXs0"], "dXs1": k.ctr["dXs1"]})
            pe0 = 0
            for j in range(NJ):
                jn = 128 if j < NJ - 1 else JL
                w = None
                if j == 0:
                    w = {**u_all, "dW": lw["w2"], "cV": stm.get(oc - 2, 0)}
                pe0 = mm(psB[:, buf * 512:buf * 512 + BLK],
                         warena[0:jn, j * 1024 + oc * 128: j * 1024 + (oc + 1) * 128],
                         usb[0:jn, j, :], start=(j == 0), stop=(j == NJ - 1), waits=w)
            v1 = op("vector", lambda h, oc=oc, buf=buf: h.scalar_tensor_tensor(
                xfio[:, 2 + buf, :], psB[:, buf * 512:buf * 512 + BLK], parc(f"b2{l}", oc), xfio[:, buf, :],
                ALU.add, ALU.add), waits={"cP": pe0, f"dXl{buf}": ld, f"dXs{buf}": stq[buf]})
            stm[oc] = v1
            stq[buf] = dma("scalar", f"dXs{buf}", xmas[oc, :, t0:t0 + BLK], xfio[:, 2 + buf, :], waits={"cV": v1})
            xb_store(oc, t0, xfio[:, 2 + buf, :], buf, v1)
        k.ph.flush()

    def epi_block(b, ldm):
        t0 = b * BLK
        srcs = [xsl(b % 2, c, False) for c in range(NCH)]
        pe = ln_stats(srcs, 1024, {"cV": k.ctr["cV"], "cG": k.ctr["cG"], "cP": k.ctr["cP"],
                                   f"dXr{b % 2}": ldm[b]})
        rstd = fsc[:, 3, 0:BLK]
        a3 = ln_chain(1024, rstd, {"cP": pe})
        stm = {}
        stq = {0: k.ctr["dOut"], 1: k.ctr["dOut"]}
        for c in range(NCH):
            buf = c % 2
            ld = dma("scalar", f"dXl{buf}", xfio[:, buf, :], x1m[c, :, t0:t0 + BLK],
                     waits={"cV": stm.get(c - 2, 0), "dXs0": k.ctr["dXs0"], "dXs1": k.ctr["dXs1"]})
            v1 = op("vector", lambda h, c=c, s=srcs[c]: h.scalar_tensor_tensor(
                ga[:, c % 2, :], psA[:, 1024:1024 + BLK], -RD, s, ALU.mult, ALU.add),
                waits={"cA": a3, "cP": pe, "cV": stm.get(c - 2, 0)})
            v2 = op("vector", lambda h, c=c: h.tensor_tensor(ga[:, c % 2, :], ga[:, c % 2, :], rstd, ALU.mult),
                    waits={"cV": v1})
            v3 = op("vector", lambda h, c=c, buf=buf: h.tensor_tensor(xfio[:, 2 + buf, :], ga[:, c % 2, :],
                                                                      xfio[:, buf, :], ALU.add),
                    waits={"cV": v2, f"dXl{buf}": ld, "dOut": stq[buf]})
            stm[c] = v3
            stq[buf] = dma("scalar", "dOut", outT[c, :, t0:t0 + BLK], xfio[:, 2 + buf, :], waits={"cV": v3})
        xrd_mark(b % 2)
        if b + 1 < NBLK:
            ldm[b + 1] = xrd_load(b + 1)[1]
        k.ph.flush()

    k.ph.flush()
    for b in range(NBLK):
        gla_block(b)
    for l in range(NL):
        if l + 1 < NL:
            conv_push(l + 1)
        attn_phase(l)
        ffn_phase(l)
    ldm_e = {0: xrd_load(0)[1]}
    for b in range(NBLK):
        epi_block(b, ldm_e)
    for q in ["dOut", "dTap", "dXs0", "dXs1", "dXb0", "dXb1"]:
        k.emit("sync", lambda h, q=q: h.wait_ge(k.sems[q], k.ctr[q]))
    k.ph.flush()
    return nc


def _in_maps(inputs):
    c = host_consts()
    par = host_par(inputs)
    f32c = lambda a: np.ascontiguousarray(np.asarray(a, np.float32))
    shared = dict(
        wqT=f32c(np.asarray(inputs["L0_attn_Wq"]).T), wkT=f32c(np.asarray(inputs["L0_attn_Wk"]).T),
        wvT=f32c(np.asarray(inputs["L0_attn_Wv"]).T), woTg=f32c(np.asarray(inputs["L0_attn_Wo"]).T),
        wqkv_in=f32c(inputs["L0_ffn_Wqkv"]), wo_in=f32c(inputs["L0_ffn_Wo"]),
        w1_in=f32c(inputs["L0_ffn_W1"]), w2_in=f32c(inputs["L0_ffn_W2"]),
        par_in=par,
        c_ident=c["ident"], c_swapT=c["swapT"], c_onesf=c["onesf"], c_onesh=c["onesh"],
        c_mask=c["masklog"], c_dmask=c["dmask"],
        c_ctq=c["ctq"], c_stq=c["stq"], c_ctk=c["ctk"], c_stk=c["stk"],
    )
    x = np.asarray(inputs["L0_attn_x"], np.float32)
    maps = []
    for core in range(8):
        m = dict(shared)
        m["xT"] = np.ascontiguousarray(x[core].T)
        maps.append(m)
    return maps


def kernel(**inputs):
    if "nc" not in _bcache:
        _bcache["nc"] = build()
    nc = _bcache["nc"]
    maps = _in_maps(inputs)
    res = run_bass_kernel_spmd(nc, maps, core_ids=list(range(8)))
    out = np.empty((8, T, D), np.float32)
    for core in range(8):
        out[core] = res.results[core]["outT"].reshape(D, T).T
    return out


# version 11
# speedup vs baseline: 1.1085x; 1.0838x over previous
import numpy as np

import concourse.bass as bass
import concourse.mybir as mybir
from concourse.bass_utils import run_bass_kernel_spmd

F32 = mybir.dt.float32
BF16 = mybir.dt.bfloat16
AF = mybir.ActivationFunctionType
ALU = mybir.AluOpType

D = 1024
T = 4096
NCH = 8
H = 16
DK = 64
NP = 8
NL = 6
DI = 2730
NJ = 22
JL = 42
BLK = 512
NBLK = T // BLK
NTC = BLK // 128
NEG = -1e30
EPS_LN = 1e-5
EPS_L2 = 1e-19
EPS_GLA = 1e-19
RD = 1.0 / D

ENGINES = ("sync", "tensor", "vector", "scalar", "gpsimd")
EX = {"tensor": "cP", "vector": "cV", "scalar": "cA", "gpsimd": "cG"}
_bcache = {}

_PC = {}
_cidx = 0
def _pc(name, n=1):
    global _cidx
    _PC[name] = _cidx
    _cidx += n
for _p in range(NP):
    _pc(f"wg{_p}"); _pc(f"bg{_p}"); _pc(f"wl{_p}"); _pc(f"bl{_p}")
for _l in range(NL):
    _pc(f"qs{_l}"); _pc(f"qssw{_l}"); _pc(f"ks{_l}"); _pc(f"kssw{_l}")
    _pc(f"lng{_l}", NCH); _pc(f"lnb{_l}", NCH); _pc(f"b2{_l}", NCH)
    _pc(f"b1a{_l}", NJ); _pc(f"b1g{_l}", NJ)
PARW = _cidx


def host_consts():
    if "c" in _bcache:
        return _bcache["c"]
    ident = np.eye(128, dtype=np.float32)
    S = np.zeros((128, 128), np.float32)
    for b0 in (0, 64):
        for d in range(64):
            S[b0 + d, b0 + (d + 32) % 64] = 1.0
    swapT = np.ascontiguousarray(S.T)
    onesf = np.ones((128, 128), np.float32)
    onesh = np.zeros((128, 128), np.float32)
    onesh[:64, :64] = 1.0
    onesh[64:, 64:] = 1.0
    ii = np.arange(128)[:, None]
    jj = np.arange(128)[None, :]
    masklog = np.zeros((128, 384), np.float32)
    masklog[:, :128] = np.where(ii <= jj, 0.0, NEG)
    masklog[:, 256:] = np.where(ii >= jj, 0.0, NEG)
    dmask = np.where(ii <= jj, 0.0, NEG).astype(np.float32)
    half = np.arange(0, DK, 2, dtype=np.float32)
    inv_freq = 1.0 / 10000.0 ** (half / DK)
    t = np.arange(T, dtype=np.float32)
    freqs = np.concatenate([t[:, None] * inv_freq[None, :]] * 2, -1)
    base_scale = (half + 0.4 * DK) / (1.4 * DK)
    power = (t - T // 2) / 128.0
    scale = np.concatenate([base_scale[None, :] ** power[:, None]] * 2, -1)
    cos, sin = np.cos(freqs), np.sin(freqs)
    sign = np.concatenate([-np.ones(32), np.ones(32)]).astype(np.float32)
    s8 = np.sqrt(8.0)
    dup = lambda a: np.ascontiguousarray(np.concatenate([a, a], 0), dtype=np.float32)
    c = dict(
        ident=ident, swapT=swapT, onesf=onesf, onesh=onesh, masklog=masklog, dmask=dmask,
        ctq=dup((cos * scale * s8).T), stq=dup((sin * scale * sign[None, :] * s8).T),
        ctk=dup((cos / scale * s8).T), stk=dup((sin / scale * sign[None, :] * s8).T),
    )
    _bcache["c"] = c
    return c


def host_par(ins):
    par = np.zeros((128, PARW), np.float32)
    dup2 = lambda v: np.concatenate([v, v])
    for p in range(NP):
        for nm, key in [("wg", "L0_attn_W_gamma"), ("bg", "L0_attn_b_gamma"),
                        ("wl", "L0_attn_W_lambda"), ("bl", "L0_attn_b_lambda")]:
            arr = np.asarray(ins[key], np.float32)
            col = np.empty(128, np.float32)
            col[:64] = arr[2 * p]
            col[64:] = arr[2 * p + 1]
            par[:, _PC[f"{nm}{p}"]] = col
    for l in range(NL):
        qs = np.asarray(ins["L0_ffn_q_scale"][l], np.float32)
        ks = np.asarray(ins["L0_ffn_k_scale"][l], np.float32)
        sw = lambda v: np.concatenate([v[32:], v[:32]])
        par[:, _PC[f"qs{l}"]] = dup2(qs)
        par[:, _PC[f"qssw{l}"]] = dup2(sw(qs))
        par[:, _PC[f"ks{l}"]] = dup2(ks)
        par[:, _PC[f"kssw{l}"]] = dup2(sw(ks))
        par[:, _PC[f"lng{l}"]:_PC[f"lng{l}"] + NCH] = np.asarray(ins["L0_ffn_ln_g"][l], np.float32).reshape(NCH, 128).T
        par[:, _PC[f"lnb{l}"]:_PC[f"lnb{l}"] + NCH] = np.asarray(ins["L0_ffn_ln_b"][l], np.float32).reshape(NCH, 128).T
        par[:, _PC[f"b2{l}"]:_PC[f"b2{l}"] + NCH] = np.asarray(ins["L0_ffn_b2"][l], np.float32).reshape(NCH, 128).T
        b1 = np.asarray(ins["L0_ffn_b1"][l], np.float32)
        b1p = np.zeros((2, NJ * 128), np.float32)
        b1p[0, :DI] = b1[:DI]
        b1p[1, :DI] = b1[DI:]
        par[:, _PC[f"b1a{l}"]:_PC[f"b1a{l}"] + NJ] = b1p[0].reshape(NJ, 128).T
        par[:, _PC[f"b1g{l}"]:_PC[f"b1g{l}"] + NJ] = b1p[1].reshape(NJ, 128).T
    return par


class Ph:
    def __init__(self, k):
        self.k = k
        self.fns = {e: [] for e in ENGINES}

    def flush(self):
        if not any(self.fns.values()):
            return
        with self.k.nc.Block() as blk:
            for e in ENGINES:
                if self.fns[e]:
                    def body(h, fs=tuple(self.fns[e])):
                        for f in fs:
                            f(h)
                    getattr(blk, e)(body)
        self.fns = {e: [] for e in ENGINES}


class Kern:
    def __init__(self, nc):
        self.nc = nc
        self.ctr = {}
        self.sems = {}
        self.inc = {}
        self.lastw = {}
        self.ph = Ph(self)

    def mksem(self, name, inc):
        h = self.nc.semaphore(name).__enter__()
        self.sems[name] = h
        self.ctr[name] = 0
        self.inc[name] = inc

    def emit(self, eng, fn, tick=None, waits=None):
        wl = []
        if waits:
            for q, v in waits.items():
                if not v or v <= 0:
                    continue
                assert v <= self.ctr[q], f"wait on future count {q}={v}>{self.ctr[q]}"
                key = (eng, q)
                if self.lastw.get(key, -1) >= v:
                    continue
                self.lastw[key] = v
                wl.append((self.sems[q], v))
        tk = None
        if tick is not None:
            self.ctr[tick] += self.inc[tick]
            tk = (self.sems[tick], self.inc[tick])

        def closure(h, fn=fn, wl=tuple(wl), tk=tk):
            for s, v in wl:
                h.wait_ge(s, v)
            inst = fn(h)
            if tk is not None:
                inst.then_inc(tk[0], tk[1])
        self.ph.fns[eng].append(closure)
        return self.ctr[tick] if tick else None


def build(taps=()):
    nc = bass.Bass("TRN2", target_bir_lowering=False)
    k = Kern(nc)
    dsems = ["dW", "dT", "dWo0", "dWo1", "dWo2", "dW10", "dW11", "dCv0", "dCv1",
             "dCs0", "dCs1", "dXl0", "dXl1", "dXs0", "dXs1", "dOut", "dTap",
             "dXr0", "dXr1", "dXb0", "dXb1", "dCw"]
    for q in dsems:
        k.mksem(q, 16)
    for q in ["cP", "cV", "cA", "cG"]:
        k.mksem(q, 1)

    def din(name, shape, dtype=F32):
        return nc.dram_tensor(name, list(shape), dtype, kind="ExternalInput").ap()

    xT = din("xT", (D, T))
    wqT = din("wqT", (D, D)); wkT = din("wkT", (D, D)); wvT = din("wvT", (D, D)); woTg = din("woTg", (D, D))
    wqkv_in = din("wqkv_in", (NL, D, 3 * D))
    wo_in = din("wo_in", (NL, D, D))
    w1_in = din("w1_in", (NL, D, 2 * DI))
    w2_in = din("w2_in", (NL, DI, D))
    par_in = din("par_in", (128, PARW))
    c_ident = din("c_ident", (128, 128)); c_swapT = din("c_swapT", (128, 128))
    c_onesf = din("c_onesf", (128, 128)); c_onesh = din("c_onesh", (128, 128))
    c_mask = din("c_mask", (128, 384)); c_dmask = din("c_dmask", (128, 128))
    ctabs = [din("c_ctq", (128, T)), din("c_stq", (128, T)), din("c_ctk", (128, T)), din("c_stk", (128, T))]

    outT = nc.dram_tensor("outT", [NCH, 128, T], F32, kind="ExternalOutput").ap()
    tap_t = {}
    for name, shape, *dt in taps:
        tdt = BF16 if (dt and dt[0] == "bf16") else F32
        tap_t[name] = nc.dram_tensor("tap_" + name, list(shape), tdt, kind="ExternalOutput").ap()

    xmas = nc.dram_tensor("xmas", [NCH, 128, T], F32, kind="Internal").ap()
    xbfh = nc.dram_tensor("xbfh", [NCH, 128, T + 1], BF16, kind="Internal").ap()
    x1m = nc.dram_tensor("x1m", [NCH, 128, T], F32, kind="Internal").ap()
    qkvc = nc.dram_tensor("qkvc", [NL, D, 3 * D], BF16, kind="Internal").ap()
    woc = nc.dram_tensor("woc", [NL, D, D], BF16, kind="Internal").ap()
    wotc = nc.dram_tensor("wotc", [D, D], BF16, kind="Internal").ap()
    gwc = nc.dram_tensor("gwc", [3, D, D], BF16, kind="Internal").ap()
    w1c = nc.dram_tensor("w1c", [NL, D, 2 * DI], BF16, kind="Internal").ap()
    w2c = nc.dram_tensor("w2c", [NL, NJ * 128, D], BF16, kind="Internal").ap()

    def S(name, shape, dtype):
        return nc.alloc_sbuf_tensor(name, list(shape), dtype)

    warena = S("warena", (128, 24576), BF16)
    wor = S("wor", (128, 3, NCH, 128), BF16)
    w1r = S("w1r", (128, 4, 2, NCH, 128), BF16)
    tabs = S("tabs", (128, 4, BLK), F32)
    xfio = S("xfio", (128, 4, BLK), F32)
    qp = S("qp", (128, NP, BLK), BF16)
    ovl = S("ovl", (128, 12288), BF16)
    aout = S("aout", (128, NCH, BLK), BF16)
    chs = S("chs", (128, 6, BLK), BF16)
    rns = S("rns", (128, 2, BLK), BF16)
    fsc = S("fsc", (128, 4, BLK), F32)
    pbf = S("pbf", (128, 5, 384), BF16)
    ga = S("ga", (128, 2, BLK), F32)
    gb = S("gb", (128, 2, BLK), F32)
    gc = S("gc", (128, 2, BLK), F32)
    xrd = S("xrd", (128, 2, NCH, BLK + 1), BF16)
    sstg = S("sstg", (128, 2, BLK), BF16)
    xnb = S("xnb", (128, NCH, BLK), BF16)
    ssb = S("ssb", (128, NP, 64), F32)
    sbfA = S("sbfA", (128, NP, 64), BF16)
    sbfB = S("sbfB", (128, NP, 64), BF16)
    cst = S("cst", (128, 1056), F32)
    cbt = S("cbt", (128, 1024), BF16)
    par = S("par", (128, PARW), F32)

    rem = nc.sbuf_bytes_remaining
    assert rem >= 0, f"SBUF over budget by {-rem} bytes"

    kring = ovl.ap()[:, 0:6144].rearrange("p (a b c) -> p a b c", a=NP, b=6)
    vtok = ovl.ap()[:, 6144:12288].rearrange("p (a b) -> p a b", a=6)
    usb = ovl.ap()[:, 0:NJ * BLK].rearrange("p (a b) -> p a b", a=NJ)
    kview = ovl.ap()[:, 0:4096].rearrange("p (a b) -> p a b", a=NP)
    vview = ovl.ap()[:, 4096:8192]
    t1bv = ovl.ap()[:, 8192:12288].rearrange("p (a b) -> p a b", a=NCH)

    psA = nc.alloc_psum_tensor("psA", [128, 2048], F32)
    psB = nc.alloc_psum_tensor("psB", [128, 1024], F32)
    psC = nc.alloc_psum_tensor("psC", [128, 2048], BF16)

    CO = dict(ident=0, swapT=128, onesf=256, onesh=384, mask=512, dmask=896)
    CB = dict(identb=0, swapTb=128, onesfb=256, oneshb=384, epsrow=512)
    GW = {"wq": 0, "wk": 8192, "wv": 16384}

    def dma(queue, sem, out, in_, waits=None):
        return k.emit(queue, lambda h, o=out, i=in_: h.dma_start(out=o, in_=i), tick=sem, waits=waits)

    def op(eng, fn, waits=None):
        return k.emit(eng, fn, tick=EX[eng], waits=waits)

    def mm(out_ap, lhsT_ap, rhs_ap, start, stop, waits=None):
        return op("tensor", lambda h, o=out_ap, l=lhsT_ap, r=rhs_ap, s1=start, s2=stop:
                  h.matmul(o, l, r, start=s1, stop=s2), waits=waits)

    def tr(out_ap, in_ap, identity, waits=None):
        return op("tensor", lambda h, o=out_ap, i=in_ap, d=identity: h.transpose(o, i, d), waits=waits)

    def allnow():
        return dict(k.ctr)

    def parc(name, off=0, rows=128):
        c0 = _PC[name] + off
        return par[0:rows, c0:c0 + 1]

    def xsl(slot, c, shifted):
        off = (0 if (shifted and c >= 4) else 1)
        return xrd[:, slot, c, off:off + BLK]

    xr_gate = {0: {}, 1: {}}

    def xrd_load(b):
        slot = b % 2
        ld = dma("scalar", f"dXr{slot}", xrd[:, slot, :, :],
                 xbfh[:, :, b * BLK: b * BLK + BLK + 1].transpose([1, 0, 2]),
                 waits=xr_gate[slot])
        return slot, ld

    def xrd_mark(slot):
        xr_gate[slot] = {"cP": k.ctr["cP"], "cV": k.ctr["cV"], "cG": k.ctr["cG"]}

    def tap(name, src_ap, idx=None, waits=None):
        if name in tap_t:
            dst = tap_t[name] if idx is None else tap_t[name][idx]
            dma("scalar", "dTap", dst, src_ap, waits=waits if waits is not None else allnow())

    onesb_row = cbt[0:1, CB["onesfb"]:CB["onesfb"] + 128]
    onesf_row = cst[0:1, CO["onesf"]:CO["onesf"] + 128]
    eps_row = cbt[0:1, CB["epsrow"]:CB["epsrow"] + BLK]
    identb = cbt[:, CB["identb"]:CB["identb"] + 128]
    swapTb = cbt[:, CB["swapTb"]:CB["swapTb"] + 128]
    onesfb = cbt[:, CB["onesfb"]:CB["onesfb"] + 128]
    oneshb = cbt[:, CB["oneshb"]:CB["oneshb"] + 128]

    dma("sync", "dW", cst[:, CO["ident"]:CO["ident"] + 128], c_ident)
    dma("sync", "dW", cst[:, CO["swapT"]:CO["swapT"] + 128], c_swapT)
    dma("sync", "dW", cst[:, CO["onesf"]:CO["onesf"] + 128], c_onesf)
    dma("sync", "dW", cst[:, CO["onesh"]:CO["onesh"] + 128], c_onesh)
    dma("sync", "dW", cst[:, CO["mask"]:CO["mask"] + 384], c_mask)
    dma("sync", "dW", cst[:, CO["dmask"]:CO["dmask"] + 128], c_dmask)
    pld = dma("sync", "dW", par[:, :], par_in)
    w0 = {"dW": pld}
    for nm, dst in [("ident", "identb"), ("swapT", "swapTb"), ("onesf", "onesfb"), ("onesh", "oneshb")]:
        op("gpsimd", lambda h, s=CO[nm], d=CB[dst]: h.tensor_copy(cbt[:, d:d + 128], cst[:, s:s + 128]), waits=w0)
    op("gpsimd", lambda h: h.memset(cbt[0:1, CB["epsrow"]:CB["epsrow"] + BLK], EPS_L2))
    op("gpsimd", lambda h: h.memset(cst[:, 1024:1025], EPS_LN))
    op("gpsimd", lambda h: h.memset(ssb[:, :, :], 0.0))
    op("gpsimd", lambda h: h.memset(sbfA[:, :, :], 0.0))
    op("gpsimd", lambda h: h.memset(sbfB[:, :, :], 0.0))
    zc = op("gpsimd", lambda h: h.memset(sstg[:, 0, 0:16], 0.0))
    k.emit("scalar", lambda h: h.dma_start(out=xbfh[:, :, 0:1].transpose([1, 0, 2]),
                                           in_=sstg[:, 0:1, 0:NCH].transpose([0, 2, 1]),
                                           allow_slow_non_contiguous=True),
           tick="dXb0", waits={"cG": zc})

    def cast_dma(dst, src_ap):
        return k.emit("gpsimd", lambda h, o=dst, i=src_ap: h.dma_start(out=o, in_=i), tick="dCw")

    cast_dma(gwc[0], wqT)
    cast_dma(gwc[1], wkT)
    cast_dma(gwc[2], wvT)
    cast_dma(wotc, woTg)
    glaw_cast = k.ctr["dCw"]
    garena_ld = 0
    for gi, sec in enumerate(["wq", "wk", "wv"]):
        garena_ld = dma("sync", "dW",
                        warena[:, GW[sec]:GW[sec] + 8192].rearrange("p (c n) -> p c n", c=NCH),
                        gwc[gi].rearrange("(c p) n -> p c n", p=128),
                        waits={"dCw": glaw_cast})
    glaw = allnow()

    castq = {"items": [], "done": {}}

    def cast_push(l):
        castq["items"].append((l, lambda l=l: cast_dma(qkvc[l], wqkv_in[l])))
        castq["items"].append((l, lambda l=l: cast_dma(woc[l], wo_in[l])))
        castq["items"].append((l, lambda l=l: cast_dma(w1c[l], w1_in[l])))
        castq["items"].append((l, lambda l=l: cast_dma(w2c[l, 0:DI, :], w2_in[l])))

    def conv_step(nmax):
        for _ in range(max(1, nmax // 7)):
            if not castq["items"]:
                return
            l, f = castq["items"].pop(0)
            f()
            if not castq["items"] or castq["items"][0][0] != l:
                castq["done"][l] = k.ctr["dCw"]

    cast_push(0)

    def ln_stats(srcs, bank, extra_waits=None):
        pe = 0
        ws = dict(extra_waits or {})
        for c, src in enumerate(srcs):
            sq = chs[:, 4 + (c % 2), :]
            pl = op("gpsimd", lambda h, s=src, d=sq: h.tensor_tensor(d, s, s, ALU.mult),
                    waits={**ws, "cP": pe})
            mm(psA[:, bank:bank + BLK], onesfb, src, start=(c == 0), stop=(c == NCH - 1), waits=ws if c == 0 else None)
            pe = mm(psA[:, bank + 512:bank + 512 + BLK], onesfb, sq, start=(c == 0), stop=(c == NCH - 1),
                    waits={"cG": pl})
        return pe

    fsc2_rd = [0]

    def ln_chain(bank, rstd_dst, waits):
        mu2 = fsc[:, 2, 0:BLK]
        a1 = op("scalar", lambda h: h.activation(mu2, psA[:, bank:bank + BLK], AF.Square, scale=RD),
                waits={**waits, "cV": max(waits.get("cV", 0), fsc2_rd[0])})
        v1 = op("vector", lambda h: h.scalar_tensor_tensor(rstd_dst, psA[:, bank + 512:bank + 512 + BLK], RD, mu2,
                                                           ALU.mult, ALU.subtract), waits={"cA": a1})
        fsc2_rd[0] = v1
        a2 = op("scalar", lambda h: h.activation(rstd_dst, rstd_dst, AF.Ln, bias=cst[:, 1024:1025]), waits={"cV": v1})
        return op("scalar", lambda h: h.activation(rstd_dst, rstd_dst, AF.Exp, scale=-0.5), waits={"cA": a2})

    xbst = {0: 0, 1: 0}

    def xb_store(c, t0, src_f32, buf, vmark):
        pl = op("gpsimd", lambda h, s=src_f32, buf=buf: h.tensor_copy(sstg[:, buf, :], s),
                waits={"cV": vmark, f"dXb{buf}": xbst[buf]})
        xbst[buf] = dma("scalar", f"dXb{buf}", xbfh[c, :, 1 + t0:1 + t0 + BLK], sstg[:, buf, :],
                        waits={"cG": pl})
        return pl

    prep_state = {"g1": 0, "last": 0}

    def prep(src_bf, rn_ap, dst_writes, cS, sS, swap_bank, l, isq, waits):
        waits = dict(waits)
        waits["cG"] = max(waits.get("cG", 0), prep_state["g1"])
        waits["cV"] = max(waits.get("cV", 0), prep_state["last"])
        sw = mm(psA[:, swap_bank:swap_bank + BLK], swapTb, src_bf, start=True, stop=True, waits=waits)
        t1 = chs[:, 2, :]
        t2 = chs[:, 3, :]
        sc = ("qs", "qssw") if isq else ("ks", "kssw")
        v1 = op("vector", lambda h, s=src_bf: h.scalar_tensor_tensor(t1, s, parc(f"{sc[0]}{l}"), cS, ALU.mult, ALU.mult),
                waits=waits)
        v2 = op("vector", lambda h, sb=swap_bank: h.scalar_tensor_tensor(
            t2, psA[:, sb:sb + BLK], parc(f"{sc[1]}{l}"), sS, ALU.mult, ALU.mult),
            waits={"cP": sw, "cG": waits["cG"]})
        g1 = op("gpsimd", lambda h: h.tensor_tensor(t1, t1, t2, ALU.add), waits={"cV": v2})
        last = 0
        for dst, c0, n in dst_writes:
            last = op("vector", lambda h, d=dst, c0=c0, n=n: h.tensor_tensor(
                d, t1[:, c0:c0 + n], rn_ap[:, c0:c0 + n], ALU.mult), waits={"cG": g1})
        prep_state["g1"] = g1
        prep_state["last"] = last
        return {"cV": last, "cG": g1}

    wo_ring = {"use": {}, "m": {}}

    def wo_load(cache, oc, slot, conv_w=None):
        w = {"cP": wo_ring["use"].get(slot, 0)}
        if conv_w:
            w.update(conv_w)
        m = dma("sync", f"dWo{slot}", wor[:, slot, :, :],
                cache[:, oc * 128:(oc + 1) * 128].rearrange("(c p) n -> p c n", p=128), waits=w)
        wo_ring["m"][oc] = (f"dWo{slot}", m)
        return m

    def gla_block(b):
        t0 = b * BLK
        cg = 0
        for c in range(NCH):
            sl = fsc[:, c % 2, 0:BLK]
            xld = dma("scalar", f"dXl{c % 2}", sl, xT[c * 128:(c + 1) * 128, t0:t0 + BLK], waits={"cG": cg})
            cg = op("gpsimd", lambda h, c=c, sl=sl: h.tensor_copy(xnb[:, c, :], sl), waits={f"dXl{c % 2}": xld})
        pe = ln_stats([xnb[:, c, :] for c in range(NCH)], 1024, {"cG": cg})
        rstd = fsc[:, 3, 0:BLK]
        a3 = ln_chain(1024, rstd, {"cP": pe})
        t1b = t1bv
        vdone = 0
        for c in range(NCH):
            v1 = op("vector", lambda h, c=c: h.scalar_tensor_tensor(
                ga[:, c % 2, :], psA[:, 1024:1024 + BLK], -RD, xnb[:, c, :], ALU.mult, ALU.add),
                waits={"cA": a3, "cP": pe, "cV": vdone})
            vdone = op("vector", lambda h, c=c: h.tensor_tensor(t1b[:, c, :], ga[:, c % 2, :], rstd, ALU.mult),
                       waits={"cV": v1})
        tap("t1b", ovl[:, 8192:12288].rearrange("p (a b) -> p a b", a=NCH), np.s_[:, :, t0:t0 + BLK])
        evm = {}
        for oc in range(24):
            kind, p = oc // NP, oc % NP
            buf = oc % 2
            pe0 = 0
            for c in range(NCH):
                w = None
                if c == 0:
                    w = {"dW": garena_ld, "cV": max(vdone, evm.get(oc - 2, 0))}
                sec = ("wq", "wk", "wv")[kind]
                pe0 = mm(psB[:, buf * 512:buf * 512 + BLK],
                         warena[:, GW[sec] + c * 1024 + p * 128: GW[sec] + c * 1024 + (p + 1) * 128],
                         t1b[:, c, :], start=(c == 0), stop=(c == NCH - 1), waits=w)
            if kind == 0:
                dst = qp[:, p, :]
            elif kind == 1:
                dst = kview[:, p, 0:BLK]
            else:
                dst = vview[:, p * BLK:(p + 1) * BLK]
            evm[oc] = op("vector", lambda h, d=dst, buf=buf: h.tensor_copy(d, psB[:, buf * 512:buf * 512 + BLK]),
                         waits={"cP": pe0})
        conv_step(16)
        evq = max(evm[oc] for oc in range(0, 8))
        evk = max(evm[oc] for oc in range(8, 16))
        evv = max(evm[oc] for oc in range(16, 24))
        khm = {}
        rn_a = {}
        rn_pe = {}
        for p in range(NP):
            bank = 1024 + (p % 2) * 512
            sq = chs[:, 4 + (p % 2), :]
            pl = op("gpsimd", lambda h, s=kview[:, p, 0:BLK], d=sq: h.tensor_tensor(d, s, s, ALU.mult),
                    waits={"cV": evk, "cP": rn_pe.get(p - 2, 0), "cA": rn_a.get(p - 2, 0)})
            pe1 = mm(psA[:, bank:bank + BLK], oneshb, sq, start=True, stop=False,
                     waits={"cG": pl, "cA": rn_a.get(p - 2, 0)})
            rn_pe[p] = pe1 = mm(psA[:, bank:bank + BLK], onesb_row, eps_row, start=False, stop=True)
            a1 = op("scalar", lambda h, bank=bank: h.activation(fsc[:, 2, 0:BLK], psA[:, bank:bank + BLK], AF.Ln),
                    waits={"cP": pe1, "cV": fsc2_rd[0]})
            a2 = op("scalar", lambda h, p=p: h.activation(rns[:, p % 2, :], fsc[:, 2, 0:BLK], AF.Exp, scale=-0.5),
                    waits={"cA": a1, "cV": khm.get(p - 2, 0)})
            rn_a[p] = a2
            khm[p] = op("vector", lambda h, p=p: h.tensor_tensor(
                kview[:, p, 0:BLK], kview[:, p, 0:BLK], rns[:, p % 2, :], ALU.mult), waits={"cA": a2})
        tap("qp", qp[:, :, :], np.s_[b])
        tap("kh", ovl[:, 0:4096], np.s_[b])
        tap("vv", ovl[:, 4096:8192], np.s_[b])
        lastr = {"chs": 0, "vt_pe": 0, "yev": 0}
        dpipe = {}

        def pipeline(p):
            s = p % 2
            gap, gbp, gcp = ga[:, s, :], gb[:, s, :], gc[:, s, :]
            pe1 = mm(psA[:, 1024:1024 + BLK], oneshb, kview[:, p, 0:BLK], start=True, stop=True,
                     waits={"cV": max(khm[p], lastr["yev"], dpipe.get(p - 2, 0)), "cA": k.ctr["cA"]})
            v1 = op("vector", lambda h, gap=gap: h.tensor_scalar(gap, psA[:, 1024:1024 + BLK], 1.0 / DK, None, ALU.mult),
                    waits={"cP": pe1})
            s1 = op("scalar", lambda h, p=p, gap=gap, gbp=gbp: h.activation(
                gbp, gap, AF.Sigmoid, bias=parc(f"bg{p}"), scale=parc(f"wg{p}")), waits={"cV": v1, **w0})
            s2 = op("scalar", lambda h, p=p, gap=gap, gcp=gcp: h.activation(
                gcp, gap, AF.Sigmoid, bias=parc(f"bl{p}"), scale=parc(f"wl{p}")), waits={"cV": v1})
            v2 = op("vector", lambda h, gap=gap, gbp=gbp, gcp=gcp: h.tensor_tensor(gap, gcp, gbp, ALU.mult),
                    waits={"cA": s2})
            v2 = op("vector", lambda h, gap=gap, gbp=gbp: h.tensor_tensor(gap, gbp, gap, ALU.subtract),
                    waits={"cV": v2, "cA": s1})
            v3 = op("vector", lambda h, gap=gap, gbp=gbp: h.tensor_scalar(gbp, gap, -1.0, 1.0, ALU.mult, ALU.add),
                    waits={"cV": v2})
            v2 = op("vector", lambda h, gap=gap: h.tensor_scalar_max(gap, gap, EPS_GLA), waits={"cV": v2})
            v3 = op("vector", lambda h, gbp=gbp: h.tensor_scalar_max(gbp, gbp, EPS_GLA), waits={"cV": v3})
            s3 = op("scalar", lambda h, gap=gap, gcp=gcp: h.activation(gcp, gap, AF.Ln), waits={"cV": v2})
            s4 = op("scalar", lambda h, gap=gap, gbp=gbp: h.activation(gap, gbp, AF.Ln), waits={"cV": v3, "cA": s3})
            sc = 0
            for tc in range(NTC):
                sc = op("vector", lambda h, gbp=gbp, gcp=gcp, tc=tc: h.tensor_tensor_scan(
                    gbp[:, tc * 128:(tc + 1) * 128], gcp[:, tc * 128:(tc + 1) * 128],
                    gcp[:, tc * 128:(tc + 1) * 128], 0.0, ALU.add, ALU.bypass),
                    waits={"cA": s4})
            v4 = op("vector", lambda h, gap=gap, gbp=gbp, gcp=gcp: h.tensor_tensor(gcp, gap, gbp, ALU.subtract),
                    waits={"cV": sc, "cA": s4})
            dpipe[p] = v4

        pipeline(0)
        for p in range(NP):
            if p + 1 < NP:
                pipeline(p + 1)
            s = p % 2
            gbp, gcp = gb[:, s, :], gc[:, s, :]
            expL = chs[:, 0, :]
            qq = chs[:, 1, :]
            e1 = op("scalar", lambda h, gbp=gbp: h.activation(expL, gbp, AF.Exp),
                    waits={"cV": max(dpipe[p], lastr["chs"])})
            q2 = op("vector", lambda h, p=p: h.tensor_tensor(qq, qp[:, p, :], expL, ALU.mult),
                    waits={"cA": e1, "cV": evq, "cP": k.ctr["cP"]})
            for tc in range(NTC):
                c0 = tc * 128
                eL = fsc[:, 1, tc:tc + 1]
                eA = op("scalar", lambda h, gbp=gbp, eL=eL, c0=c0: h.activation(
                    eL, gbp[:, c0 + 127:c0 + 128], AF.Exp), waits={"cV": max(dpipe[p], lastr["chs"])})
                fv = chs[:, 2, 0:128]
                f1 = op("scalar", lambda h, gbp=gbp, gcp=gcp, c0=c0: h.activation(
                    chs[:, 2, 0:128], gcp[:, c0:c0 + 128], AF.Exp, bias=gbp[:, c0 + 127:c0 + 128]),
                    waits={"cV": max(dpipe[p], lastr["chs"])})
                f2 = op("vector", lambda h, p=p, c0=c0: h.tensor_tensor(
                    chs[:, 3, 0:128], vview[:, p * BLK + c0: p * BLK + c0 + 128], fv, ALU.mult),
                    waits={"cA": f1, "cV": evv, "cP": lastr["vt_pe"]})
                t1m = tr(psC[:, 0:128], kview[:, p, c0:c0 + 128], identb,
                         waits={"cV": max(khm[p], lastr["chs"])})
                ke = op("vector", lambda h: h.tensor_copy(chs[:, 4, 0:128], psC[:, 0:128]),
                        waits={"cP": t1m})
                t2m = tr(psC[:, 1024:1152], chs[:, 3, 0:128], identb, waits={"cV": f2})
                ve = op("vector", lambda h: h.tensor_copy(chs[:, 5, 0:128], psC[:, 1024:1152]),
                        waits={"cP": t2m})
                amk = {}
                for hh in range(2):
                    r0 = hh * 64
                    araw_bank = 0 if hh == 0 else 512
                    d_bank = 256 if hh == 0 else 768
                    ar = mm(psB[:, araw_bank:araw_bank + 128],
                            vview[r0:r0 + 64, p * BLK + c0: p * BLK + c0 + 128],
                            qp[r0:r0 + 64, p, c0:c0 + 128], start=True, stop=True,
                            waits={"cV": max(evv, evq, dpipe[p])})
                    ones_r = cst[r0:r0 + 1, CO["onesf"]:CO["onesf"] + 128]
                    d1 = mm(psA[:, d_bank:d_bank + 128], gcp[r0:r0 + 1, c0:c0 + 128], ones_r,
                            start=True, stop=True, waits={"cV": dpipe[p]})
                    d1 = mm(psA[:, d_bank:d_bank + 128], ones_r, gbp[r0:r0 + 1, c0:c0 + 128],
                            start=False, stop=True)
                    md = op("vector", lambda h, d_bank=d_bank: h.tensor_tensor(
                        fsc[:, 2, 0:128], psA[:, d_bank:d_bank + 128], cst[:, CO["dmask"]:CO["dmask"] + 128], ALU.add),
                        waits={"cP": d1, "cA": k.ctr["cA"]})
                    de = op("scalar", lambda h: h.activation(pbf[:, 2, 0:128], fsc[:, 2, 0:128], AF.Exp),
                            waits={"cV": md})
                    amk[hh] = op("vector", lambda h, hh=hh, araw_bank=araw_bank: h.tensor_tensor(
                        pbf[:, 2, 128 * (hh + 1):128 * (hh + 2)], psB[:, araw_bank:araw_bank + 128],
                        pbf[:, 2, 0:128], ALU.mult),
                        waits={"cA": de, "cP": ar})
                if b == 0 and p == 0 and tc == 0:
                    tap("dbg_at", pbf[:, 2, :])
                    tap("dbg_L", gb[:, 0, :])
                    tap("dbg_lnad", gc[:, 0, :])
                    tap("dbg_ktok", chs[:, 4, 0:128])
                    tap("dbg_qq", chs[:, 1, :])
                ylast = mm(psA[0:64, 1024:1152], chs[:, 4, 0:64], pbf[:, 2, 128:256], start=True, stop=False,
                           waits={"cV": max(ke, amk[0], q2), "cG": k.ctr["cG"]})
                mm(psA[0:64, 1024:1152], sbfA[:, p, :], qq[:, c0:c0 + 128], start=False, stop=False)
                mm(psA[64:128, 1024:1152], chs[:, 4, 64:128], pbf[:, 2, 256:384], start=True, stop=False,
                   waits={"cV": amk[1]})
                ylast = mm(psA[64:128, 1024:1152], sbfB[:, p, :], qq[:, c0:c0 + 128], start=False, stop=True)
                yev = op("vector", lambda h, p=p, c0=c0: h.tensor_copy(aout[:, p, c0:c0 + 128], psA[:, 1024:1152]),
                         waits={"cP": ylast})
                lastr["yev"] = yev
                wA = mm(psA[0:64, 1536:1600], chs[:, 5, 0:64], chs[:, 4, 0:64], start=True, stop=True,
                        waits={"cV": max(ve, ke), "cG": k.ctr["cG"]})
                wB = mm(psA[64:128, 1536:1600], chs[:, 5, 64:128], chs[:, 4, 64:128], start=True, stop=True)
                lastr["vt_pe"] = wB
                su = op("vector", lambda h, p=p, eL=eL: h.scalar_tensor_tensor(
                    ssb[:, p, :], ssb[:, p, :], eL, psA[:, 1536:1600], ALU.mult, ALU.add),
                    waits={"cP": wB, "cA": eA, "cG": k.ctr["cG"]})
                op("gpsimd", lambda h, p=p: h.tensor_copy(sbfA[0:64, p, :], ssb[0:64, p, :]),
                   waits={"cV": su, "cP": ylast})
                op("gpsimd", lambda h, p=p: h.tensor_copy(sbfB[64:128, p, :], ssb[64:128, p, :]),
                   waits={"cV": su})
                lastr["chs"] = max(yev, su)
        tap("glay", aout[:, :, :], np.s_[:, :, t0:t0 + BLK])
        rb = xnb
        evo = 0
        cvw = {"dCw": glaw_cast}
        wo_load(wotc, 0, 0, cvw)
        wo_load(wotc, 1, 1, cvw)
        for oc in range(NCH):
            slot = oc % 3
            buf = oc % 2
            pe0 = 0
            for c in range(NCH):
                w = None
                if c == 0:
                    w = {"cV": max(lastr["yev"], evo), "cG": k.ctr["cG"], wo_ring["m"][oc][0]: wo_ring["m"][oc][1]}
                pe0 = mm(psB[:, buf * 512:buf * 512 + BLK], wor[:, slot, c, :], aout[:, c, :],
                         start=(c == 0), stop=(c == NCH - 1), waits=w)
            wo_ring["use"][slot] = pe0
            if oc + 2 < NCH:
                wo_load(wotc, oc + 2, (oc + 2) % 3, cvw)
            evo = op("vector", lambda h, oc=oc, buf=buf: h.tensor_copy(rb[:, oc, :], psB[:, buf * 512:buf * 512 + BLK]),
                     waits={"cP": pe0})
        pe = ln_stats([rb[:, c, :] for c in range(NCH)], 1024, {"cV": evo})
        a3 = ln_chain(1024, fsc[:, 3, 0:BLK], {"cP": pe})
        x1st = {0: 0, 1: 0}
        x1cg = 0
        for c in range(NCH):
            sl = fsc[:, c % 2, 0:BLK]
            ld = dma("scalar", f"dXl{c % 2}", sl, xT[c * 128:(c + 1) * 128, t0:t0 + BLK],
                     waits={"cV": k.ctr["cV"]})
            s2_ = c % 2
            v1 = op("vector", lambda h, c=c, s2_=s2_: h.scalar_tensor_tensor(
                ga[:, s2_, :], psA[:, 1024:1024 + BLK], -RD, rb[:, c, :], ALU.mult, ALU.add),
                waits={"cA": a3, "cP": pe, "cG": x1cg, f"dXs{c % 2}": x1st[c % 2]})
            v2 = op("vector", lambda h, s2_=s2_: h.tensor_tensor(ga[:, s2_, :], ga[:, s2_, :], fsc[:, 3, 0:BLK], ALU.mult),
                    waits={"cV": v1})
            v3 = op("vector", lambda h, sl=sl, s2_=s2_: h.tensor_tensor(gb[:, s2_, :], ga[:, s2_, :], sl, ALU.add),
                    waits={"cV": v2, f"dXl{c % 2}": ld})
            x1st[c % 2] = dma("scalar", f"dXs{c % 2}", x1m[c, :, t0:t0 + BLK], gb[:, s2_, :], waits={"cV": v3})
            x1cg = op("gpsimd", lambda h, c=c, s2_=s2_: h.tensor_copy(t1bv[:, c, :], gb[:, s2_, :]), waits={"cV": v3})
        x1b = t1bv
        tap("x1", ovl[:, 8192:12288].rearrange("p (a b) -> p a b", a=NCH), np.s_[:, :, t0:t0 + BLK])
        pe = ln_stats([x1b[:, c, :] for c in range(NCH)], 1024, {"cG": x1cg})
        a3 = ln_chain(1024, fsc[:, 3, 0:BLK], {"cP": pe})
        xst = {0: 0, 1: 0}
        t2cg = 0
        for c in range(NCH):
            s2_ = c % 2
            v1 = op("vector", lambda h, c=c, s2_=s2_: h.scalar_tensor_tensor(
                ga[:, s2_, :], psA[:, 1024:1024 + BLK], -RD, x1b[:, c, :], ALU.mult, ALU.add),
                waits={"cA": a3, "cP": pe, f"dXs{c % 2}": xst[c % 2], "cG": t2cg})
            v2 = op("vector", lambda h, s2_=s2_: h.tensor_tensor(ga[:, s2_, :], ga[:, s2_, :], fsc[:, 3, 0:BLK], ALU.mult),
                    waits={"cV": v1})
            xst[c % 2] = dma("scalar", f"dXs{c % 2}", xmas[c, :, t0:t0 + BLK], ga[:, s2_, :], waits={"cV": v2})
            t2cg = xb_store(c, t0, ga[:, s2_, :], c % 2, v2)
        conv_step(12)
        k.ph.flush()

    lw = {"wqkv": None, "w2": None}

    def attn_phase(l):
        cw = castq["done"].get(l)
        assert cw is not None, f"casts for layer {l} not done in time"
        if l > 0:
            for c in range(NCH):
                tap(f"ffn{l - 1}", xmas[c], idx=np.s_[c])
        lw["wqkv"] = dma("sync", "dW", warena.ap().rearrange("p (c n) -> p c n", c=NCH),
                         qkvc[l].rearrange("(c p) n -> p c n", p=128),
                         waits={"dCw": cw, "cP": k.ctr["cP"]})
        ldm = {0: xrd_load(0)[1]}
        for b in range(NBLK):
            attn_block(l, b, ldm)

    def attn_block(l, b, ldm):
        t0 = b * BLK
        tld = 0
        for i in range(4):
            tld = dma("sync", "dT", tabs[:, i, :], ctabs[i][:, t0:t0 + BLK],
                      waits={"cV": k.ctr["cV"], "cG": k.ctr["cG"]} if i == 0 else None)
        evac = {}
        prep_marks = {}
        rn_a = {}
        rnuse = {0: 0, 1: 0}
        order = list(range(8, 16)) + list(range(16, 24)) + list(range(0, 8))
        for idx, oc in enumerate(order):
            kind, p = oc // NP, oc % NP
            buf = idx % 2
            pe0 = 0
            for c in range(NCH):
                w = None
                if c == 0:
                    pv_ = evac.get(idx - 2, ("cV", 0))
                    w = {"dW": lw["wqkv"], pv_[0]: pv_[1], f"dXr{b % 2}": ldm[b]}
                pe0 = mm(psB[:, buf * 512:buf * 512 + BLK],
                         warena[:, c * 3072 + oc * 128: c * 3072 + (oc + 1) * 128],
                         xsl(b % 2, c, True), start=(c == 0), stop=(c == NCH - 1), waits=w)
            dst = chs[:, buf, :]
            pmw = prep_marks.get(idx - 2, {})
            evw = {"cP": pe0, "cG": pmw.get("cG", 0), "cV": pmw.get("cV", 0), "cA": pmw.get("cA", 0)}
            if idx % 2 == 0:
                ev = ("cV", op("vector", lambda h, d=dst, buf=buf: h.tensor_copy(d, psB[:, buf * 512:buf * 512 + BLK]),
                               waits=evw))
            else:
                ev = ("cA", op("scalar", lambda h, d=dst, buf=buf: h.activation(d, psB[:, buf * 512:buf * 512 + BLK], AF.Copy),
                               waits=evw))
            evac[idx] = ev
            if kind == 2:
                tm = tprev = 0
                tmbank = {0: 0, 1: 0}
                for tc in range(NTC):
                    slot = (NTC * b + tc) % 6
                    _w = {"cV": tmbank[tc % 2]}
                    _w[ev[0]] = max(_w.get(ev[0], 0), ev[1])
                    t1m = tr(psC[:, (tc % 2) * 1024:(tc % 2) * 1024 + 128], dst[:, tc * 128:(tc + 1) * 128], identb,
                             waits=_w)
                    tm = op("vector", lambda h, slot=slot, p=p, tc=tc: h.tensor_copy(
                        vtok[:, slot, p * 128:(p + 1) * 128], psC[:, (tc % 2) * 1024:(tc % 2) * 1024 + 128]),
                        waits={"cP": t1m})
                    tmbank[tc % 2] = tm
                    tprev = t1m
                prep_marks[idx] = {"cV": tm, "cG": 0, "cA": 0, "cP": tprev}
            else:
                bank = 1024 + (p % 2) * 512
                sq = chs[:, 4 + (p % 2), :]
                _w = {"cA": rnuse[p % 2]}
                _w[ev[0]] = max(_w.get(ev[0], 0), ev[1])
                pl = op("gpsimd", lambda h, d=dst, sq=sq: h.tensor_tensor(sq, d, d, ALU.mult), waits=_w)
                pe1 = mm(psA[:, bank:bank + BLK], oneshb, sq, start=True, stop=False,
                         waits={"cG": pl, "cA": rnuse[p % 2]})
                pe1 = mm(psA[:, bank:bank + BLK], onesb_row, eps_row, start=False, stop=True)
                a1 = op("scalar", lambda h, bank=bank: h.activation(fsc[:, 2, 0:BLK], psA[:, bank:bank + BLK], AF.Ln),
                        waits={"cP": pe1, "cV": max(prep_state["last"], fsc2_rd[0])})
                rnuse[p % 2] = a1
                a2 = op("scalar", lambda h, p=p: h.activation(rns[:, p % 2, :], fsc[:, 2, 0:BLK], AF.Exp, scale=-0.5),
                        waits={"cA": a1, "cV": prep_state["last"]})
                rn_a[(kind, p)] = a2
                _w = {"cA": a2, "dT": tld, "dW": pld}
                _w[ev[0]] = max(_w.get(ev[0], 0), ev[1])
                if kind == 1:
                    dsts = [(kring[:, p, (NTC * b + tc) % 6, :], tc * 128, 128) for tc in range(NTC)]
                    pm = prep(dst, rns[:, p % 2, :], dsts, tabs[:, 2, :], tabs[:, 3, :],
                              swap_bank=(p % 2) * 512, l=l, isq=False, waits=_w)
                else:
                    pm = prep(dst, rns[:, p % 2, :], [(qp[:, p, :], 0, BLK)], tabs[:, 0, :], tabs[:, 1, :],
                              swap_bank=(p % 2) * 512, l=l, isq=True, waits=_w)
                pm["cA"] = a2
                prep_marks[idx] = pm
        qkv_done = k.ctr["cP"]
        xrd_mark(b % 2)
        if b + 1 < NBLK:
            ldm[b + 1] = xrd_load(b + 1)[1]
        conv_step(14)
        for _oc in range(3):
            wo_load(woc[l], _oc, _oc)
        prep_all = {"cV": k.ctr["cV"], "cG": k.ctr["cG"], "cA": k.ctr["cA"]}
        pvev = {}
        scm = {}
        pvpe = {}
        last_a1 = 0
        last_tpe = 0
        stq_state = {}

        def iters():
            for qt in range(NTC):
                g = NTC * b + qt
                kcs = [c for c in (g - 2, g - 1, g) if c >= 0]
                for p in range(NP):
                    yield qt, p, kcs

        def stageA(qt, p, kcs, pi):
            nonlocal last_a1
            nkc = len(kcs)
            nk = nkc * 128
            sb0 = (pi % 2) * 1024
            sb1 = (pi % 2) * 1024 + 512
            pslot = (pi % 2) * 2
            pe0 = 0
            for i, kc in enumerate(kcs):
                slot = kc % 6
                w = {**prep_all, "cA": max(prep_all["cA"], scm.get(pi - 1, 0))} if i == 0 else None
                mm(psA[:, sb0 + i * 128: sb0 + (i + 1) * 128], qp[0:64, p, qt * 128:(qt + 1) * 128],
                   kring[0:64, p, slot, :], start=True, stop=True, waits=w)
                pe0 = mm(psA[:, sb1 + i * 128: sb1 + (i + 1) * 128], qp[64:128, p, qt * 128:(qt + 1) * 128],
                         kring[64:128, p, slot, :], start=True, stop=True)
            res = []
            for hh, sb in ((0, sb0), (1, sb1)):
                mrow = cst[:, CO["mask"] + 384 - nk: CO["mask"] + 384]
                v1 = op("vector", lambda h, sb=sb, nk=nk, mrow=mrow, hh=hh: h.tensor_tensor(
                    fsc[:, hh, 0:nk], psA[:, sb:sb + nk], mrow, ALU.add),
                    waits={"cP": pe0, "cA": last_a1})
                v2 = op("vector", lambda h, nk=nk, hh=hh: h.reduce_max(
                    fsc[:, 2, 4 * hh:4 * hh + 1], fsc[:, hh, 0:nk], mybir.AxisListType.X, negate=True),
                    waits={"cV": v1})
                a1 = op("scalar", lambda h, nk=nk, hh=hh, pslot=pslot: h.activation(
                    pbf[:, pslot + hh, 0:nk], fsc[:, hh, 0:nk], AF.Exp, bias=fsc[:, 2, 4 * hh:4 * hh + 1],
                    accum_out=fsc[:, 2, 4 * hh + 1:4 * hh + 2]),
                    waits={"cV": v2, "cP": last_tpe})
                last_a1 = a1
                v3 = op("vector", lambda h, hh=hh: h.reciprocal(fsc[:, 2, 4 * hh + 2:4 * hh + 3],
                                                                fsc[:, 2, 4 * hh + 1:4 * hh + 2]), waits={"cA": a1})
                v4 = op("vector", lambda h, nk=nk, hh=hh, pslot=pslot: h.tensor_scalar_mul(
                    pbf[:, pslot + hh, 0:nk], pbf[:, pslot + hh, 0:nk], fsc[:, 2, 4 * hh + 2:4 * hh + 3]),
                    waits={"cV": v3})
                scm[pi] = max(scm.get(pi, 0), a1)
                res.append(v4)
            return res

        def stageB(qt, p, kcs, pi, v4s):
            nonlocal last_tpe
            nkc = len(kcs)
            nk = nkc * 128
            pslot = (pi % 2) * 2
            for hh in range(2):
                tpe = 0
                for i in range(nkc):
                    tpe = tr(psC[:, (pi % 2) * 1024 + i * 128:(pi % 2) * 1024 + (i + 1) * 128],
                             pbf[:, pslot + hh, i * 128:(i + 1) * 128], identb, waits={"cV": v4s[hh]})
                last_tpe = tpe
                v5 = op("vector", lambda h, pi=pi, nk=nk: h.tensor_copy(
                    pbf[:, 4, 0:nk], psC[:, (pi % 2) * 1024:(pi % 2) * 1024 + nk]), waits={"cP": tpe})
                pvbank = 512 * ((pi // 4) % 2)
                col = (pi % 4) * 128
                for i, kc in enumerate(kcs):
                    slot = kc % 6
                    w = {"cV": max(v5, pvev.get((pi // 4) - 2, 0))}
                    pvpe[pi] = mm(psB[hh * 64:(hh + 1) * 64, pvbank + col: pvbank + col + 128],
                                  vtok[:, slot, p * 128 + hh * 64: p * 128 + hh * 64 + 64],
                                  pbf[:, 4, i * 128:(i + 1) * 128],
                                  start=(i == 0), stop=(i == nkc - 1), waits=w)
            if pi % 4 == 3:
                grp = pi // 4
                pvev[grp] = op("vector", lambda h, qt=qt, p=p, pvbank=512 * (grp % 2): h.tensor_copy(
                    aout[:, p - 3:p + 1, qt * 128:(qt + 1) * 128], psB[:, pvbank:pvbank + 512]),
                    waits={"cP": pvpe[pi]})

        prev = None
        for pi, (qt, p, kcs) in enumerate(iters()):
            v4s = stageA(qt, p, kcs, pi)
            if prev is not None:
                stageB(*prev)
            prev = (qt, p, kcs, pi, v4s)
        stageB(*prev)
        att_done = {"cV": k.ctr["cV"]}
        stm = {}
        stq = {0: k.ctr["dXs0"], 1: k.ctr["dXs1"]}
        for oc in range(NCH):
            slot = oc % 3
            buf = oc % 2
            ld = dma("scalar", f"dXl{buf}", xfio[:, buf, :], xmas[oc, :, t0:t0 + BLK],
                     waits={"cV": stm.get(oc - 2, 0), "dXs0": k.ctr["dXs0"], "dXs1": k.ctr["dXs1"]})
            pe0 = 0
            for c in range(NCH):
                w = None
                if c == 0:
                    w = {wo_ring["m"][oc][0]: wo_ring["m"][oc][1], "cV": max(att_done["cV"], stm.get(oc - 2, 0))}
                pe0 = mm(psB[:, buf * 512:buf * 512 + BLK], wor[:, slot, c, :], aout[:, c, :],
                         start=(c == 0), stop=(c == NCH - 1), waits=w)
            wo_ring["use"][slot] = pe0
            if oc + 3 < NCH:
                wo_load(woc[l], oc + 3, (oc + 3) % 3)
            v1 = op("vector", lambda h, buf=buf: h.tensor_tensor(
                xfio[:, 2 + buf, :], xfio[:, buf, :], psB[:, buf * 512:buf * 512 + BLK], ALU.add),
                waits={"cP": pe0, f"dXl{buf}": ld, f"dXs{buf}": stq[buf]})
            stm[oc] = v1
            stq[buf] = dma("scalar", f"dXs{buf}", xmas[oc, :, t0:t0 + BLK], xfio[:, 2 + buf, :], waits={"cV": v1})
            xb_store(oc, t0, xfio[:, 2 + buf, :], buf, v1)
        k.ph.flush()

    def ffn_phase(l):
        cw = castq["done"][l]
        for c in range(NCH):
            tap(f"attn{l}", xmas[c], idx=np.s_[c])
        lw["w2"] = dma("sync", "dW", warena[:, 0:NJ * 1024].rearrange("p (j n) -> p j n", j=NJ),
                       w2c[l].rearrange("(j p) n -> p j n", p=128),
                       waits={"dCw": cw, "cP": k.ctr["cP"]})
        ldm = {0: xrd_load(0)[1]}
        for b in range(NBLK):
            ffn_block(l, b, ldm)

    def ffn_block(l, b, ldm):
        t0 = b * BLK
        w1m = {}
        ring_free = {}

        def w1_load(j):
            slot = j % 4
            jn_ = 128 if j < NJ - 1 else JL
            m = 0
            for ag in range(2):
                m = dma("sync", f"dW1{slot % 2}", w1r[:, slot, ag, :, 0:jn_],
                        w1c[l][:, ag * DI + j * 128: ag * DI + j * 128 + jn_].rearrange("(c p) n -> p c n", p=128),
                        waits={"cP": ring_free.get(slot, 0)})
            w1m[j] = (f"dW1{slot % 2}", m)
        for jj in range(4):
            w1_load(jj)
        srcs = [xsl(b % 2, c, True) for c in range(NCH)]
        pe = ln_stats(srcs, 1024, {"cV": k.ctr["cV"], "cG": k.ctr["cG"], "cP": k.ctr["cP"],
                                   f"dXr{b % 2}": ldm[b]})
        rstd = fsc[:, 3, 0:BLK]
        a3 = ln_chain(1024, rstd, {"cP": pe})
        xn_done = 0
        for c in range(NCH):
            v1 = op("vector", lambda h, c=c, s=srcs[c]: h.scalar_tensor_tensor(
                ga[:, c % 2, :], psA[:, 1024:1024 + BLK], -RD, s, ALU.mult, ALU.add),
                waits={"cA": a3, "cP": pe, "cV": xn_done})
            v2 = op("vector", lambda h, c=c: h.tensor_tensor(ga[:, c % 2, :], ga[:, c % 2, :], rstd, ALU.mult),
                    waits={"cV": v1})
            xn_done = op("vector", lambda h, c=c: h.tensor_scalar(
                xnb[:, c, :], ga[:, c % 2, :], parc(f"lng{l}", c), parc(f"lnb{l}", c), ALU.mult, ALU.add),
                waits={"cV": v2, **w0})
        xrd_mark(b % 2)
        if b + 1 < NBLK:
            ldm[b + 1] = xrd_load(b + 1)[1]
        conv_step(7)
        um = {}
        for j in range(NJ):
            jn = 128 if j < NJ - 1 else JL
            slot = j % 4
            ab = (j % 2) * 1024
            gbk = (j % 2) * 1024 + 512
            pe0 = 0
            for c in range(NCH):
                w = None
                if c == 0:
                    w = {w1m[j][0]: w1m[j][1], "cV": xn_done, "cG": um.get(j - 2, 0)}
                mm(psA[0:jn, ab:ab + BLK], w1r[:, slot, 0, c, 0:jn], xnb[:, c, :],
                   start=(c == 0), stop=(c == NCH - 1), waits=w)
                pe0 = mm(psA[0:jn, gbk:gbk + BLK], w1r[:, slot, 1, c, 0:jn], xnb[:, c, :],
                         start=(c == 0), stop=(c == NCH - 1))
            ring_free[slot] = pe0
            if j + 4 < NJ:
                w1_load(j + 4)
            ga1 = op("scalar", lambda h, j=j, jn=jn, gbk=gbk: h.activation(
                chs[0:jn, 0, :], psA[0:jn, gbk:gbk + BLK], AF.Gelu, bias=parc(f"b1g{l}", j, jn)),
                waits={"cP": pe0, "cG": um.get(j - 1, 0)})
            v1 = op("vector", lambda h, j=j, jn=jn, ab=ab: h.tensor_scalar(
                chs[0:jn, 1, :], psA[0:jn, ab:ab + BLK], parc(f"b1a{l}", j, jn), None, ALU.add),
                waits={"cP": pe0, "cG": um.get(j - 1, 0)})
            um[j] = op("gpsimd", lambda h, j=j, jn=jn: h.tensor_tensor(
                usb[0:jn, j, :], chs[0:jn, 1, :], chs[0:jn, 0, :], ALU.mult),
                waits={"cA": ga1, "cV": v1})
        conv_step(7)
        u_all = {"cG": k.ctr["cG"]}
        stm = {}
        stq = {0: k.ctr["dXs0"], 1: k.ctr["dXs1"]}
        for oc in range(NCH):
            buf = oc % 2
            ld = dma("scalar", f"dXl{buf}", xfio[:, buf, :], xmas[oc, :, t0:t0 + BLK],
                     waits={"cV": stm.get(oc - 2, 0), "dXs0": k.ctr["dXs0"], "dXs1": k.ctr["dXs1"]})
            pe0 = 0
            for j in range(NJ):
                jn = 128 if j < NJ - 1 else JL
                w = None
                if j == 0:
                    w = {**u_all, "dW": lw["w2"], "cV": stm.get(oc - 2, 0)}
                pe0 = mm(psB[:, buf * 512:buf * 512 + BLK],
                         warena[0:jn, j * 1024 + oc * 128: j * 1024 + (oc + 1) * 128],
                         usb[0:jn, j, :], start=(j == 0), stop=(j == NJ - 1), waits=w)
            v1 = op("vector", lambda h, oc=oc, buf=buf: h.scalar_tensor_tensor(
                xfio[:, 2 + buf, :], psB[:, buf * 512:buf * 512 + BLK], parc(f"b2{l}", oc), xfio[:, buf, :],
                ALU.add, ALU.add), waits={"cP": pe0, f"dXl{buf}": ld, f"dXs{buf}": stq[buf]})
            stm[oc] = v1
            stq[buf] = dma("scalar", f"dXs{buf}", xmas[oc, :, t0:t0 + BLK], xfio[:, 2 + buf, :], waits={"cV": v1})
            xb_store(oc, t0, xfio[:, 2 + buf, :], buf, v1)
        k.ph.flush()

    def epi_block(b, ldm):
        t0 = b * BLK
        srcs = [xsl(b % 2, c, False) for c in range(NCH)]
        pe = ln_stats(srcs, 1024, {"cV": k.ctr["cV"], "cG": k.ctr["cG"], "cP": k.ctr["cP"],
                                   f"dXr{b % 2}": ldm[b]})
        rstd = fsc[:, 3, 0:BLK]
        a3 = ln_chain(1024, rstd, {"cP": pe})
        stm = {}
        stq = {0: k.ctr["dOut"], 1: k.ctr["dOut"]}
        for c in range(NCH):
            buf = c % 2
            ld = dma("scalar", f"dXl{buf}", xfio[:, buf, :], x1m[c, :, t0:t0 + BLK],
                     waits={"cV": stm.get(c - 2, 0), "dXs0": k.ctr["dXs0"], "dXs1": k.ctr["dXs1"]})
            v1 = op("vector", lambda h, c=c, s=srcs[c]: h.scalar_tensor_tensor(
                ga[:, c % 2, :], psA[:, 1024:1024 + BLK], -RD, s, ALU.mult, ALU.add),
                waits={"cA": a3, "cP": pe, "cV": stm.get(c - 2, 0)})
            v2 = op("vector", lambda h, c=c: h.tensor_tensor(ga[:, c % 2, :], ga[:, c % 2, :], rstd, ALU.mult),
                    waits={"cV": v1})
            v3 = op("vector", lambda h, c=c, buf=buf: h.tensor_tensor(xfio[:, 2 + buf, :], ga[:, c % 2, :],
                                                                      xfio[:, buf, :], ALU.add),
                    waits={"cV": v2, f"dXl{buf}": ld, "dOut": stq[buf]})
            stm[c] = v3
            stq[buf] = dma("scalar", "dOut", outT[c, :, t0:t0 + BLK], xfio[:, 2 + buf, :], waits={"cV": v3})
        xrd_mark(b % 2)
        if b + 1 < NBLK:
            ldm[b + 1] = xrd_load(b + 1)[1]
        k.ph.flush()

    k.ph.flush()
    for b in range(NBLK):
        gla_block(b)
    for l in range(NL):
        if l + 1 < NL:
            cast_push(l + 1)
        attn_phase(l)
        ffn_phase(l)
    ldm_e = {0: xrd_load(0)[1]}
    for b in range(NBLK):
        epi_block(b, ldm_e)
    for q in ["dOut", "dTap", "dXs0", "dXs1", "dXb0", "dXb1"]:
        k.emit("sync", lambda h, q=q: h.wait_ge(k.sems[q], k.ctr[q]))
    k.ph.flush()
    return nc


def _in_maps(inputs):
    c = host_consts()
    par = host_par(inputs)
    f32c = lambda a: np.ascontiguousarray(np.asarray(a, np.float32))
    shared = dict(
        wqT=f32c(np.asarray(inputs["L0_attn_Wq"]).T), wkT=f32c(np.asarray(inputs["L0_attn_Wk"]).T),
        wvT=f32c(np.asarray(inputs["L0_attn_Wv"]).T), woTg=f32c(np.asarray(inputs["L0_attn_Wo"]).T),
        wqkv_in=f32c(inputs["L0_ffn_Wqkv"]), wo_in=f32c(inputs["L0_ffn_Wo"]),
        w1_in=f32c(inputs["L0_ffn_W1"]), w2_in=f32c(inputs["L0_ffn_W2"]),
        par_in=par,
        c_ident=c["ident"], c_swapT=c["swapT"], c_onesf=c["onesf"], c_onesh=c["onesh"],
        c_mask=c["masklog"], c_dmask=c["dmask"],
        c_ctq=c["ctq"], c_stq=c["stq"], c_ctk=c["ctk"], c_stk=c["stk"],
    )
    x = np.asarray(inputs["L0_attn_x"], np.float32)
    maps = []
    for core in range(8):
        m = dict(shared)
        m["xT"] = np.ascontiguousarray(x[core].T)
        maps.append(m)
    return maps


def kernel(**inputs):
    if "nc" not in _bcache:
        _bcache["nc"] = build()
    nc = _bcache["nc"]
    maps = _in_maps(inputs)
    res = run_bass_kernel_spmd(nc, maps, core_ids=list(range(8)))
    out = np.empty((8, T, D), np.float32)
    for core in range(8):
        out[core] = res.results[core]["outT"].reshape(D, T).T
    return out


# version 12 (final)
# speedup vs baseline: 1.1801x; 1.0646x over previous
import numpy as np

import concourse.bass as bass
import concourse.mybir as mybir
from concourse.bass_utils import run_bass_kernel_spmd

F32 = mybir.dt.float32
BF16 = mybir.dt.bfloat16
AF = mybir.ActivationFunctionType
ALU = mybir.AluOpType

D = 1024
T = 4096
NCH = 8
H = 16
DK = 64
NP = 8
NL = 6
DI = 2730
NJ = 22
JL = 42
BLK = 512
NBLK = T // BLK
NTC = BLK // 128
NEG = -1e30
EPS_LN = 1e-5
EPS_L2 = 1e-19
EPS_GLA = 1e-19
RD = 1.0 / D

ENGINES = ("sync", "tensor", "vector", "scalar", "gpsimd")
EX = {"tensor": "cP", "vector": "cV", "scalar": "cA", "gpsimd": "cG"}
_bcache = {}

_PC = {}
_cidx = 0
def _pc(name, n=1):
    global _cidx
    _PC[name] = _cidx
    _cidx += n
for _p in range(NP):
    _pc(f"wg{_p}"); _pc(f"bg{_p}"); _pc(f"wl{_p}"); _pc(f"bl{_p}")
for _l in range(NL):
    _pc(f"qs{_l}"); _pc(f"qssw{_l}"); _pc(f"ks{_l}"); _pc(f"kssw{_l}")
    _pc(f"lng{_l}", NCH); _pc(f"lnb{_l}", NCH); _pc(f"b2{_l}", NCH)
    _pc(f"b1a{_l}", NJ); _pc(f"b1g{_l}", NJ)
PARW = _cidx


def host_consts():
    if "c" in _bcache:
        return _bcache["c"]
    ident = np.eye(128, dtype=np.float32)
    S = np.zeros((128, 128), np.float32)
    for b0 in (0, 64):
        for d in range(64):
            S[b0 + d, b0 + (d + 32) % 64] = 1.0
    swapT = np.ascontiguousarray(S.T)
    onesf = np.ones((128, 128), np.float32)
    onesh = np.zeros((128, 128), np.float32)
    onesh[:64, :64] = 1.0
    onesh[64:, 64:] = 1.0
    ii = np.arange(128)[:, None]
    jj = np.arange(128)[None, :]
    masklog = np.zeros((128, 384), np.float32)
    masklog[:, :128] = np.where(ii <= jj, 0.0, NEG)
    masklog[:, 256:] = np.where(ii >= jj, 0.0, NEG)
    dmask = np.where(ii <= jj, 0.0, NEG).astype(np.float32)
    half = np.arange(0, DK, 2, dtype=np.float32)
    inv_freq = 1.0 / 10000.0 ** (half / DK)
    t = np.arange(T, dtype=np.float32)
    freqs = np.concatenate([t[:, None] * inv_freq[None, :]] * 2, -1)
    base_scale = (half + 0.4 * DK) / (1.4 * DK)
    power = (t - T // 2) / 128.0
    scale = np.concatenate([base_scale[None, :] ** power[:, None]] * 2, -1)
    cos, sin = np.cos(freqs), np.sin(freqs)
    sign = np.concatenate([-np.ones(32), np.ones(32)]).astype(np.float32)
    s8 = np.sqrt(8.0)
    dup = lambda a: np.ascontiguousarray(np.concatenate([a, a], 0), dtype=np.float32)
    c = dict(
        ident=ident, swapT=swapT, onesf=onesf, onesh=onesh, masklog=masklog, dmask=dmask,
        ctq=dup((cos * scale * s8).T), stq=dup((sin * scale * sign[None, :] * s8).T),
        ctk=dup((cos / scale * s8).T), stk=dup((sin / scale * sign[None, :] * s8).T),
    )
    _bcache["c"] = c
    return c


def host_par(ins):
    par = np.zeros((128, PARW), np.float32)
    dup2 = lambda v: np.concatenate([v, v])
    for p in range(NP):
        for nm, key in [("wg", "L0_attn_W_gamma"), ("bg", "L0_attn_b_gamma"),
                        ("wl", "L0_attn_W_lambda"), ("bl", "L0_attn_b_lambda")]:
            arr = np.asarray(ins[key], np.float32)
            col = np.empty(128, np.float32)
            col[:64] = arr[2 * p]
            col[64:] = arr[2 * p + 1]
            par[:, _PC[f"{nm}{p}"]] = col
    for l in range(NL):
        qs = np.asarray(ins["L0_ffn_q_scale"][l], np.float32)
        ks = np.asarray(ins["L0_ffn_k_scale"][l], np.float32)
        sw = lambda v: np.concatenate([v[32:], v[:32]])
        par[:, _PC[f"qs{l}"]] = dup2(qs)
        par[:, _PC[f"qssw{l}"]] = dup2(sw(qs))
        par[:, _PC[f"ks{l}"]] = dup2(ks)
        par[:, _PC[f"kssw{l}"]] = dup2(sw(ks))
        par[:, _PC[f"lng{l}"]:_PC[f"lng{l}"] + NCH] = np.asarray(ins["L0_ffn_ln_g"][l], np.float32).reshape(NCH, 128).T
        par[:, _PC[f"lnb{l}"]:_PC[f"lnb{l}"] + NCH] = np.asarray(ins["L0_ffn_ln_b"][l], np.float32).reshape(NCH, 128).T
        par[:, _PC[f"b2{l}"]:_PC[f"b2{l}"] + NCH] = np.asarray(ins["L0_ffn_b2"][l], np.float32).reshape(NCH, 128).T
        b1 = np.asarray(ins["L0_ffn_b1"][l], np.float32)
        b1p = np.zeros((2, NJ * 128), np.float32)
        b1p[0, :DI] = b1[:DI]
        b1p[1, :DI] = b1[DI:]
        par[:, _PC[f"b1a{l}"]:_PC[f"b1a{l}"] + NJ] = b1p[0].reshape(NJ, 128).T
        par[:, _PC[f"b1g{l}"]:_PC[f"b1g{l}"] + NJ] = b1p[1].reshape(NJ, 128).T
    return par


class Ph:
    def __init__(self, k):
        self.k = k
        self.fns = {e: [] for e in ENGINES}

    def flush(self):
        if not any(self.fns.values()):
            return
        with self.k.nc.Block() as blk:
            for e in ENGINES:
                if self.fns[e]:
                    def body(h, fs=tuple(self.fns[e])):
                        for f in fs:
                            f(h)
                    getattr(blk, e)(body)
        self.fns = {e: [] for e in ENGINES}


class Kern:
    def __init__(self, nc):
        self.nc = nc
        self.ctr = {}
        self.sems = {}
        self.inc = {}
        self.lastw = {}
        self.ph = Ph(self)

    def mksem(self, name, inc):
        h = self.nc.semaphore(name).__enter__()
        self.sems[name] = h
        self.ctr[name] = 0
        self.inc[name] = inc

    def emit(self, eng, fn, tick=None, waits=None):
        wl = []
        if waits:
            for q, v in waits.items():
                if not v or v <= 0:
                    continue
                assert v <= self.ctr[q], f"wait on future count {q}={v}>{self.ctr[q]}"
                key = (eng, q)
                if self.lastw.get(key, -1) >= v:
                    continue
                self.lastw[key] = v
                wl.append((self.sems[q], v))
        tk = None
        if tick is not None:
            self.ctr[tick] += self.inc[tick]
            tk = (self.sems[tick], self.inc[tick])

        def closure(h, fn=fn, wl=tuple(wl), tk=tk):
            for s, v in wl:
                h.wait_ge(s, v)
            inst = fn(h)
            if tk is not None:
                inst.then_inc(tk[0], tk[1])
        self.ph.fns[eng].append(closure)
        return self.ctr[tick] if tick else None


def build(taps=()):
    nc = bass.Bass("TRN2", target_bir_lowering=False)
    k = Kern(nc)
    dsems = ["dW", "dT", "dWo0", "dWo1", "dWo2", "dW10", "dW11", "dCv0", "dCv1",
             "dCs0", "dCs1", "dXl0", "dXl1", "dXs0", "dXs1", "dOut", "dTap",
             "dXr0", "dXr1", "dXb0", "dXb1", "dCw"]
    for q in dsems:
        k.mksem(q, 16)
    for q in ["cP", "cV", "cA", "cG"]:
        k.mksem(q, 1)

    def din(name, shape, dtype=F32):
        return nc.dram_tensor(name, list(shape), dtype, kind="ExternalInput").ap()

    xT = din("xT", (D, T))
    wqT = din("wqT", (D, D)); wkT = din("wkT", (D, D)); wvT = din("wvT", (D, D)); woTg = din("woTg", (D, D))
    wqkv_in = din("wqkv_in", (NL, D, 3 * D))
    wo_in = din("wo_in", (NL, D, D))
    w1_in = din("w1_in", (NL, D, 2 * DI))
    w2_in = din("w2_in", (NL, DI, D))
    par_in = din("par_in", (128, PARW))
    c_ident = din("c_ident", (128, 128)); c_swapT = din("c_swapT", (128, 128))
    c_onesf = din("c_onesf", (128, 128)); c_onesh = din("c_onesh", (128, 128))
    c_mask = din("c_mask", (128, 384)); c_dmask = din("c_dmask", (128, 128))
    ctabs = [din("c_ctq", (128, T)), din("c_stq", (128, T)), din("c_ctk", (128, T)), din("c_stk", (128, T))]

    outT = nc.dram_tensor("outT", [NCH, 128, T], F32, kind="ExternalOutput").ap()
    tap_t = {}
    for name, shape, *dt in taps:
        tdt = BF16 if (dt and dt[0] == "bf16") else F32
        tap_t[name] = nc.dram_tensor("tap_" + name, list(shape), tdt, kind="ExternalOutput").ap()

    xmas = nc.dram_tensor("xmas", [NCH, 128, T], F32, kind="Internal").ap()
    xbfh = nc.dram_tensor("xbfh", [NCH, 128, T + 1], BF16, kind="Internal").ap()
    x1m = nc.dram_tensor("x1m", [NCH, 128, T], F32, kind="Internal").ap()
    qkvc = nc.dram_tensor("qkvc", [NL, D, 3 * D], BF16, kind="Internal").ap()
    woc = nc.dram_tensor("woc", [NL, D, D], BF16, kind="Internal").ap()
    wotc = nc.dram_tensor("wotc", [D, D], BF16, kind="Internal").ap()
    gwc = nc.dram_tensor("gwc", [3, D, D], BF16, kind="Internal").ap()
    w1c = nc.dram_tensor("w1c", [NL, D, 2 * DI], BF16, kind="Internal").ap()
    w2c = nc.dram_tensor("w2c", [NL, NJ * 128, D], BF16, kind="Internal").ap()

    def S(name, shape, dtype):
        return nc.alloc_sbuf_tensor(name, list(shape), dtype)

    warena = S("warena", (128, 24576), BF16)
    wor = S("wor", (128, 3, NCH, 128), BF16)
    w1r = S("w1r", (128, 4, 2, NCH, 128), BF16)
    tabs = S("tabs", (128, 4, BLK), F32)
    xfio = S("xfio", (128, 6, BLK), F32)
    qp = S("qp", (128, NP, BLK), BF16)
    ovl = S("ovl", (128, 12288), BF16)
    aout = S("aout", (128, NCH, BLK), BF16)
    chs = S("chs", (128, 6, BLK), BF16)
    rns = S("rns", (128, 2, BLK), BF16)
    fsc = S("fsc", (128, 4, BLK), F32)
    pbf = S("pbf", (128, 5, 384), BF16)
    ga = S("ga", (128, 2, BLK), F32)
    gb = S("gb", (128, 2, BLK), F32)
    gc = S("gc", (128, 2, BLK), F32)
    xrd = S("xrd", (128, 2, NCH, BLK + 1), BF16)
    sstg = S("sstg", (128, 2, BLK), BF16)
    xnb = S("xnb", (128, NCH, BLK), BF16)
    ssb = S("ssb", (128, NP, 64), F32)
    sbfA = S("sbfA", (128, NP, 64), BF16)
    sbfB = S("sbfB", (128, NP, 64), BF16)
    cst = S("cst", (128, 1056), F32)
    cbt = S("cbt", (128, 1024), BF16)
    par = S("par", (128, PARW), F32)

    rem = nc.sbuf_bytes_remaining
    assert rem >= 0, f"SBUF over budget by {-rem} bytes"

    kring = ovl.ap()[:, 0:6144].rearrange("p (a b c) -> p a b c", a=NP, b=6)
    vtok = ovl.ap()[:, 6144:12288].rearrange("p (a b) -> p a b", a=6)
    usb = ovl.ap()[:, 0:NJ * BLK].rearrange("p (a b) -> p a b", a=NJ)
    kview = ovl.ap()[:, 0:4096].rearrange("p (a b) -> p a b", a=NP)
    vview = ovl.ap()[:, 4096:8192]
    t1bv = ovl.ap()[:, 8192:12288].rearrange("p (a b) -> p a b", a=NCH)

    psA = nc.alloc_psum_tensor("psA", [128, 2048], F32)
    psB = nc.alloc_psum_tensor("psB", [128, 1024], F32)
    psC = nc.alloc_psum_tensor("psC", [128, 2048], BF16)

    CO = dict(ident=0, swapT=128, onesf=256, onesh=384, mask=512, dmask=896)
    CB = dict(identb=0, swapTb=128, onesfb=256, oneshb=384, epsrow=512)
    GW = {"wq": 0, "wk": 8192, "wv": 16384}

    def dma(queue, sem, out, in_, waits=None):
        return k.emit(queue, lambda h, o=out, i=in_: h.dma_start(out=o, in_=i), tick=sem, waits=waits)

    def op(eng, fn, waits=None):
        return k.emit(eng, fn, tick=EX[eng], waits=waits)

    def mm(out_ap, lhsT_ap, rhs_ap, start, stop, waits=None):
        return op("tensor", lambda h, o=out_ap, l=lhsT_ap, r=rhs_ap, s1=start, s2=stop:
                  h.matmul(o, l, r, start=s1, stop=s2), waits=waits)

    def tr(out_ap, in_ap, identity, waits=None):
        return op("tensor", lambda h, o=out_ap, i=in_ap, d=identity: h.transpose(o, i, d), waits=waits)

    def allnow():
        return dict(k.ctr)

    def parc(name, off=0, rows=128):
        c0 = _PC[name] + off
        return par[0:rows, c0:c0 + 1]

    def xsl(slot, c, shifted):
        off = (0 if (shifted and c >= 4) else 1)
        return xrd[:, slot, c, off:off + BLK]

    xr_gate = {0: {}, 1: {}}

    def xrd_load(b):
        slot = b % 2
        ld = dma("scalar", f"dXr{slot}", xrd[:, slot, :, :],
                 xbfh[:, :, b * BLK: b * BLK + BLK + 1].transpose([1, 0, 2]),
                 waits=xr_gate[slot])
        return slot, ld

    def xrd_mark(slot):
        xr_gate[slot] = {"cP": k.ctr["cP"], "cV": k.ctr["cV"], "cG": k.ctr["cG"]}

    def tap(name, src_ap, idx=None, waits=None):
        if name in tap_t:
            dst = tap_t[name] if idx is None else tap_t[name][idx]
            dma("scalar", "dTap", dst, src_ap, waits=waits if waits is not None else allnow())

    onesb_row = cbt[0:1, CB["onesfb"]:CB["onesfb"] + 128]
    onesf_row = cst[0:1, CO["onesf"]:CO["onesf"] + 128]
    eps_row = cbt[0:1, CB["epsrow"]:CB["epsrow"] + BLK]
    identb = cbt[:, CB["identb"]:CB["identb"] + 128]
    swapTb = cbt[:, CB["swapTb"]:CB["swapTb"] + 128]
    onesfb = cbt[:, CB["onesfb"]:CB["onesfb"] + 128]
    oneshb = cbt[:, CB["oneshb"]:CB["oneshb"] + 128]

    dma("sync", "dW", cst[:, CO["ident"]:CO["ident"] + 128], c_ident)
    dma("sync", "dW", cst[:, CO["swapT"]:CO["swapT"] + 128], c_swapT)
    dma("sync", "dW", cst[:, CO["onesf"]:CO["onesf"] + 128], c_onesf)
    dma("sync", "dW", cst[:, CO["onesh"]:CO["onesh"] + 128], c_onesh)
    dma("sync", "dW", cst[:, CO["mask"]:CO["mask"] + 384], c_mask)
    dma("sync", "dW", cst[:, CO["dmask"]:CO["dmask"] + 128], c_dmask)
    pld = dma("sync", "dW", par[:, :], par_in)
    w0 = {"dW": pld}
    for nm, dst in [("ident", "identb"), ("swapT", "swapTb"), ("onesf", "onesfb"), ("onesh", "oneshb")]:
        op("gpsimd", lambda h, s=CO[nm], d=CB[dst]: h.tensor_copy(cbt[:, d:d + 128], cst[:, s:s + 128]), waits=w0)
    op("gpsimd", lambda h: h.memset(cbt[0:1, CB["epsrow"]:CB["epsrow"] + BLK], EPS_L2))
    op("gpsimd", lambda h: h.memset(cst[:, 1024:1025], EPS_LN))
    op("gpsimd", lambda h: h.memset(ssb[:, :, :], 0.0))
    op("gpsimd", lambda h: h.memset(sbfA[:, :, :], 0.0))
    op("gpsimd", lambda h: h.memset(sbfB[:, :, :], 0.0))
    zc = op("gpsimd", lambda h: h.memset(sstg[:, 0, 0:16], 0.0))
    k.emit("scalar", lambda h: h.dma_start(out=xbfh[:, :, 0:1].transpose([1, 0, 2]),
                                           in_=sstg[:, 0:1, 0:NCH].transpose([0, 2, 1]),
                                           allow_slow_non_contiguous=True),
           tick="dXb0", waits={"cG": zc})

    def cast_dma(dst, src_ap):
        return k.emit("gpsimd", lambda h, o=dst, i=src_ap: h.dma_start(out=o, in_=i), tick="dCw")

    cast_dma(gwc[0], wqT)
    cast_dma(gwc[1], wkT)
    cast_dma(gwc[2], wvT)
    cast_dma(wotc, woTg)
    glaw_cast = k.ctr["dCw"]
    garena_ld = 0
    for gi, sec in enumerate(["wq", "wk", "wv"]):
        garena_ld = dma("sync", "dW",
                        warena[:, GW[sec]:GW[sec] + 8192].rearrange("p (c n) -> p c n", c=NCH),
                        gwc[gi].rearrange("(c p) n -> p c n", p=128),
                        waits={"dCw": glaw_cast})
    glaw = allnow()

    castq = {"items": [], "done": {}}

    def cast_push(l):
        castq["items"].append((l, lambda l=l: cast_dma(qkvc[l], wqkv_in[l])))
        castq["items"].append((l, lambda l=l: cast_dma(woc[l], wo_in[l])))
        castq["items"].append((l, lambda l=l: cast_dma(w1c[l], w1_in[l])))
        castq["items"].append((l, lambda l=l: cast_dma(w2c[l, 0:DI, :], w2_in[l])))

    def conv_step(nmax):
        for _ in range(max(1, nmax // 7)):
            if not castq["items"]:
                return
            l, f = castq["items"].pop(0)
            f()
            if not castq["items"] or castq["items"][0][0] != l:
                castq["done"][l] = k.ctr["dCw"]

    cast_push(0)

    def ln_stats(srcs, bank, extra_waits=None):
        pe = 0
        ws = dict(extra_waits or {})
        for c, src in enumerate(srcs):
            sq = chs[:, 4 + (c % 2), :]
            pl = op("gpsimd", lambda h, s=src, d=sq: h.tensor_tensor(d, s, s, ALU.mult),
                    waits={**ws, "cP": pe})
            mm(psA[:, bank:bank + BLK], onesfb, src, start=(c == 0), stop=(c == NCH - 1), waits=ws if c == 0 else None)
            pe = mm(psA[:, bank + 512:bank + 512 + BLK], onesfb, sq, start=(c == 0), stop=(c == NCH - 1),
                    waits={"cG": pl})
        return pe

    fsc2_rd = [0]

    def ln_chain(bank, rstd_dst, waits):
        mu2 = fsc[:, 2, 0:BLK]
        a1 = op("scalar", lambda h: h.activation(mu2, psA[:, bank:bank + BLK], AF.Square, scale=RD),
                waits={**waits, "cV": max(waits.get("cV", 0), fsc2_rd[0])})
        v1 = op("vector", lambda h: h.scalar_tensor_tensor(rstd_dst, psA[:, bank + 512:bank + 512 + BLK], RD, mu2,
                                                           ALU.mult, ALU.subtract), waits={"cA": a1})
        fsc2_rd[0] = v1
        a2 = op("scalar", lambda h: h.activation(rstd_dst, rstd_dst, AF.Ln, bias=cst[:, 1024:1025]), waits={"cV": v1})
        return op("scalar", lambda h: h.activation(rstd_dst, rstd_dst, AF.Exp, scale=-0.5), waits={"cA": a2})

    xbst = {0: 0, 1: 0}

    def xb_store(c, t0, src_f32, buf, vmark):
        pl = op("gpsimd", lambda h, s=src_f32, buf=buf: h.tensor_copy(sstg[:, buf, :], s),
                waits={"cV": vmark, f"dXb{buf}": xbst[buf]})
        xbst[buf] = dma("scalar", f"dXb{buf}", xbfh[c, :, 1 + t0:1 + t0 + BLK], sstg[:, buf, :],
                        waits={"cG": pl})
        return pl

    prep_state = {"g1": 0, "last": 0}

    def prep(src_bf, rn_ap, dst_writes, cS, sS, swap_bank, l, isq, waits):
        waits = dict(waits)
        waits["cG"] = max(waits.get("cG", 0), prep_state["g1"])
        waits["cV"] = max(waits.get("cV", 0), prep_state["last"])
        sw = mm(psA[:, swap_bank:swap_bank + BLK], swapTb, src_bf, start=True, stop=True, waits=waits)
        t1 = chs[:, 2, :]
        t2 = chs[:, 3, :]
        sc = ("qs", "qssw") if isq else ("ks", "kssw")
        v1 = op("vector", lambda h, s=src_bf: h.scalar_tensor_tensor(t1, s, parc(f"{sc[0]}{l}"), cS, ALU.mult, ALU.mult),
                waits=waits)
        v2 = op("vector", lambda h, sb=swap_bank: h.scalar_tensor_tensor(
            t2, psA[:, sb:sb + BLK], parc(f"{sc[1]}{l}"), sS, ALU.mult, ALU.mult),
            waits={"cP": sw, "cG": waits["cG"]})
        g1 = op("gpsimd", lambda h: h.tensor_tensor(t1, t1, t2, ALU.add), waits={"cV": v2})
        last = 0
        for dst, c0, n in dst_writes:
            last = op("vector", lambda h, d=dst, c0=c0, n=n: h.tensor_tensor(
                d, t1[:, c0:c0 + n], rn_ap[:, c0:c0 + n], ALU.mult), waits={"cG": g1})
        prep_state["g1"] = g1
        prep_state["last"] = last
        return {"cV": last, "cG": g1}

    wo_ring = {"use": {}, "m": {}}

    def wo_load(cache, oc, slot, conv_w=None):
        w = {"cP": wo_ring["use"].get(slot, 0)}
        if conv_w:
            w.update(conv_w)
        m = dma("sync", f"dWo{slot}", wor[:, slot, :, :],
                cache[:, oc * 128:(oc + 1) * 128].rearrange("(c p) n -> p c n", p=128), waits=w)
        wo_ring["m"][oc] = (f"dWo{slot}", m)
        return m

    def gla_block(b):
        t0 = b * BLK
        cg = 0
        for c in range(NCH):
            sl = fsc[:, c % 2, 0:BLK]
            xld = dma("scalar", f"dXl{c % 2}", sl, xT[c * 128:(c + 1) * 128, t0:t0 + BLK], waits={"cG": cg})
            cg = op("gpsimd", lambda h, c=c, sl=sl: h.tensor_copy(xnb[:, c, :], sl), waits={f"dXl{c % 2}": xld})
        pe = ln_stats([xnb[:, c, :] for c in range(NCH)], 1024, {"cG": cg})
        rstd = fsc[:, 3, 0:BLK]
        a3 = ln_chain(1024, rstd, {"cP": pe})
        t1b = t1bv
        vdone = 0
        for c in range(NCH):
            v1 = op("vector", lambda h, c=c: h.scalar_tensor_tensor(
                ga[:, c % 2, :], psA[:, 1024:1024 + BLK], -RD, xnb[:, c, :], ALU.mult, ALU.add),
                waits={"cA": a3, "cP": pe, "cV": vdone})
            vdone = op("vector", lambda h, c=c: h.tensor_tensor(t1b[:, c, :], ga[:, c % 2, :], rstd, ALU.mult),
                       waits={"cV": v1})
        tap("t1b", ovl[:, 8192:12288].rearrange("p (a b) -> p a b", a=NCH), np.s_[:, :, t0:t0 + BLK])
        evm = {}
        for oc in range(24):
            kind, p = oc // NP, oc % NP
            buf = oc % 2
            pe0 = 0
            for c in range(NCH):
                w = None
                if c == 0:
                    w = {"dW": garena_ld, "cV": max(vdone, evm.get(oc - 2, 0))}
                sec = ("wq", "wk", "wv")[kind]
                pe0 = mm(psB[:, buf * 512:buf * 512 + BLK],
                         warena[:, GW[sec] + c * 1024 + p * 128: GW[sec] + c * 1024 + (p + 1) * 128],
                         t1b[:, c, :], start=(c == 0), stop=(c == NCH - 1), waits=w)
            if kind == 0:
                dst = qp[:, p, :]
            elif kind == 1:
                dst = kview[:, p, 0:BLK]
            else:
                dst = vview[:, p * BLK:(p + 1) * BLK]
            evm[oc] = op("vector", lambda h, d=dst, buf=buf: h.tensor_copy(d, psB[:, buf * 512:buf * 512 + BLK]),
                         waits={"cP": pe0})
        conv_step(16)
        evq = max(evm[oc] for oc in range(0, 8))
        evk = max(evm[oc] for oc in range(8, 16))
        evv = max(evm[oc] for oc in range(16, 24))
        khm = {}
        rn_a = {}
        rn_pe = {}
        for p in range(NP):
            bank = 1024 + (p % 2) * 512
            sq = chs[:, 4 + (p % 2), :]
            pl = op("gpsimd", lambda h, s=kview[:, p, 0:BLK], d=sq: h.tensor_tensor(d, s, s, ALU.mult),
                    waits={"cV": evk, "cP": rn_pe.get(p - 2, 0), "cA": rn_a.get(p - 2, 0)})
            pe1 = mm(psA[:, bank:bank + BLK], oneshb, sq, start=True, stop=False,
                     waits={"cG": pl, "cA": rn_a.get(p - 2, 0)})
            rn_pe[p] = pe1 = mm(psA[:, bank:bank + BLK], onesb_row, eps_row, start=False, stop=True)
            a1 = op("scalar", lambda h, bank=bank: h.activation(fsc[:, 2, 0:BLK], psA[:, bank:bank + BLK], AF.Ln),
                    waits={"cP": pe1, "cV": fsc2_rd[0]})
            a2 = op("scalar", lambda h, p=p: h.activation(rns[:, p % 2, :], fsc[:, 2, 0:BLK], AF.Exp, scale=-0.5),
                    waits={"cA": a1, "cV": khm.get(p - 2, 0)})
            rn_a[p] = a2
            khm[p] = op("vector", lambda h, p=p: h.tensor_tensor(
                kview[:, p, 0:BLK], kview[:, p, 0:BLK], rns[:, p % 2, :], ALU.mult), waits={"cA": a2})
        tap("qp", qp[:, :, :], np.s_[b])
        tap("kh", ovl[:, 0:4096], np.s_[b])
        tap("vv", ovl[:, 4096:8192], np.s_[b])
        lastr = {"chs": 0, "vt_pe": 0, "yev": 0}
        dpipe = {}

        def pipeline(p):
            s = p % 2
            gap, gbp, gcp = ga[:, s, :], gb[:, s, :], gc[:, s, :]
            pe1 = mm(psA[:, 1024:1024 + BLK], oneshb, kview[:, p, 0:BLK], start=True, stop=True,
                     waits={"cV": max(khm[p], lastr["yev"], dpipe.get(p - 2, 0)), "cA": k.ctr["cA"]})
            v1 = op("vector", lambda h, gap=gap: h.tensor_scalar(gap, psA[:, 1024:1024 + BLK], 1.0 / DK, None, ALU.mult),
                    waits={"cP": pe1})
            s1 = op("scalar", lambda h, p=p, gap=gap, gbp=gbp: h.activation(
                gbp, gap, AF.Sigmoid, bias=parc(f"bg{p}"), scale=parc(f"wg{p}")), waits={"cV": v1, **w0})
            s2 = op("scalar", lambda h, p=p, gap=gap, gcp=gcp: h.activation(
                gcp, gap, AF.Sigmoid, bias=parc(f"bl{p}"), scale=parc(f"wl{p}")), waits={"cV": v1})
            v2 = op("vector", lambda h, gap=gap, gbp=gbp, gcp=gcp: h.tensor_tensor(gap, gcp, gbp, ALU.mult),
                    waits={"cA": s2})
            v2 = op("vector", lambda h, gap=gap, gbp=gbp: h.tensor_tensor(gap, gbp, gap, ALU.subtract),
                    waits={"cV": v2, "cA": s1})
            v3 = op("vector", lambda h, gap=gap, gbp=gbp: h.tensor_scalar(gbp, gap, -1.0, 1.0, ALU.mult, ALU.add),
                    waits={"cV": v2})
            v2 = op("vector", lambda h, gap=gap: h.tensor_scalar_max(gap, gap, EPS_GLA), waits={"cV": v2})
            v3 = op("vector", lambda h, gbp=gbp: h.tensor_scalar_max(gbp, gbp, EPS_GLA), waits={"cV": v3})
            s3 = op("scalar", lambda h, gap=gap, gcp=gcp: h.activation(gcp, gap, AF.Ln), waits={"cV": v2})
            s4 = op("scalar", lambda h, gap=gap, gbp=gbp: h.activation(gap, gbp, AF.Ln), waits={"cV": v3, "cA": s3})
            sc = 0
            for tc in range(NTC):
                sc = op("vector", lambda h, gbp=gbp, gcp=gcp, tc=tc: h.tensor_tensor_scan(
                    gbp[:, tc * 128:(tc + 1) * 128], gcp[:, tc * 128:(tc + 1) * 128],
                    gcp[:, tc * 128:(tc + 1) * 128], 0.0, ALU.add, ALU.bypass),
                    waits={"cA": s4})
            v4 = op("vector", lambda h, gap=gap, gbp=gbp, gcp=gcp: h.tensor_tensor(gcp, gap, gbp, ALU.subtract),
                    waits={"cV": sc, "cA": s4})
            dpipe[p] = v4

        pipeline(0)
        for p in range(NP):
            if p + 1 < NP:
                pipeline(p + 1)
            s = p % 2
            gbp, gcp = gb[:, s, :], gc[:, s, :]
            expL = chs[:, 0, :]
            qq = chs[:, 1, :]
            e1 = op("scalar", lambda h, gbp=gbp: h.activation(expL, gbp, AF.Exp),
                    waits={"cV": max(dpipe[p], lastr["chs"])})
            q2 = op("vector", lambda h, p=p: h.tensor_tensor(qq, qp[:, p, :], expL, ALU.mult),
                    waits={"cA": e1, "cV": evq, "cP": k.ctr["cP"]})
            for tc in range(NTC):
                c0 = tc * 128
                eL = fsc[:, 1, tc:tc + 1]
                eA = op("scalar", lambda h, gbp=gbp, eL=eL, c0=c0: h.activation(
                    eL, gbp[:, c0 + 127:c0 + 128], AF.Exp), waits={"cV": max(dpipe[p], lastr["chs"])})
                fv = chs[:, 2, 0:128]
                f1 = op("scalar", lambda h, gbp=gbp, gcp=gcp, c0=c0: h.activation(
                    chs[:, 2, 0:128], gcp[:, c0:c0 + 128], AF.Exp, bias=gbp[:, c0 + 127:c0 + 128]),
                    waits={"cV": max(dpipe[p], lastr["chs"])})
                f2 = op("vector", lambda h, p=p, c0=c0: h.tensor_tensor(
                    chs[:, 3, 0:128], vview[:, p * BLK + c0: p * BLK + c0 + 128], fv, ALU.mult),
                    waits={"cA": f1, "cV": evv, "cP": lastr["vt_pe"]})
                t1m = tr(psC[:, 0:128], kview[:, p, c0:c0 + 128], identb,
                         waits={"cV": max(khm[p], lastr["chs"])})
                ke = op("vector", lambda h: h.tensor_copy(chs[:, 4, 0:128], psC[:, 0:128]),
                        waits={"cP": t1m})
                t2m = tr(psC[:, 1024:1152], chs[:, 3, 0:128], identb, waits={"cV": f2})
                ve = op("vector", lambda h: h.tensor_copy(chs[:, 5, 0:128], psC[:, 1024:1152]),
                        waits={"cP": t2m})
                amk = {}
                for hh in range(2):
                    r0 = hh * 64
                    araw_bank = 0 if hh == 0 else 512
                    d_bank = 256 if hh == 0 else 768
                    ar = mm(psB[:, araw_bank:araw_bank + 128],
                            vview[r0:r0 + 64, p * BLK + c0: p * BLK + c0 + 128],
                            qp[r0:r0 + 64, p, c0:c0 + 128], start=True, stop=True,
                            waits={"cV": max(evv, evq, dpipe[p])})
                    ones_r = cst[r0:r0 + 1, CO["onesf"]:CO["onesf"] + 128]
                    d1 = mm(psA[:, d_bank:d_bank + 128], gcp[r0:r0 + 1, c0:c0 + 128], ones_r,
                            start=True, stop=True, waits={"cV": dpipe[p]})
                    d1 = mm(psA[:, d_bank:d_bank + 128], ones_r, gbp[r0:r0 + 1, c0:c0 + 128],
                            start=False, stop=True)
                    md = op("vector", lambda h, d_bank=d_bank: h.tensor_tensor(
                        fsc[:, 2, 0:128], psA[:, d_bank:d_bank + 128], cst[:, CO["dmask"]:CO["dmask"] + 128], ALU.add),
                        waits={"cP": d1, "cA": k.ctr["cA"]})
                    de = op("scalar", lambda h: h.activation(pbf[:, 2, 0:128], fsc[:, 2, 0:128], AF.Exp),
                            waits={"cV": md})
                    amk[hh] = op("vector", lambda h, hh=hh, araw_bank=araw_bank: h.tensor_tensor(
                        pbf[:, 2, 128 * (hh + 1):128 * (hh + 2)], psB[:, araw_bank:araw_bank + 128],
                        pbf[:, 2, 0:128], ALU.mult),
                        waits={"cA": de, "cP": ar})
                if b == 0 and p == 0 and tc == 0:
                    tap("dbg_at", pbf[:, 2, :])
                    tap("dbg_L", gb[:, 0, :])
                    tap("dbg_lnad", gc[:, 0, :])
                    tap("dbg_ktok", chs[:, 4, 0:128])
                    tap("dbg_qq", chs[:, 1, :])
                ylast = mm(psA[0:64, 1024:1152], chs[:, 4, 0:64], pbf[:, 2, 128:256], start=True, stop=False,
                           waits={"cV": max(ke, amk[0], q2), "cG": k.ctr["cG"]})
                mm(psA[0:64, 1024:1152], sbfA[:, p, :], qq[:, c0:c0 + 128], start=False, stop=False)
                mm(psA[64:128, 1024:1152], chs[:, 4, 64:128], pbf[:, 2, 256:384], start=True, stop=False,
                   waits={"cV": amk[1]})
                ylast = mm(psA[64:128, 1024:1152], sbfB[:, p, :], qq[:, c0:c0 + 128], start=False, stop=True)
                yev = op("vector", lambda h, p=p, c0=c0: h.tensor_copy(aout[:, p, c0:c0 + 128], psA[:, 1024:1152]),
                         waits={"cP": ylast})
                lastr["yev"] = yev
                wA = mm(psA[0:64, 1536:1600], chs[:, 5, 0:64], chs[:, 4, 0:64], start=True, stop=True,
                        waits={"cV": max(ve, ke), "cG": k.ctr["cG"]})
                wB = mm(psA[64:128, 1536:1600], chs[:, 5, 64:128], chs[:, 4, 64:128], start=True, stop=True)
                lastr["vt_pe"] = wB
                su = op("vector", lambda h, p=p, eL=eL: h.scalar_tensor_tensor(
                    ssb[:, p, :], ssb[:, p, :], eL, psA[:, 1536:1600], ALU.mult, ALU.add),
                    waits={"cP": wB, "cA": eA, "cG": k.ctr["cG"]})
                op("gpsimd", lambda h, p=p: h.tensor_copy(sbfA[0:64, p, :], ssb[0:64, p, :]),
                   waits={"cV": su, "cP": ylast})
                op("gpsimd", lambda h, p=p: h.tensor_copy(sbfB[64:128, p, :], ssb[64:128, p, :]),
                   waits={"cV": su})
                lastr["chs"] = max(yev, su)
        tap("glay", aout[:, :, :], np.s_[:, :, t0:t0 + BLK])
        rb = xnb
        evo = 0
        cvw = {"dCw": glaw_cast}
        wo_load(wotc, 0, 0, cvw)
        wo_load(wotc, 1, 1, cvw)
        for oc in range(NCH):
            slot = oc % 3
            buf = oc % 2
            pe0 = 0
            for c in range(NCH):
                w = None
                if c == 0:
                    w = {"cV": max(lastr["yev"], evo), "cG": k.ctr["cG"], wo_ring["m"][oc][0]: wo_ring["m"][oc][1]}
                pe0 = mm(psB[:, buf * 512:buf * 512 + BLK], wor[:, slot, c, :], aout[:, c, :],
                         start=(c == 0), stop=(c == NCH - 1), waits=w)
            wo_ring["use"][slot] = pe0
            if oc + 2 < NCH:
                wo_load(wotc, oc + 2, (oc + 2) % 3, cvw)
            evo = op("vector", lambda h, oc=oc, buf=buf: h.tensor_copy(rb[:, oc, :], psB[:, buf * 512:buf * 512 + BLK]),
                     waits={"cP": pe0})
        pe = ln_stats([rb[:, c, :] for c in range(NCH)], 1024, {"cV": evo})
        a3 = ln_chain(1024, fsc[:, 3, 0:BLK], {"cP": pe})
        x1st = {0: 0, 1: 0}
        x1cg = 0
        for c in range(NCH):
            sl = fsc[:, c % 2, 0:BLK]
            ld = dma("scalar", f"dXl{c % 2}", sl, xT[c * 128:(c + 1) * 128, t0:t0 + BLK],
                     waits={"cV": k.ctr["cV"]})
            s2_ = c % 2
            v1 = op("vector", lambda h, c=c, s2_=s2_: h.scalar_tensor_tensor(
                ga[:, s2_, :], psA[:, 1024:1024 + BLK], -RD, rb[:, c, :], ALU.mult, ALU.add),
                waits={"cA": a3, "cP": pe, "cG": x1cg, f"dXs{c % 2}": x1st[c % 2]})
            v2 = op("vector", lambda h, s2_=s2_: h.tensor_tensor(ga[:, s2_, :], ga[:, s2_, :], fsc[:, 3, 0:BLK], ALU.mult),
                    waits={"cV": v1})
            v3 = op("vector", lambda h, sl=sl, s2_=s2_: h.tensor_tensor(gb[:, s2_, :], ga[:, s2_, :], sl, ALU.add),
                    waits={"cV": v2, f"dXl{c % 2}": ld})
            x1st[c % 2] = dma("scalar", f"dXs{c % 2}", x1m[c, :, t0:t0 + BLK], gb[:, s2_, :], waits={"cV": v3})
            x1cg = op("gpsimd", lambda h, c=c, s2_=s2_: h.tensor_copy(t1bv[:, c, :], gb[:, s2_, :]), waits={"cV": v3})
        x1b = t1bv
        tap("x1", ovl[:, 8192:12288].rearrange("p (a b) -> p a b", a=NCH), np.s_[:, :, t0:t0 + BLK])
        pe = ln_stats([x1b[:, c, :] for c in range(NCH)], 1024, {"cG": x1cg})
        a3 = ln_chain(1024, fsc[:, 3, 0:BLK], {"cP": pe})
        xst = {0: 0, 1: 0}
        t2cg = 0
        for c in range(NCH):
            s2_ = c % 2
            v1 = op("vector", lambda h, c=c, s2_=s2_: h.scalar_tensor_tensor(
                ga[:, s2_, :], psA[:, 1024:1024 + BLK], -RD, x1b[:, c, :], ALU.mult, ALU.add),
                waits={"cA": a3, "cP": pe, f"dXs{c % 2}": xst[c % 2], "cG": t2cg})
            v2 = op("vector", lambda h, s2_=s2_: h.tensor_tensor(ga[:, s2_, :], ga[:, s2_, :], fsc[:, 3, 0:BLK], ALU.mult),
                    waits={"cV": v1})
            xst[c % 2] = dma("scalar", f"dXs{c % 2}", xmas[c, :, t0:t0 + BLK], ga[:, s2_, :], waits={"cV": v2})
            t2cg = xb_store(c, t0, ga[:, s2_, :], c % 2, v2)
        conv_step(12)
        k.ph.flush()

    lw = {"wqkv": None, "w2": None}

    def attn_phase(l):
        cw = castq["done"].get(l)
        assert cw is not None, f"casts for layer {l} not done in time"
        if l > 0:
            for c in range(NCH):
                tap(f"ffn{l - 1}", xmas[c], idx=np.s_[c])
        lw["wqkv"] = dma("sync", "dW", warena.ap().rearrange("p (c n) -> p c n", c=NCH),
                         qkvc[l].rearrange("(c p) n -> p c n", p=128),
                         waits={"dCw": cw, "cP": k.ctr["cP"]})
        ldm = {0: xrd_load(0)[1]}
        for b in range(NBLK):
            attn_block(l, b, ldm)

    def attn_block(l, b, ldm):
        t0 = b * BLK
        tld = 0
        for i in range(4):
            tld = dma("sync", "dT", tabs[:, i, :], ctabs[i][:, t0:t0 + BLK],
                      waits={"cV": k.ctr["cV"], "cG": k.ctr["cG"]} if i == 0 else None)
        evac = {}
        prep_marks = {}
        rn_a = {}
        rnuse = {0: 0, 1: 0}
        order = list(range(8, 16)) + list(range(16, 24)) + list(range(0, 8))
        for idx, oc in enumerate(order):
            kind, p = oc // NP, oc % NP
            buf = idx % 2
            pe0 = 0
            for c in range(NCH):
                w = None
                if c == 0:
                    pv_ = evac.get(idx - 2, ("cV", 0))
                    w = {"dW": lw["wqkv"], pv_[0]: pv_[1], f"dXr{b % 2}": ldm[b]}
                pe0 = mm(psB[:, buf * 512:buf * 512 + BLK],
                         warena[:, c * 3072 + oc * 128: c * 3072 + (oc + 1) * 128],
                         xsl(b % 2, c, True), start=(c == 0), stop=(c == NCH - 1), waits=w)
            dst = chs[:, buf, :]
            pmw = prep_marks.get(idx - 2, {})
            evw = {"cP": pe0, "cG": pmw.get("cG", 0), "cV": pmw.get("cV", 0), "cA": pmw.get("cA", 0)}
            if idx % 2 == 0:
                ev = ("cV", op("vector", lambda h, d=dst, buf=buf: h.tensor_copy(d, psB[:, buf * 512:buf * 512 + BLK]),
                               waits=evw))
            else:
                ev = ("cA", op("scalar", lambda h, d=dst, buf=buf: h.activation(d, psB[:, buf * 512:buf * 512 + BLK], AF.Copy),
                               waits=evw))
            evac[idx] = ev
            if kind == 2:
                tm = tprev = 0
                tmbank = {0: 0, 1: 0}
                for tc in range(NTC):
                    slot = (NTC * b + tc) % 6
                    _w = {"cV": tmbank[tc % 2]}
                    _w[ev[0]] = max(_w.get(ev[0], 0), ev[1])
                    t1m = tr(psC[:, (tc % 2) * 1024:(tc % 2) * 1024 + 128], dst[:, tc * 128:(tc + 1) * 128], identb,
                             waits=_w)
                    tm = op("vector", lambda h, slot=slot, p=p, tc=tc: h.tensor_copy(
                        vtok[:, slot, p * 128:(p + 1) * 128], psC[:, (tc % 2) * 1024:(tc % 2) * 1024 + 128]),
                        waits={"cP": t1m})
                    tmbank[tc % 2] = tm
                    tprev = t1m
                prep_marks[idx] = {"cV": tm, "cG": 0, "cA": 0, "cP": tprev}
            else:
                bank = 1024 + (p % 2) * 512
                sq = chs[:, 4 + (p % 2), :]
                _w = {"cA": rnuse[p % 2]}
                _w[ev[0]] = max(_w.get(ev[0], 0), ev[1])
                pl = op("gpsimd", lambda h, d=dst, sq=sq: h.tensor_tensor(sq, d, d, ALU.mult), waits=_w)
                pe1 = mm(psA[:, bank:bank + BLK], oneshb, sq, start=True, stop=False,
                         waits={"cG": pl, "cA": rnuse[p % 2]})
                pe1 = mm(psA[:, bank:bank + BLK], onesb_row, eps_row, start=False, stop=True)
                a1 = op("scalar", lambda h, bank=bank: h.activation(fsc[:, 2, 0:BLK], psA[:, bank:bank + BLK], AF.Ln),
                        waits={"cP": pe1, "cV": max(prep_state["last"], fsc2_rd[0])})
                rnuse[p % 2] = a1
                a2 = op("scalar", lambda h, p=p: h.activation(rns[:, p % 2, :], fsc[:, 2, 0:BLK], AF.Exp, scale=-0.5),
                        waits={"cA": a1, "cV": prep_state["last"]})
                rn_a[(kind, p)] = a2
                _w = {"cA": a2, "dT": tld, "dW": pld}
                _w[ev[0]] = max(_w.get(ev[0], 0), ev[1])
                if kind == 1:
                    dsts = [(kring[:, p, (NTC * b + tc) % 6, :], tc * 128, 128) for tc in range(NTC)]
                    pm = prep(dst, rns[:, p % 2, :], dsts, tabs[:, 2, :], tabs[:, 3, :],
                              swap_bank=(p % 2) * 512, l=l, isq=False, waits=_w)
                else:
                    pm = prep(dst, rns[:, p % 2, :], [(qp[:, p, :], 0, BLK)], tabs[:, 0, :], tabs[:, 1, :],
                              swap_bank=(p % 2) * 512, l=l, isq=True, waits=_w)
                pm["cA"] = a2
                prep_marks[idx] = pm
        qkv_done = k.ctr["cP"]
        xrd_mark(b % 2)
        if b + 1 < NBLK:
            ldm[b + 1] = xrd_load(b + 1)[1]
        conv_step(14)
        for _oc in range(3):
            wo_load(woc[l], _oc, _oc)
        prep_all = {"cV": k.ctr["cV"], "cG": k.ctr["cG"], "cA": k.ctr["cA"]}
        pvev = {}
        scm = {}
        pvpe = {}
        last_a1 = 0
        last_tpe = 0
        stq_state = {}

        def iters():
            for qt in range(NTC):
                g = NTC * b + qt
                kcs = [c for c in (g - 2, g - 1, g) if c >= 0]
                for p in range(NP):
                    yield qt, p, kcs

        def stageA(qt, p, kcs, pi):
            nonlocal last_a1
            nkc = len(kcs)
            nk = nkc * 128
            sb0 = (pi % 2) * 1024
            sb1 = (pi % 2) * 1024 + 512
            pslot = (pi % 2) * 2
            pe0 = 0
            for i, kc in enumerate(kcs):
                slot = kc % 6
                w = {**prep_all, "cA": max(prep_all["cA"], scm.get(pi - 1, 0))} if i == 0 else None
                mm(psA[:, sb0 + i * 128: sb0 + (i + 1) * 128], qp[0:64, p, qt * 128:(qt + 1) * 128],
                   kring[0:64, p, slot, :], start=True, stop=True, waits=w)
                pe0 = mm(psA[:, sb1 + i * 128: sb1 + (i + 1) * 128], qp[64:128, p, qt * 128:(qt + 1) * 128],
                         kring[64:128, p, slot, :], start=True, stop=True)
            res = []
            for hh, sb in ((0, sb0), (1, sb1)):
                mrow = cst[:, CO["mask"] + 384 - nk: CO["mask"] + 384]
                v1 = op("vector", lambda h, sb=sb, nk=nk, mrow=mrow, hh=hh: h.tensor_tensor(
                    fsc[:, hh, 0:nk], psA[:, sb:sb + nk], mrow, ALU.add),
                    waits={"cP": pe0, "cA": last_a1})
                v2 = op("vector", lambda h, nk=nk, hh=hh: h.reduce_max(
                    fsc[:, 2, 4 * hh:4 * hh + 1], fsc[:, hh, 0:nk], mybir.AxisListType.X, negate=True),
                    waits={"cV": v1})
                a1 = op("scalar", lambda h, nk=nk, hh=hh, pslot=pslot: h.activation(
                    pbf[:, pslot + hh, 0:nk], fsc[:, hh, 0:nk], AF.Exp, bias=fsc[:, 2, 4 * hh:4 * hh + 1],
                    accum_out=fsc[:, 2, 4 * hh + 1:4 * hh + 2]),
                    waits={"cV": v2, "cP": last_tpe})
                last_a1 = a1
                v3 = op("vector", lambda h, hh=hh: h.reciprocal(fsc[:, 2, 4 * hh + 2:4 * hh + 3],
                                                                fsc[:, 2, 4 * hh + 1:4 * hh + 2]), waits={"cA": a1})
                v4 = op("vector", lambda h, nk=nk, hh=hh, pslot=pslot: h.tensor_scalar_mul(
                    pbf[:, pslot + hh, 0:nk], pbf[:, pslot + hh, 0:nk], fsc[:, 2, 4 * hh + 2:4 * hh + 3]),
                    waits={"cV": v3})
                scm[pi] = max(scm.get(pi, 0), a1)
                res.append(v4)
            return res

        def stageB(qt, p, kcs, pi, v4s):
            nonlocal last_tpe
            nkc = len(kcs)
            nk = nkc * 128
            pslot = (pi % 2) * 2
            for hh in range(2):
                tpe = 0
                for i in range(nkc):
                    tpe = tr(psC[:, (pi % 2) * 1024 + i * 128:(pi % 2) * 1024 + (i + 1) * 128],
                             pbf[:, pslot + hh, i * 128:(i + 1) * 128], identb, waits={"cV": v4s[hh]})
                last_tpe = tpe
                v5 = op("vector", lambda h, pi=pi, nk=nk: h.tensor_copy(
                    pbf[:, 4, 0:nk], psC[:, (pi % 2) * 1024:(pi % 2) * 1024 + nk]), waits={"cP": tpe})
                pvbank = 512 * ((pi // 4) % 2)
                col = (pi % 4) * 128
                for i, kc in enumerate(kcs):
                    slot = kc % 6
                    w = {"cV": max(v5, pvev.get((pi // 4) - 2, 0))}
                    pvpe[pi] = mm(psB[hh * 64:(hh + 1) * 64, pvbank + col: pvbank + col + 128],
                                  vtok[:, slot, p * 128 + hh * 64: p * 128 + hh * 64 + 64],
                                  pbf[:, 4, i * 128:(i + 1) * 128],
                                  start=(i == 0), stop=(i == nkc - 1), waits=w)
            if pi % 4 == 3:
                grp = pi // 4
                pvev[grp] = op("vector", lambda h, qt=qt, p=p, pvbank=512 * (grp % 2): h.tensor_copy(
                    aout[:, p - 3:p + 1, qt * 128:(qt + 1) * 128], psB[:, pvbank:pvbank + 512]),
                    waits={"cP": pvpe[pi]})

        prev = None
        for pi, (qt, p, kcs) in enumerate(iters()):
            v4s = stageA(qt, p, kcs, pi)
            if prev is not None:
                stageB(*prev)
            prev = (qt, p, kcs, pi, v4s)
        stageB(*prev)
        att_done = {"cV": k.ctr["cV"]}
        stm = {}
        stq = {0: k.ctr["dXs0"], 1: k.ctr["dXs1"]}
        ldd = {}
        for oc in range(NCH):
            ldd[oc] = dma("scalar", f"dXl{oc % 2}", xfio[:, oc % 4, :], xmas[oc, :, t0:t0 + BLK],
                          waits={"cV": stm.get(oc - 4, 0), "dXs0": k.ctr["dXs0"], "dXs1": k.ctr["dXs1"]}) \
                if oc < 4 else 0
        for oc in range(NCH):
            slot = oc % 3
            buf = oc % 2
            pe0 = 0
            for c in range(NCH):
                w = None
                if c == 0:
                    w = {wo_ring["m"][oc][0]: wo_ring["m"][oc][1], "cV": max(att_done["cV"], stm.get(oc - 2, 0))}
                pe0 = mm(psB[:, buf * 512:buf * 512 + BLK], wor[:, slot, c, :], aout[:, c, :],
                         start=(c == 0), stop=(c == NCH - 1), waits=w)
            wo_ring["use"][slot] = pe0
            if oc + 3 < NCH:
                wo_load(woc[l], oc + 3, (oc + 3) % 3)
            v1 = op("vector", lambda h, buf=buf, oc=oc: h.tensor_tensor(
                xfio[:, 4 + buf, :], xfio[:, oc % 4, :], psB[:, buf * 512:buf * 512 + BLK], ALU.add),
                waits={"cP": pe0, f"dXl{oc % 2}": ldd[oc], f"dXs{buf}": stq[buf]})
            stm[oc] = v1
            if oc + 4 < NCH:
                ldd[oc + 4] = dma("scalar", f"dXl{oc % 2}", xfio[:, oc % 4, :], xmas[oc + 4, :, t0:t0 + BLK],
                                  waits={"cV": v1})
            stq[buf] = dma("scalar", f"dXs{buf}", xmas[oc, :, t0:t0 + BLK], xfio[:, 4 + buf, :], waits={"cV": v1})
            xb_store(oc, t0, xfio[:, 4 + buf, :], buf, v1)
        k.ph.flush()

    def ffn_phase(l):
        cw = castq["done"][l]
        for c in range(NCH):
            tap(f"attn{l}", xmas[c], idx=np.s_[c])
        lw["w2"] = dma("sync", "dW", warena[:, 0:NJ * 1024].rearrange("p (j n) -> p j n", j=NJ),
                       w2c[l].rearrange("(j p) n -> p j n", p=128),
                       waits={"dCw": cw, "cP": k.ctr["cP"]})
        ldm = {0: xrd_load(0)[1]}
        for b in range(NBLK):
            ffn_block(l, b, ldm)

    def ffn_block(l, b, ldm):
        t0 = b * BLK
        w1m = {}
        ring_free = {}

        def w1_load(j):
            slot = j % 4
            jn_ = 128 if j < NJ - 1 else JL
            m = 0
            for ag in range(2):
                m = dma("sync", f"dW1{slot % 2}", w1r[:, slot, ag, :, 0:jn_],
                        w1c[l][:, ag * DI + j * 128: ag * DI + j * 128 + jn_].rearrange("(c p) n -> p c n", p=128),
                        waits={"cP": ring_free.get(slot, 0)})
            w1m[j] = (f"dW1{slot % 2}", m)
        for jj in range(4):
            w1_load(jj)
        srcs = [xsl(b % 2, c, True) for c in range(NCH)]
        pe = ln_stats(srcs, 1024, {"cV": k.ctr["cV"], "cG": k.ctr["cG"], "cP": k.ctr["cP"],
                                   f"dXr{b % 2}": ldm[b]})
        rstd = fsc[:, 3, 0:BLK]
        a3 = ln_chain(1024, rstd, {"cP": pe})
        xn_done = 0
        for c in range(NCH):
            v1 = op("vector", lambda h, c=c, s=srcs[c]: h.scalar_tensor_tensor(
                ga[:, c % 2, :], psA[:, 1024:1024 + BLK], -RD, s, ALU.mult, ALU.add),
                waits={"cA": a3, "cP": pe, "cV": xn_done})
            v2 = op("vector", lambda h, c=c: h.tensor_tensor(ga[:, c % 2, :], ga[:, c % 2, :], rstd, ALU.mult),
                    waits={"cV": v1})
            xn_done = op("vector", lambda h, c=c: h.tensor_scalar(
                xnb[:, c, :], ga[:, c % 2, :], parc(f"lng{l}", c), parc(f"lnb{l}", c), ALU.mult, ALU.add),
                waits={"cV": v2, **w0})
        xrd_mark(b % 2)
        if b + 1 < NBLK:
            ldm[b + 1] = xrd_load(b + 1)[1]
        conv_step(7)
        um = {}
        for j in range(NJ):
            jn = 128 if j < NJ - 1 else JL
            slot = j % 4
            ab = (j % 2) * 1024
            gbk = (j % 2) * 1024 + 512
            pe0 = 0
            for c in range(NCH):
                w = None
                if c == 0:
                    w = {w1m[j][0]: w1m[j][1], "cV": xn_done, "cG": um.get(j - 2, 0)}
                mm(psA[0:jn, ab:ab + BLK], w1r[:, slot, 0, c, 0:jn], xnb[:, c, :],
                   start=(c == 0), stop=(c == NCH - 1), waits=w)
                pe0 = mm(psA[0:jn, gbk:gbk + BLK], w1r[:, slot, 1, c, 0:jn], xnb[:, c, :],
                         start=(c == 0), stop=(c == NCH - 1))
            ring_free[slot] = pe0
            if j + 4 < NJ:
                w1_load(j + 4)
            ga1 = op("scalar", lambda h, j=j, jn=jn, gbk=gbk: h.activation(
                chs[0:jn, 0, :], psA[0:jn, gbk:gbk + BLK], AF.Gelu, bias=parc(f"b1g{l}", j, jn)),
                waits={"cP": pe0, "cG": um.get(j - 1, 0)})
            v1 = op("vector", lambda h, j=j, jn=jn, ab=ab: h.tensor_scalar(
                chs[0:jn, 1, :], psA[0:jn, ab:ab + BLK], parc(f"b1a{l}", j, jn), None, ALU.add),
                waits={"cP": pe0, "cG": um.get(j - 1, 0)})
            um[j] = op("gpsimd", lambda h, j=j, jn=jn: h.tensor_tensor(
                usb[0:jn, j, :], chs[0:jn, 1, :], chs[0:jn, 0, :], ALU.mult),
                waits={"cA": ga1, "cV": v1})
        conv_step(7)
        u_all = {"cG": k.ctr["cG"]}
        stm = {}
        stq = {0: k.ctr["dXs0"], 1: k.ctr["dXs1"]}
        ldd = {}
        for oc in range(4):
            ldd[oc] = dma("scalar", f"dXl{oc % 2}", xfio[:, oc % 4, :], xmas[oc, :, t0:t0 + BLK],
                          waits={"dXs0": k.ctr["dXs0"], "dXs1": k.ctr["dXs1"]})
        for oc in range(NCH):
            buf = oc % 2
            pe0 = 0
            for j in range(NJ):
                jn = 128 if j < NJ - 1 else JL
                w = None
                if j == 0:
                    w = {**u_all, "dW": lw["w2"], "cV": stm.get(oc - 2, 0)}
                pe0 = mm(psB[:, buf * 512:buf * 512 + BLK],
                         warena[0:jn, j * 1024 + oc * 128: j * 1024 + (oc + 1) * 128],
                         usb[0:jn, j, :], start=(j == 0), stop=(j == NJ - 1), waits=w)
            v1 = op("vector", lambda h, oc=oc, buf=buf: h.scalar_tensor_tensor(
                xfio[:, 4 + buf, :], psB[:, buf * 512:buf * 512 + BLK], parc(f"b2{l}", oc), xfio[:, oc % 4, :],
                ALU.add, ALU.add), waits={"cP": pe0, f"dXl{oc % 2}": ldd[oc], f"dXs{buf}": stq[buf]})
            stm[oc] = v1
            if oc + 4 < NCH:
                ldd[oc + 4] = dma("scalar", f"dXl{oc % 2}", xfio[:, oc % 4, :], xmas[oc + 4, :, t0:t0 + BLK],
                                  waits={"cV": v1})
            stq[buf] = dma("scalar", f"dXs{buf}", xmas[oc, :, t0:t0 + BLK], xfio[:, 4 + buf, :], waits={"cV": v1})
            xb_store(oc, t0, xfio[:, 4 + buf, :], buf, v1)
        k.ph.flush()

    def epi_block(b, ldm):
        t0 = b * BLK
        srcs = [xsl(b % 2, c, False) for c in range(NCH)]
        pe = ln_stats(srcs, 1024, {"cV": k.ctr["cV"], "cG": k.ctr["cG"], "cP": k.ctr["cP"],
                                   f"dXr{b % 2}": ldm[b]})
        rstd = fsc[:, 3, 0:BLK]
        a3 = ln_chain(1024, rstd, {"cP": pe})
        stm = {}
        stq = {0: k.ctr["dOut"], 1: k.ctr["dOut"]}
        for c in range(NCH):
            buf = c % 2
            ld = dma("scalar", f"dXl{buf}", xfio[:, buf, :], x1m[c, :, t0:t0 + BLK],
                     waits={"cV": stm.get(c - 2, 0), "dXs0": k.ctr["dXs0"], "dXs1": k.ctr["dXs1"]})
            v1 = op("vector", lambda h, c=c, s=srcs[c]: h.scalar_tensor_tensor(
                ga[:, c % 2, :], psA[:, 1024:1024 + BLK], -RD, s, ALU.mult, ALU.add),
                waits={"cA": a3, "cP": pe, "cV": stm.get(c - 2, 0)})
            v2 = op("vector", lambda h, c=c: h.tensor_tensor(ga[:, c % 2, :], ga[:, c % 2, :], rstd, ALU.mult),
                    waits={"cV": v1})
            v3 = op("vector", lambda h, c=c, buf=buf: h.tensor_tensor(xfio[:, 4 + buf, :], ga[:, c % 2, :],
                                                                      xfio[:, buf, :], ALU.add),
                    waits={"cV": v2, f"dXl{buf}": ld, "dOut": stq[buf]})
            stm[c] = v3
            stq[buf] = dma("scalar", "dOut", outT[c, :, t0:t0 + BLK], xfio[:, 4 + buf, :], waits={"cV": v3})
        xrd_mark(b % 2)
        if b + 1 < NBLK:
            ldm[b + 1] = xrd_load(b + 1)[1]
        k.ph.flush()

    k.ph.flush()
    for b in range(NBLK):
        gla_block(b)
    for l in range(NL):
        if l + 1 < NL:
            cast_push(l + 1)
        attn_phase(l)
        ffn_phase(l)
    ldm_e = {0: xrd_load(0)[1]}
    for b in range(NBLK):
        epi_block(b, ldm_e)
    for q in ["dOut", "dTap", "dXs0", "dXs1", "dXb0", "dXb1"]:
        k.emit("sync", lambda h, q=q: h.wait_ge(k.sems[q], k.ctr[q]))
    k.ph.flush()
    return nc


def _in_maps(inputs):
    c = host_consts()
    par = host_par(inputs)
    f32c = lambda a: np.ascontiguousarray(np.asarray(a, np.float32))
    shared = dict(
        wqT=f32c(np.asarray(inputs["L0_attn_Wq"]).T), wkT=f32c(np.asarray(inputs["L0_attn_Wk"]).T),
        wvT=f32c(np.asarray(inputs["L0_attn_Wv"]).T), woTg=f32c(np.asarray(inputs["L0_attn_Wo"]).T),
        wqkv_in=f32c(inputs["L0_ffn_Wqkv"]), wo_in=f32c(inputs["L0_ffn_Wo"]),
        w1_in=f32c(inputs["L0_ffn_W1"]), w2_in=f32c(inputs["L0_ffn_W2"]),
        par_in=par,
        c_ident=c["ident"], c_swapT=c["swapT"], c_onesf=c["onesf"], c_onesh=c["onesh"],
        c_mask=c["masklog"], c_dmask=c["dmask"],
        c_ctq=c["ctq"], c_stq=c["stq"], c_ctk=c["ctk"], c_stk=c["stk"],
    )
    x = np.asarray(inputs["L0_attn_x"], np.float32)
    maps = []
    for core in range(8):
        m = dict(shared)
        m["xT"] = np.ascontiguousarray(x[core].T)
        maps.append(m)
    return maps


def kernel(**inputs):
    if "nc" not in _bcache:
        _bcache["nc"] = build()
    nc = _bcache["nc"]
    maps = _in_maps(inputs)
    res = run_bass_kernel_spmd(nc, maps, core_ids=list(range(8)))
    out = np.empty((8, T, D), np.float32)
    for core in range(8):
        out[core] = res.results[core]["outT"].reshape(D, T).T
    return out
